# version 28
# baseline (speedup 1.0000x reference)
import os
import numpy as np
import ml_dtypes
import concourse.bass as bass
import concourse.mybir as mybir
from concourse.bass_utils import run_bass_kernel_spmd
from concourse.alu_op_type import AluOpType as A
from contextlib import ExitStack

F32 = mybir.dt.float32
BF16 = mybir.dt.bfloat16
AF = mybir.ActivationFunctionType

NCORE = 8
S_TOK = 2048
D = 1024
NTB = 16
NDC = 8
NQC = 4
NPAIR = 8
EPS = 1e-5
ESHIFT = -8.0


def _host_consts():
    ident = np.eye(128, dtype=np.float32)
    identb = np.eye(128, dtype=ml_dtypes.bfloat16)
    j = np.arange(64)
    p = np.arange(128)
    tria = np.zeros((128, 256), dtype=np.float32)
    tria[0:64, 0:128] = (p[None, :] > j[:, None]).astype(np.float32)
    tria[0:64, 128:256] = (p[None, :] > j[:, None] + 64).astype(np.float32)
    tria[64:128] = tria[0:64]
    negi = np.zeros((128, 256), dtype=np.float32)
    negi[j, j] = -1e30
    negi[j, 128 + 64 + j] = -1e30
    negi[64:128] = negi[0:64]
    fsel = np.zeros((128, 128), dtype=np.float32)
    fsel[0, 0:64] = 1.0
    fsel[32, 64:128] = 1.0
    fsel[64:128] = fsel[0:64]
    t = np.arange(S_TOK, dtype=np.float64)[:, None]
    f = 1000.0 ** (-np.arange(32, dtype=np.float64) / 32.0)[None, :]
    tf = t * f
    two_pi = 2.0 * np.pi

    def hilo(theta):
        y = np.mod(theta, two_pi) - np.pi
        hi = y.astype(np.float32)
        lo = (y - hi.astype(np.float64)).astype(np.float32)
        return hi, lo
    thc = np.concatenate([tf, tf], axis=1) + np.pi / 2.0
    ths = np.concatenate([-tf, tf], axis=1)
    yc, ycd = hilo(thc)
    ys, ysd = hilo(ths)
    return {
        "c_ident": ident, "c_identb": identb,
        "c_tria": tria.astype(ml_dtypes.bfloat16), "c_negi": negi.astype(ml_dtypes.bfloat16),
        "c_fsel": fsel,
        "c_yc": yc, "c_ycd": ycd, "c_ys": ys, "c_ysd": ysd,
    }


def build_nc():
    stage = int(os.environ.get("KSTAGE", "6"))
    nc = bass.Bass("TRN2", target_bir_lowering=False)
    xin = nc.dram_tensor("x", [S_TOK, D], F32, kind="ExternalInput").ap()
    wq = nc.dram_tensor("wq", [D, D], F32, kind="ExternalInput").ap()
    wk = nc.dram_tensor("wk", [D, 64], F32, kind="ExternalInput").ap()
    wv = nc.dram_tensor("wv", [D, 64], F32, kind="ExternalInput").ap()
    wo = nc.dram_tensor("wo", [D, D], F32, kind="ExternalInput").ap()
    win = nc.dram_tensor("win", [D, 64], F32, kind="ExternalInput").ap()
    binp = nc.dram_tensor("binp", [64, 1], F32, kind="ExternalInput").ap()
    w1 = nc.dram_tensor("w1", [64, 16], F32, kind="ExternalInput").ap()
    b1 = nc.dram_tensor("b1", [64, 16], F32, kind="ExternalInput").ap()
    w2 = nc.dram_tensor("w2", [64, 16], F32, kind="ExternalInput").ap()
    b2 = nc.dram_tensor("b2", [64, 1], F32, kind="ExternalInput").ap()
    wout = nc.dram_tensor("wout", [64, D], F32, kind="ExternalInput").ap()
    bout = nc.dram_tensor("bout", [1, D], F32, kind="ExternalInput").ap()
    c_ident = nc.dram_tensor("c_ident", [128, 128], F32, kind="ExternalInput").ap()
    c_identb = nc.dram_tensor("c_identb", [128, 128], BF16, kind="ExternalInput").ap()
    c_tria = nc.dram_tensor("c_tria", [128, 256], BF16, kind="ExternalInput").ap()
    c_negi = nc.dram_tensor("c_negi", [128, 256], BF16, kind="ExternalInput").ap()
    c_fsel = nc.dram_tensor("c_fsel", [128, 128], F32, kind="ExternalInput").ap()
    c_yc = nc.dram_tensor("c_yc", [S_TOK, 64], F32, kind="ExternalInput").ap()
    c_ycd = nc.dram_tensor("c_ycd", [S_TOK, 64], F32, kind="ExternalInput").ap()
    c_ys = nc.dram_tensor("c_ys", [S_TOK, 64], F32, kind="ExternalInput").ap()
    c_ysd = nc.dram_tensor("c_ysd", [S_TOK, 64], F32, kind="ExternalInput").ap()
    yout = nc.dram_tensor("y", [S_TOK, D], F32, kind="ExternalOutput").ap()

    with ExitStack() as st:
        def sb(name, shape, dt=F32):
            return st.enter_context(nc.sbuf_tensor(name, shape, dt))

        x_sb = sb("x_sb", [128, NTB * 1024])
        bufA = sb("bufA", [128, 16384], BF16)
        qfm = sb("qfm", [128, 16384], BF16)
        wT = sb("wT", [128, 8192], BF16)
        h2fm = sb("h2fm", [128, 4096], BF16)
        attnq = sb("attnq", [128, 4096], BF16)
        kvd = sb("kvd", [128, 2048], BF16)
        v1 = sb("v1", [128, 16 * 65], BF16)
        wstage = sb("wstage", [128, 512])
        tmroll = sb("tmroll", [128, 1024])
        qtm = sb("qtm", [128, 1024], BF16)
        qrope = sb("qrope", [128, 2048], BF16)
        cs_y = sb("cs_y", [128, 2048], BF16)
        sums = sb("sums", [128, 2048])
        lnbuf = sb("lnbuf", [128, 1024], BF16)
        f_sb = sb("f_sb", [128, 1024], BF16)
        kvw = sb("kvw", [128, 1024], BF16)
        woutd = sb("woutd", [128, 1024], BF16)
        boutb = sb("boutb", [128, 1024], BF16)
        w1d = sb("w1d", [128, 16])
        b1pd = sb("b1pd", [128, 16])
        w2d = sb("w2d", [128, 16])
        b2d = sb("b2d", [128, 1])
        stats = sb("stats", [128, 192])
        bind = stats[:, 174:175]
        b1d = stats[:, 176:192]
        win_sb = kvw[:, 0:512]
        mv = sb("mv", [128, 32])
        nm1 = sb("nm1", [128, 16])
        iv1 = sb("iv1", [128, 16])
        vep = sb("vep", [128, 16])
        nm2 = sb("nm2", [128, 4]); iv2 = sb("iv2", [128, 4])
        nm3 = sb("nm3", [128, 4]); iv3 = sb("iv3", [128, 4])
        nm4 = sb("nm4", [128, 4]); iv4 = sb("iv4", [128, 4])
        neghalf = sb("neghalf", [128, 16])
        negone = sb("negone", [128, 1])
        expb = sb("expb", [128, 1])
        ones1 = sb("ones1", [1, 128])
        zerob = sb("zerob", [128, 1])
        junk = sb("junk", [128, 16])
        junkg = sb("junkg", [128, 16])
        id_sb = sb("id_sb", [128, 128])
        idb_sb = sb("idb_sb", [128, 128], BF16)
        tria_sb = sb("tria_sb", [128, 256], BF16)
        negi_sb = sb("negi_sb", [128, 256], BF16)
        fsel_sb = sb("fsel_sb", [128, 128])

        ps_st = st.enter_context(nc.psum_tensor("ps_st", [128, 2048], F32))
        ps_pv = st.enter_context(nc.psum_tensor("ps_pv", [128, 1024], F32))
        ps_w = st.enter_context(nc.psum_tensor("ps_w", [128, 1024], F32))

        s_sbv = qtm[:, 0:256]
        hbuf = [qtm[:, 256:512], qtm[:, 512:768]]
        gb = qtm[:, 768:1024]
        gsl = [qrope[:, :].bitcast(F32)[:, 0:256], qrope[:, :].bitcast(F32)[:, 256:512]]

        tmbf = tmroll[:, :].bitcast(BF16)
        qfm_f32 = qfm[:, :].bitcast(F32)
        psst_bf = ps_st[:, :].bitcast(BF16)
        psw_bf = ps_w[:, :].bitcast(BF16)
        pspv_bf = ps_pv[:, :].bitcast(BF16)
        wstage_bf = wstage[:, :].bitcast(BF16)

        sems = {}
        for s in ("dX", "dW", "dO", "dQ", "T", "V", "S", "G"):
            sems[s] = [st.enter_context(nc.semaphore(s)), 0]
        streams = {k: [] for k in ("t", "v", "s", "g", "y")}

        def op(eng, build, inc=None, inck=1):
            if inc is not None:
                sems[inc][1] += inck
            val = sems[inc][1] if inc is not None else None

            def f(e, build=build, inc=inc, inck=inck):
                ins = build(e)
                if inc is not None:
                    ins.then_inc(sems[inc][0], inck)
            streams[eng].append(f)
            return val

        def wt(eng, sem, val):
            if val is None or val <= 0:
                return
            streams[eng].append(lambda e, s=sem, v=val: e.wait_ge(sems[s][0], v))

        def vsync(val):
            wt("v", "V", val)

        def gsync(val):
            wt("g", "G", val)

        def ssync(val):
            wt("s", "S", val)

        _fr = [0, 0]

        def vfence(inc=None):
            _fr[0] += 1
            c = _fr[0] % 16
            return op("v", lambda e, c=c: e.memset(junk[:, c:c + 1], 0.0), inc=inc)

        def gfence(inc=None):
            _fr[1] += 1
            c = _fr[1] % 16
            return op("g", lambda e, c=c: e.memset(junkg[:, c:c + 1], 0.0), inc=inc)

        _done = [False]

        def debug_finish():
            if _done[0]:
                return
            _done[0] = True
            for semname in ("T", "V", "S", "G"):
                v = sems[semname][1]
                if v > 0:
                    streams["y"].append(lambda e, sn=semname, v=v: e.wait_ge(sems[sn][0], v))
            op("y", lambda e: e.dma_start(
                out=yout.rearrange("(t p) d -> p t d", p=128),
                in_=x_sb[:, :].rearrange("p (t d) -> p t d", t=NTB, d=1024)),
                inc="dO", inck=16)

        op("v", lambda e: e.memset(expb[:, :], ESHIFT))
        op("v", lambda e: e.memset(zerob[:, :], 0.0))
        op("v", lambda e: e.memset(neghalf[:, :], -0.5))
        m_negone = op("v", lambda e: e.memset(negone[:, :], -1.0), inc="V")
        op("v", lambda e: e.memset(sums[:, :], 1.0))
        m_ones = op("v", lambda e: e.memset(
            v1[:, :].rearrange("p (t c) -> p t c", t=16, c=65)[:, :, 64:65], 1.0), inc="V")

        def dma(build):
            return op("y", build, inc="dW", inck=16)

        m_xg = [None] * 4

        dma(lambda e: e.dma_start(out=id_sb[:, :], in_=c_ident[:, :]))
        m_idB = dma(lambda e: e.dma_start(out=idb_sb[:, :], in_=c_identb[:, :]))
        streams["y"].append(lambda e, v=m_idB: e.wait_ge(sems["dW"][0], v))
        dma(lambda e: e.dma_start(out=tria_sb[:, :], in_=c_tria[:, :]))
        dma(lambda e: e.dma_start(out=negi_sb[:, :], in_=c_negi[:, :]))
        dma(lambda e: e.dma_start(out=fsel_sb[:, :], in_=c_fsel[:, :]))
        dma(lambda e: e.dma_start(out=w1d[0:64, :], in_=w1[:, :]))
        dma(lambda e: e.dma_start(out=b1d[0:64, :], in_=b1[:, :]))
        dma(lambda e: e.dma_start(out=w2d[0:64, :], in_=w2[:, :]))
        dma(lambda e: e.dma_start(out=b2d[0:64, :], in_=b2[:, :]))
        m_small = dma(lambda e: e.dma_start(out=bind[0:64, :], in_=binp[:, :]))

        def tab_view(tab, half=None):
            v = tab.rearrange("(t p) j -> p t j", p=128)
            if half is None:
                return v
            return v[:, half * 8:(half + 1) * 8, :]

        m_tabL = None
        for i, tab in enumerate((c_yc, c_ycd, c_ys, c_ysd)):
            m_tabL = dma(lambda e, tab=tab, i=i: e.dma_start(
                out=qfm_f32[:, i * 1024:(i + 1) * 1024].rearrange("p (t j) -> p t j", t=16, j=64),
                in_=tab_view(tab)))
        streams["y"].append(lambda e, v=m_tabL: e.wait_ge(sems["dW"][0], v))
        wq_r = wq.rearrange("(ic r) d -> r ic d", r=128)
        m_wqL = [None]
        for h2 in range(2):
            for eo in range(2):
                src = wq_r[h2 * 64 + eo:h2 * 64 + 64:2, :, :]
                dst = qfm[h2 * 64 + eo * 32:h2 * 64 + eo * 32 + 32, 8192:16384].rearrange(
                    "p (ic d) -> p ic d", ic=8)
                m_wqL[0] = op("g", lambda e, src=src, dst=dst: e.dma_start(out=dst, in_=src),
                              inc="dQ", inck=16)
        streams["g"].append(lambda e, v=m_wqL[0]: e.wait_ge(sems["dQ"][0], v))
        for g in range(4):
            m_xg[g] = op("g", lambda e, g=g: e.dma_start(
                out=x_sb[:, g * 4096:(g + 1) * 4096].rearrange("p (t d) -> p t d", t=4, d=1024),
                in_=xin.rearrange("(t p) d -> p t d", p=128)[:, g * 4:(g + 1) * 4, :]),
                inc="dX", inck=16)
            streams["g"].append(lambda e, v=m_xg[g]: e.wait_ge(sems["dX"][0], v))
        m_x = m_xg[3]
        def emit_sin(hi_col, lo_col, dstcol, prev_gate):
            if prev_gate is not None:
                ssync(prev_gate)
                wt("v", "S", prev_gate)
            wt("v", "dW", m_tabL)
            m_add = op("v", lambda e, hi_col=hi_col, lo_col=lo_col: e.tensor_tensor(
                out=tmroll[:, :], in0=qfm_f32[:, hi_col:hi_col + 1024],
                in1=qfm_f32[:, lo_col:lo_col + 1024], op=A.add), inc="V")
            wt("s", "V", m_add)
            return op("s", lambda e, dstcol=dstcol: e.activation(
                out=cs_y[:, dstcol:dstcol + 1024], in_=tmroll[:, :],
                func=AF.Sin, bias=zerob[:, 0:1], scale=-1.0), inc="S")

        m_sin1 = emit_sin(0, 1024, 0, None)
        m_sin2 = emit_sin(2048, 3072, 1024, m_sin1)

        m_wkL = dma(lambda e: e.dma_start(
            out=wstage[:, :].rearrange("p (c h) -> p c h", c=8, h=64),
            in_=wk.rearrange("(c p) h -> p c h", p=128)))
        wt("v", "dW", m_wkL)
        wsv = wstage[:, :].rearrange("p (c e) -> p c e", c=512, e=1)
        m_wkC = None
        for dc in range(NDC):
            op("v", lambda e, dc=dc: e.tensor_copy(
                out=kvw[:, dc * 128 + 0:dc * 128 + 32],
                in_=wsv[:, dc * 64 + 0:dc * 64 + 64:2, 0]))
            m_wkC = op("v", lambda e, dc=dc: e.tensor_copy(
                out=kvw[:, dc * 128 + 32:dc * 128 + 64],
                in_=wsv[:, dc * 64 + 1:dc * 64 + 64:2, 0]), inc="V")
        streams["y"].append(lambda e, v=m_wkC: e.wait_ge(sems["V"][0], v))
        m_wvL = dma(lambda e: e.dma_start(
            out=wstage[:, :].rearrange("p (c h) -> p c h", c=8, h=64),
            in_=wv.rearrange("(c p) h -> p c h", p=128)))
        wt("v", "dW", m_wvL)
        m_kvw = None
        for dc in range(NDC):
            m_kvw = op("v", lambda e, dc=dc: e.tensor_copy(
                out=kvw[:, dc * 128 + 64:dc * 128 + 128],
                in_=wstage[:, dc * 64:dc * 64 + 64]), inc="V")
        streams["y"].append(lambda e, v=m_kvw: e.wait_ge(sems["V"][0], v))
        m_wout = None
        for half in range(2):
            m_woutL = dma(lambda e, half=half: e.dma_start(
                out=wstage[0:64, :], in_=wout[:, half * 512:(half + 1) * 512]))
            wt("v", "dW", m_woutL)
            op("v", lambda e, half=half: e.tensor_copy(
                out=woutd[0:64, half * 512:(half + 1) * 512], in_=wstage[0:64, :]))
            m_wout = op("v", lambda e, half=half: e.tensor_copy(
                out=woutd[64:128, half * 512:(half + 1) * 512], in_=wstage[0:64, :]), inc="V")
            streams["y"].append(lambda e, v=m_wout: e.wait_ge(sems["V"][0], v))
        m_ones1 = op("v", lambda e: e.memset(ones1[:, :], 1.0), inc="V")
        m_boutc = None
        m_bmm = None
        for half in range(2):
            m_boutL = dma(lambda e, half=half: e.dma_start(
                out=wstage[0:1, :], in_=bout[:, half * 512:(half + 1) * 512]))
            wt("t", "dW", m_boutL)
            wt("t", "V", m_ones1)
            if half == 1:
                wt("t", "V", m_boutc)
            m_bmm = op("t", lambda e: e.matmul(
                ps_w[:, 0:512], ones1[0:1, :], wstage[0:1, :], start=True, stop=True), inc="T")
            wt("v", "T", m_bmm)
            m_boutc = op("v", lambda e, half=half: e.tensor_copy(
                out=boutb[:, half * 512:(half + 1) * 512], in_=ps_w[:, 0:512]), inc="V")
            streams["y"].append(lambda e, v=m_boutc: e.wait_ge(sems["V"][0], v))
        op("v", lambda e: e.tensor_copy(out=w1d[64:128, :], in_=w1d[0:64, :]))
        op("v", lambda e: e.tensor_copy(out=b1d[64:128, :], in_=b1d[0:64, :]))
        op("v", lambda e: e.tensor_copy(out=w2d[64:128, :], in_=w2d[0:64, :]))
        op("v", lambda e: e.tensor_copy(out=b2d[64:128, :], in_=b2d[0:64, :]))
        m_dups = op("v", lambda e: e.tensor_copy(out=bind[64:128, :], in_=bind[0:64, :]), inc="V")
        vsync(m_dups)
        m_ffnw = op("v", lambda e: e.scalar_tensor_tensor(
            out=b1pd[:, :], in0=w1d[:, :], scalar=bind[:, 0:1], in1=b1d[:, :],
            op0=A.mult, op1=A.add), inc="V")

        m_ln1g = [None] * 4
        for g in range(4):
            wt("v", "dX", m_xg[g])
            if g == 0:
                vsync(m_ffnw)
            m_st1 = None
            for tb in range(4 * g, 4 * g + 4):
                op("v", lambda e, tb=tb: e.bn_stats(out=stats[:, tb * 12:tb * 12 + 6],
                                                    in_=x_sb[:, tb * 1024:tb * 1024 + 512]))
                m_st1 = op("v", lambda e, tb=tb: e.bn_stats(out=stats[:, tb * 12 + 6:tb * 12 + 12],
                                                            in_=x_sb[:, tb * 1024 + 512:tb * 1024 + 1024]),
                           inc="V")
            vsync(m_st1)
            m_ag1 = None
            for tb in range(4 * g, 4 * g + 4):
                m_ag1 = op("v", lambda e, tb=tb: e.bn_aggr(out=mv[:, tb * 2:tb * 2 + 2],
                                                           in_=stats[:, tb * 12:tb * 12 + 12]), inc="V")
            vsync(m_ag1)
            op("v", lambda e, g=g: e.tensor_scalar(
                out=nm1[:, 4 * g:4 * g + 4], in0=mv[:, 8 * g:8 * g + 8:2],
                scalar1=-1.0, scalar2=None, op0=A.mult))
            m_ln1v = op("v", lambda e, g=g: e.tensor_scalar(
                out=vep[:, 4 * g:4 * g + 4], in0=mv[:, 8 * g + 1:8 * g + 8:2],
                scalar1=EPS, scalar2=None, op0=A.add), inc="V")
            wt("g", "V", m_ln1v)
            m_ln1g[g] = op("g", lambda e, g=g: e.tensor_tensor(
                out=iv1[:, 4 * g:4 * g + 4], in0=vep[:, 4 * g:4 * g + 4],
                in1=neghalf[:, 4 * g:4 * g + 4], op=A.pow), inc="G")

        m_wqT = [None] * NDC
        m_wqE = [None] * NDC
        for ic in range(NDC):
            wt("t", "dQ", m_wqL[0])
            if ic == 0:
                wt("t", "dW", m_idB)
            if ic >= 1:
                wt("t", "S", m_wqE[ic - 1])
            for dc in range(NDC):
                def tr(e, ic=ic, dc=dc):
                    return e.transpose(
                        psst_bf[:, 2048 + dc * 128:2048 + dc * 128 + 128],
                        qfm[:, 8192 + ic * 1024 + dc * 128:8192 + ic * 1024 + (dc + 1) * 128],
                        idb_sb[:, :])
                if dc == NDC - 1:
                    m_wqT[ic] = op("t", tr, inc="T")
                else:
                    op("t", tr)
            wt("s", "T", m_wqT[ic])
            m_wqE[ic] = op("s", lambda e, ic=ic: e.activation(
                out=wT[:, :].rearrange("p (dc i) -> p dc i", dc=NDC, i=1024)[:, :, ic * 128:ic * 128 + 128],
                in_=psst_bf[:, 2048:3072].rearrange("p (s c) -> p s c", s=8, c=128),
                func=AF.Copy, bias=0.0), inc="S")

        m_qp = [None] * NTB
        m_qtmE = [None] * NTB
        m_qT = [None] * NTB
        m_qfm = [None] * NTB
        m_rope = [None] * NTB

        def emit_qproj(tb):
            par = tb % 2
            wt("t", "S", m_h1fm[tb])
            if tb == 1:
                wt("t", "S", m_wqE[NDC - 1])
            if tb >= 2:
                wt("t", "V", m_qtmE[tb - 2])
            for dc in range(NDC):
                for ih in range(2):
                    def mm(e, tb=tb, dc=dc, ih=ih, par=par):
                        out = ps_st[:, par * 1024 + ih * 512:par * 1024 + (ih + 1) * 512]
                        return e.matmul(out,
                                        bufA[:, dc * 2048 + tb * 128:dc * 2048 + tb * 128 + 128],
                                        wT[:, dc * 1024 + ih * 512:dc * 1024 + (ih + 1) * 512],
                                        start=(dc == 0), stop=(dc == NDC - 1))
                    if dc == NDC - 1 and ih == 1:
                        m_qp[tb] = op("t", mm, inc="T")
                    else:
                        op("t", mm)

        def emit_qrope(tb):
            par = tb % 2
            csc = cs_y[:, tb * 64:(tb + 1) * 64]
            css = cs_y[:, 1024 + tb * 64:1024 + (tb + 1) * 64]
            wt("v", "T", m_qp[tb])
            psq = ps_st[:, par * 1024:par * 1024 + 1024]
            if tb >= 1:
                vsync(m_rope[tb - 1])
            if tb >= 2 and m_qT[tb - 2] is not None:
                wt("v", "T", m_qT[tb - 2])
            op("v", lambda e, css=css, psq=psq: e.tensor_tensor(
                out=qtm[:, :].rearrange("p (h a j) -> p h a j", h=16, a=2, j=32),
                in0=psq.rearrange("p (h a j) -> p h a j", h=16, a=2, j=32)[:, :, ::-1, :],
                in1=css.rearrange("p (a j) -> p a j", a=2, j=32).unsqueeze(1).broadcast_to((128, 16, 2, 32)),
                op=A.mult))
            m_qtmE[tb] = op("v", lambda e, csc=csc, psq=psq, par=par: e.tensor_tensor(
                out=qrope[:, par * 1024:par * 1024 + 1024].rearrange("p (h f) -> p h f", h=16, f=64),
                in0=psq.rearrange("p (h f) -> p h f", h=16, f=64),
                in1=csc.unsqueeze(1).broadcast_to((128, 16, 64)), op=A.mult), inc="V")
            vsync(m_qtmE[tb])
            m_rope[tb] = op("v", lambda e, par=par: e.tensor_tensor(
                out=qrope[:, par * 1024:par * 1024 + 1024],
                in0=qrope[:, par * 1024:par * 1024 + 1024],
                in1=qtm[:, :], op=A.add), inc="V")

        def emit_qT(tb):
            par = tb % 2
            pso = par
            wt("t", "V", m_rope[tb])
            if tb < 2:
                wt("t", "V", m_boutc)
            else:
                wt("t", "S", m_qfm[tb - 2])
            for pr in range(NPAIR):
                def tr(e, pr=pr, par=par, pso=pso):
                    return e.transpose(psw_bf[:, pso * 1024 + pr * 128:pso * 1024 + pr * 128 + 128],
                                       qrope[:, par * 1024 + pr * 128:par * 1024 + (pr + 1) * 128], idb_sb[:, :])
                if pr == NPAIR - 1:
                    m_qT[tb] = op("t", tr, inc="T")
                else:
                    op("t", tr)
            wt("s", "T", m_qT[tb])
            m_qfm[tb] = op("s", lambda e, tb=tb, pso=pso: e.activation(
                out=qfm[:, :].rearrange("p (pr t) -> p pr t", pr=NPAIR, t=2048)[:, :, tb * 128:tb * 128 + 128],
                in_=psw_bf[:, pso * 1024:pso * 1024 + 1024].rearrange("p (pr c) -> p pr c", pr=NPAIR, c=128),
                func=AF.Copy, bias=0.0), inc="S")

        m_h1T = [None] * NTB
        m_h1fm = [None] * NTB
        wt("g", "S", m_sin2)
        for tb in range(NTB):
            slot = tb % 2
            if tb % 4 == 0:
                gsync(m_ln1g[tb // 4])
            if tb >= 2:
                wt("g", "T", m_h1T[tb - 2])
            m_n1 = op("g", lambda e, tb=tb, slot=slot: e.tensor_scalar(
                out=tmbf[:, slot * 1024:slot * 1024 + 1024],
                in0=x_sb[:, tb * 1024:(tb + 1) * 1024],
                scalar1=nm1[:, tb:tb + 1], scalar2=iv1[:, tb:tb + 1],
                op0=A.add, op1=A.mult), inc="G")
            wt("t", "G", m_n1)
            if tb >= 2:
                wt("t", "S", m_h1fm[tb - 2])
            for dc in range(NDC):
                def tr(e, dc=dc, slot=slot):
                    return e.transpose(
                        pspv_bf[:, slot * 1024 + dc * 128:slot * 1024 + dc * 128 + 128],
                        tmbf[:, slot * 1024 + dc * 128:slot * 1024 + dc * 128 + 128],
                        idb_sb[:, :])
                if dc == NDC - 1:
                    m_h1T[tb] = op("t", tr, inc="T")
                else:
                    op("t", tr)
            wt("s", "T", m_h1T[tb])
            m_h1fm[tb] = op("s", lambda e, tb=tb, slot=slot: e.activation(
                out=bufA[:, :].rearrange("p (dc t) -> p dc t", dc=NDC, t=2048)[:, :, tb * 128:tb * 128 + 128],
                in_=pspv_bf[:, slot * 1024:slot * 1024 + 1024].rearrange("p (dc c) -> p dc c", dc=NDC, c=128),
                func=AF.Copy, bias=0.0), inc="S")
            if tb >= 1:
                if tb >= 3:
                    emit_qT(tb - 3)
                emit_qproj(tb - 1)
                emit_qrope(tb - 1)
            if tb == NTB - 1:
                emit_qT(tb - 2)
                emit_qproj(tb)
                emit_qrope(tb)
                emit_qT(tb - 1)
                emit_qT(tb)

        if stage <= 1:
            debug_finish()
        m_kvT = [None] * 4
        m_kvE = [None] * 4
        for tc in range(4):
            if tc == 0:
                wt("t", "S", m_qfm[NTB - 1])
            wt("t", "V", m_kvw)
            wt("t", "S", m_h1fm[4 * tc + 3])
            if tc >= 1:
                wt("t", "S", m_kvE[tc - 1])
            for dc in range(NDC):
                def mm(e, tc=tc, dc=dc):
                    return e.matmul(ps_w[:, 0:512], kvw[:, dc * 128:(dc + 1) * 128],
                                    bufA[:, dc * 2048 + tc * 512:dc * 2048 + (tc + 1) * 512],
                                    start=(dc == 0), stop=(dc == NDC - 1))
                if dc == NDC - 1:
                    m_kvT[tc] = op("t", mm, inc="T")
                else:
                    op("t", mm)
            wt("s", "T", m_kvT[tc])
            m_kvE[tc] = op("s", lambda e, tc=tc: e.activation(
                out=h2fm[:, tc * 512:(tc + 1) * 512], in_=ps_w[:, 0:512],
                func=AF.Copy, bias=0.0), inc="S")


        streams["y"].append(lambda e, v=m_kvT[3]: e.wait_ge(sems["T"][0], v))
        m_winL = dma(lambda e: e.dma_start(
            out=wstage[:, :].rearrange("p (c h) -> p c h", c=8, h=64),
            in_=win.rearrange("(c p) h -> p c h", p=128)))
        wt("v", "dW", m_winL)
        wt("v", "T", m_kvT[3])
        m_win = op("v", lambda e: e.tensor_copy(out=win_sb, in_=wstage[:, :]), inc="V")

        m_vT = [None] * 2
        m_vE = [None] * 2
        for gs in range(2):
            wt("t", "S", m_kvE[3])
            if gs == 1:
                wt("t", "S", m_vE[0])
            for i in range(8):
                tb = gs * 8 + i
                def tr(e, tb=tb, i=i):
                    return e.transpose(psw_bf[:, i * 128:i * 128 + 64],
                                       h2fm[64:128, tb * 128:(tb + 1) * 128],
                                       idb_sb[64:128, 64:128])
                if i == 7:
                    m_vT[gs] = op("t", tr, inc="T")
                else:
                    op("t", tr)
            wt("s", "T", m_vT[gs])
            m_vE[gs] = op("s", lambda e, gs=gs: e.activation(
                out=v1[:, :].rearrange("p (t c) -> p t c", t=16, c=65)[:, gs * 8:gs * 8 + 8, 0:64],
                in_=psw_bf[:, 0:1024].rearrange("p (i c) -> p i c", i=8, c=128)[:, :, 0:64],
                func=AF.Copy, bias=0.0), inc="S")

        m_kT = [None] * 2
        m_ktm = [None] * 2
        for gs in range(2):
            wt("t", "S", m_vE[gs] if gs == 0 else m_ktm[0])
            if gs == 0:
                wt("t", "S", m_vE[1])
            for i in range(8):
                tb = gs * 8 + i
                def tr(e, tb=tb, i=i):
                    return e.transpose(psw_bf[:, i * 128:i * 128 + 64],
                                       h2fm[0:64, tb * 128:(tb + 1) * 128],
                                       idb_sb[0:64, 0:64])
                if i == 7:
                    m_kT[gs] = op("t", tr, inc="T")
                else:
                    op("t", tr)
            wt("s", "T", m_kT[gs])
            if gs == 0:
                wt("s", "V", m_rope[NTB - 1])
            m_ktm[gs] = op("s", lambda e, gs=gs: e.activation(
                out=qtm[:, gs * 512:gs * 512 + 512].rearrange("p (t c) -> p t c", t=8, c=64),
                in_=psw_bf[:, 0:1024].rearrange("p (i c) -> p i c", i=8, c=128)[:, :, 0:64],
                func=AF.Copy, bias=0.0), inc="S")

        m_krope = [None] * 2
        m_kback = [None] * NTB
        m_kvdE = {}
        m_kbackL = None
        for gs in range(2):
            wt("v", "S", m_ktm[gs])
            csc = cs_y[:, gs * 512:(gs + 1) * 512]
            css = cs_y[:, 1024 + gs * 512:1024 + (gs + 1) * 512]
            ktg = qtm[:, gs * 512:(gs + 1) * 512]
            m_sw = op("v", lambda e, ktg=ktg, css=css, gs=gs: e.tensor_tensor(
                out=tmbf[:, 1024 + gs * 512:1024 + gs * 512 + 512].rearrange(
                    "p (t a j) -> p t a j", t=8, a=2, j=32),
                in0=ktg.rearrange("p (t a j) -> p t a j", t=8, a=2, j=32)[:, :, ::-1, :],
                in1=css.rearrange("p (t a j) -> p t a j", t=8, a=2, j=32), op=A.mult), inc="V")
            vsync(m_sw)
            m_kc = op("v", lambda e, ktg=ktg, csc=csc: e.tensor_tensor(
                out=ktg, in0=ktg, in1=csc, op=A.mult), inc="V")
            vsync(m_kc)
            m_krope[gs] = op("v", lambda e, ktg=ktg, gs=gs: e.tensor_tensor(
                out=ktg, in0=ktg,
                in1=tmbf[:, 1024 + gs * 512:1024 + gs * 512 + 512], op=A.add), inc="V")
        for tb in range(NTB):
            if tb % 4 == 0:
                wt("t", "V", m_krope[(tb + 3) // 8])
                if tb == 0:
                    wt("t", "S", m_ktm[1])
                if tb >= 4:
                    wt("t", "S", m_kvdE.get(tb - 4))
            m_kback[tb] = op("t", lambda e, tb=tb: e.transpose(
                psw_bf[0:64, (tb % 4) * 128:(tb % 4) * 128 + 128],
                qtm[:, tb * 64:tb * 64 + 64], idb_sb[:, :]), inc="T")
            if tb % 4 == 3:
                wt("s", "T", m_kback[tb])
                mm_ = op("s", lambda e, tb=tb: e.activation(
                    out=kvd[0:64, (tb - 3) * 128:(tb + 1) * 128],
                    in_=psw_bf[0:64, 0:512], func=AF.Copy, bias=0.0), inc="S")
                for q_ in range(4):
                    m_kvdE[tb - q_] = mm_
        m_kbackL = m_kback[NTB - 1]
        wt("s", "S", m_kvdE[15])
        m_kdup = op("s", lambda e: e.activation(
            out=kvd[64:128, :], in_=kvd[0:64, :], func=AF.Copy, bias=0.0), inc="S")

        if stage <= 2:
            debug_finish()


        m_woTh = {}
        m_woT = [None] * NDC
        m_woE = [None] * NDC

        def emit_woT(ics):
            if ics[0] == 0:
                streams["g"].append(lambda e, v=m_qp[NTB - 1]: e.wait_ge(sems["T"][0], v))
                streams["g"].append(lambda e, v=m_win: e.wait_ge(sems["V"][0], v))
            for ic in ics:
                for cg in range(2):
                    ch = ic * 2 + cg
                    sl = ch % 2
                    prev2 = m_woTh.get((ic, cg - 2)) or m_woTh.get((ic - 1, cg))
                    if prev2 is not None:
                        streams["g"].append(lambda e, v=prev2: e.wait_ge(sems["T"][0], v))
                    m_load = op("g", lambda e, ic=ic, cg=cg, sl=sl: e.dma_start(
                        out=wstage_bf[:, sl * 512:sl * 512 + 512],
                        in_=wo[ic * 128:(ic + 1) * 128, cg * 512:(cg + 1) * 512]),
                        inc="dQ", inck=16)
                    wt("t", "dQ", m_load + (16 if ch % 2 == 0 else 0))
                    if ic == 0 and cg == 0:
                        wt("t", "S", m_kdup)
                    if ic >= 1 and cg == 0:
                        wt("t", "V", m_woE[ic - 1])
                    for j in range(4):
                        def tr(e, j=j, cg=cg, sl=sl):
                            return e.transpose(
                                psw_bf[:, cg * 512 + j * 128:cg * 512 + j * 128 + 128],
                                wstage_bf[:, sl * 512 + j * 128:sl * 512 + (j + 1) * 128], idb_sb[:, :])
                        if j == 3:
                            m_woTh[(ic, cg)] = op("t", tr, inc="T")
                        else:
                            op("t", tr)
                m_woT[ic] = m_woTh[(ic, 1)]
                wt("v", "T", m_woT[ic])
                if ic == 0:
                    wt("v", "S", m_qfm[NTB - 1])
                m_woE[ic] = op("v", lambda e, ic=ic: e.tensor_copy(
                    out=wT[:, :].rearrange("p (dc i) -> p dc i", dc=NDC, i=1024)[:, :, ic * 128:ic * 128 + 128],
                    in_=psw_bf[:, 0:1024].rearrange("p (s c) -> p s c", s=8, c=128)), inc="V")

        emit_woT(list(range(NDC)))
        strickle = []
        dqq = []

        m_ep = [[None] * NPAIR for _ in range(NQC)]
        m_bc = [[None] * NPAIR for _ in range(NQC)]
        m_nm = [[None] * NPAIR for _ in range(NQC)]
        m_recip = [None] * NQC
        m_wo = {}
        m_yev = {}
        m_nrm = {}
        m_add2 = {}
        m_add4 = {}
        m_n3 = {}
        m_h2T = {}
        m_h2E = {}
        m_fe = {}
        m_gb = [None] * NQC
        m_fpl = [None] * NQC
        m_accL = [None]
        exp_hist = []
        pv_km = {}
        exp_m = {}

        def emit_scores(qc, pair, kb):
            sub = kb % 2
            di = kb - 4 * qc
            col0 = 128 * di if di >= 0 else 0
            if qc == 0 and pair == 0 and kb < 2:
                wt("t", "S", m_qfm[NTB - 1])
                wt("t", "V", m_qtmE[NTB - 1])
                wt("t", "S", m_kdup)
                wt("t", "V", m_ones)
            if qc >= 2 and pair == 0 and kb == 0:
                wt("t", "S", m_qfm[4 * qc + 3])
            if len(exp_hist) >= 2:
                wt("t", "S", exp_hist[-2])
            mkey = None
            for hb in (0, 64):
                base = (0 if hb == 0 else 1024) + sub * 512

                def mm(e, qc=qc, pair=pair, kb=kb, hb=hb, base=base, col0=col0, di=di):
                    return e.matmul(
                        ps_st[:, base + col0:base + 512],
                        kvd[hb:hb + 64, kb * 128:(kb + 1) * 128],
                        qfm[hb:hb + 64, pair * 2048 + qc * 512 + col0:pair * 2048 + (qc + 1) * 512],
                        start=True, stop=(di < 0), tile_position=(hb, 0))
                if di < 0 and hb == 64:
                    mkey = op("t", mm, inc="T")
                else:
                    op("t", mm)
                if di >= 0:
                    op("t", lambda e, hb=hb, base=base, col0=col0: e.matmul(
                        ps_st[:, base + col0:base + col0 + 128],
                        tria_sb[hb:hb + 64, 0:128], negi_sb[hb:hb + 64, 0:128],
                        start=False, stop=False, tile_position=(hb, 0)))

                    def tr2(e, hb=hb, base=base, col0=col0):
                        return e.matmul(
                            ps_st[:, base + col0:base + col0 + 128],
                            tria_sb[hb:hb + 64, 128:256], negi_sb[hb:hb + 64, 128:256],
                            start=False, stop=True, tile_position=(hb, 0))
                    if hb == 64:
                        mkey = op("t", tr2, inc="T")
                    else:
                        op("t", tr2)
            return mkey

        def emit_exp(qc, pair, kb, m_scores):
            sub = kb % 2
            di = kb - 4 * qc
            col0 = 128 * di if di >= 0 else 0
            HS = 2048 if qc == 0 else 8192
            PT = h2fm if qc == 0 else bufA
            wt("s", "T", m_scores)
            if qc == 0 and pair == 0 and kb == 0:
                wt("s", "T", m_kT[1])
            if pair > 0:
                wt("s", "T", pv_km[(qc, pair - 1, kb)])
            elif qc > 0:
                prev_last = 4 * (qc - 1) + 3
                wt("s", "T", pv_km[(qc - 1, 7, min(kb, prev_last))])
            if qc == 1 and pair == 0 and kb == 0:
                wt("s", "T", m_qp[NTB - 1])
            in_ap = ps_st[:, :].rearrange("p (h c) -> p h c", h=2, c=1024)[:, :, sub * 512 + col0:sub * 512 + 512]
            out_ap = PT[:, :].rearrange("p (h k) -> p h k", h=2, k=HS)[:, :, kb * 512 + col0:kb * 512 + 512]
            m = op("s", lambda e, in_ap=in_ap, out_ap=out_ap: e.activation(
                out=out_ap, in_=in_ap, func=AF.Exp, bias=expb[:, 0:1], scale=0.125), inc="S")
            exp_hist.append(m)
            exp_m[(qc, pair, kb)] = m

        def emit_pv(qc, pair, kb):
            nkb = 4 * qc + 4
            di = kb - 4 * qc
            col0 = 128 * di if di >= 0 else 0
            HS = 2048 if qc == 0 else 8192
            PT = h2fm if qc == 0 else bufA
            wt("t", "S", exp_m[(qc, pair, kb)])
            mkey = None
            for h in range(2):
                def mm(e, pair=pair, kb=kb, h=h, col0=col0, HS=HS, PT=PT, nkb=nkb):
                    return e.matmul(
                        ps_pv[0:65, h * 512 + col0:h * 512 + 512],
                        v1[:, kb * 65:kb * 65 + 65],
                        PT[:, h * HS + kb * 512 + col0:h * HS + kb * 512 + 512],
                        start=(kb == 0), stop=(kb == nkb - 1))
                if h == 1:
                    mkey = op("t", mm, inc="T")
                else:
                    op("t", mm)
            pv_km[(qc, pair, kb)] = mkey

        def emit_epilogue(qc, pair):
            nkb = 4 * qc + 4
            wt("v", "T", pv_km[(qc, pair, nkb - 1)])
            if qc > 0:
                wt("v", "T", m_wo.get((qc - 1, 4 * (qc - 1) + 3)))
                wt("v", "S", m_recip[qc - 1])
            op("v", lambda e, pair=pair: e.tensor_copy(
                out=attnq[0:64, pair * 512:(pair + 1) * 512], in_=ps_pv[0:64, 0:512]))
            op("v", lambda e, pair=pair: e.tensor_copy(
                out=attnq[64:128, pair * 512:(pair + 1) * 512], in_=ps_pv[0:64, 512:1024]))
            for hh in range(2):
                h = 2 * pair + hh
                b, c = h % 4, h // 4

                def cp(e, hh=hh, b=b, c=c):
                    return e.tensor_copy(out=sums[32 * b:32 * b + 1, 512 * c:512 * c + 512],
                                         in_=ps_pv[64:65, hh * 512:hh * 512 + 512])
                if hh == 1:
                    m_ep[qc][pair] = op("v", cp, inc="V")
                else:
                    op("v", cp)

        def emit_recip(qc):
            wt("s", "V", m_ep[qc][7])
            m_ln = op("s", lambda e: e.activation(out=sums[:, :], in_=sums[:, :], func=AF.Ln,
                                                  bias=zerob[:, 0:1]), inc="S")
            ssync(m_ln)
            m_recip[qc] = op("s", lambda e: e.activation(out=sums[:, :], in_=sums[:, :], func=AF.Exp,
                                                         bias=zerob[:, 0:1], scale=-1.0), inc="S")

        def emit_bcast_norm(qc):
            vsync(m_ep[qc][7])
            for pair in range(NPAIR):
                b, c = (2 * pair) % 4, (2 * pair) // 4
                wt("t", "S", m_recip[qc])
                if pair >= 2:
                    wt("t", "V", m_nm[qc][pair - 2])
                if pair < 2:
                    wt("t", "V", m_woE[NDC - 1])
                m_bc[qc][pair] = op("t", lambda e, b=b, c=c, pair=pair: e.matmul(
                    ps_w[:, (pair % 2) * 512:(pair % 2) * 512 + 512],
                    fsel_sb[32 * b:32 * b + 64, :], sums[32 * b:32 * b + 64, 512 * c:512 * c + 512],
                    start=True, stop=True), inc="T")
                wt("v", "T", m_bc[qc][pair])
                m_nm[qc][pair] = op("v", lambda e, pair=pair: e.tensor_tensor(
                    out=attnq[:, pair * 512:(pair + 1) * 512],
                    in0=attnq[:, pair * 512:(pair + 1) * 512],
                    in1=ps_w[:, (pair % 2) * 512:(pair % 2) * 512 + 512], op=A.mult), inc="V")

        def emit_wo(qc, k):
            tb = 4 * qc + k
            wt("t", "V", m_nm[qc][7])
            if k >= 1:
                wt("t", "V", m_yev[(qc, tb - 1)])
            for dc in range(NDC):
                for ih in range(2):
                    def mm(e, qc=qc, k=k, dc=dc, ih=ih):
                        return e.matmul(ps_w[:, ih * 512:(ih + 1) * 512],
                                        attnq[:, dc * 512 + k * 128:dc * 512 + k * 128 + 128],
                                        wT[:, dc * 1024 + ih * 512:dc * 1024 + (ih + 1) * 512],
                                        start=(dc == 0), stop=(dc == NDC - 1))
                    if dc == NDC - 1 and ih == 1:
                        m_wo[(qc, tb)] = op("t", mm, inc="T")
                    else:
                        op("t", mm)

        def emit_yev(qc, k):
            tb = 4 * qc + k
            sl = k % 2
            wt("v", "T", m_wo[(qc, tb)])
            if k >= 2:
                if qc == NQC - 1:
                    vsync(m_nrm[(qc, tb - 2)])
                else:
                    wt("v", "G", m_nrm[(qc, tb - 2)])
            m_yev[(qc, tb)] = op("v", lambda e, sl=sl: e.tensor_copy(
                out=cs_y[:, sl * 1024:sl * 1024 + 1024], in_=ps_w[:, :]), inc="V")
            vsync(m_yev[(qc, tb)])
            op("v", lambda e, k=k, sl=sl: e.bn_stats(
                out=stats[:, k * 12:k * 12 + 6], in_=cs_y[:, sl * 1024:sl * 1024 + 512]))
            return op("v", lambda e, k=k, sl=sl: e.bn_stats(
                out=stats[:, k * 12 + 6:k * 12 + 12], in_=cs_y[:, sl * 1024 + 512:sl * 1024 + 1024]),
                inc="V")

        def ln_core(ks, statbase, mvbase, nmx, ivx, vepbase, mstats):
            vsync(mstats)
            m_ag = None
            for k in ks:
                m_ag = op("v", lambda e, k=k, statbase=statbase, mvbase=mvbase: e.bn_aggr(
                    out=mv[:, mvbase + k * 2:mvbase + k * 2 + 2],
                    in_=stats[:, statbase + k * 12:statbase + k * 12 + 12]), inc="V")
            vsync(m_ag)
            k0, w = ks[0], len(ks)
            op("v", lambda e, k0=k0, w=w, mvbase=mvbase, nmx=nmx: e.tensor_scalar(
                out=nmx[:, k0:k0 + w], in0=mv[:, mvbase + 2 * k0:mvbase + 2 * k0 + 2 * w:2],
                scalar1=-1.0, scalar2=None, op0=A.mult))
            mfv = op("v", lambda e, k0=k0, w=w, mvbase=mvbase, vepbase=vepbase: e.tensor_scalar(
                out=vep[:, vepbase + k0:vepbase + k0 + w],
                in0=mv[:, mvbase + 2 * k0 + 1:mvbase + 2 * k0 + 2 * w:2],
                scalar1=EPS, scalar2=None, op0=A.add), inc="V")
            wt("g", "V", mfv)
            mg = op("g", lambda e, k0=k0, w=w, ivx=ivx, vepbase=vepbase: e.tensor_tensor(
                out=ivx[:, k0:k0 + w], in0=vep[:, vepbase + k0:vepbase + k0 + w],
                in1=neghalf[:, k0:k0 + w], op=A.pow), inc="G")
            return mfv, mg

        def emit_ln2_half(qc, ks, mstats):
            fast = (qc == NQC - 1)
            eng, esem, syn = ("v", "V", vsync) if fast else ("g", "G", gsync)
            mfv, mg = ln_core(ks, 0, 0, nm2, iv2, 0, mstats)
            for k in ks:
                tb = 4 * qc + k
                sl = k % 2
                wt(eng, "V", mfv)
                wt(eng, "V", m_yev[(qc, tb)])
                wt(eng, "G", mg)
                if k != ks[0]:
                    syn(m_add2[(qc, tb - 1)])
                if fast and qc >= 1:
                    wt("v", "G", m_add4.get((qc - 1, 4 * (qc - 1) + 3)))
                m_nrm[(qc, tb)] = op(eng, lambda e, k=k, sl=sl: e.tensor_scalar(
                    out=lnbuf[:, :], in0=cs_y[:, sl * 1024:sl * 1024 + 1024],
                    scalar1=nm2[:, k:k + 1], scalar2=iv2[:, k:k + 1],
                    op0=A.add, op1=A.mult), inc=esem)
                syn(m_nrm[(qc, tb)])
                m_add2[(qc, tb)] = op(eng, lambda e, tb=tb: e.tensor_tensor(
                    out=x_sb[:, tb * 1024:(tb + 1) * 1024],
                    in0=x_sb[:, tb * 1024:(tb + 1) * 1024],
                    in1=lnbuf[:, :], op=A.add), inc=esem)

        def emit_ln3_half(qc, ks):
            fast = (qc == NQC - 1)
            eng, esem, syn = ("v", "V", vsync) if fast else ("g", "G", gsync)
            m_st = None
            for k in ks:
                tb = 4 * qc + k
                if fast:
                    vsync(m_add2[(qc, tb)])
                else:
                    wt("v", "G", m_add2[(qc, tb)])
                op("v", lambda e, k=k, tb=tb: e.bn_stats(
                    out=stats[:, 48 + k * 12:48 + k * 12 + 6], in_=x_sb[:, tb * 1024:tb * 1024 + 512]))
                m_st = op("v", lambda e, k=k, tb=tb: e.bn_stats(
                    out=stats[:, 48 + k * 12 + 6:48 + k * 12 + 12],
                    in_=x_sb[:, tb * 1024 + 512:tb * 1024 + 1024]), inc="V")
            mfv, mg = ln_core(ks, 48, 8, nm3, iv3, 4, m_st)
            for k in ks:
                tb = 4 * qc + k
                sl = k % 2
                wt(eng, "G", mg)
                syn(m_add2[(qc, tb)])
                if (qc, tb - 2) in m_h2T:
                    wt(eng, "T", m_h2T[(qc, tb - 2)])
                wt(eng, "V", mfv)
                m_n3[(qc, tb)] = op(eng, lambda e, k=k, tb=tb, sl=sl: e.tensor_scalar(
                    out=tmbf[:, sl * 1024:sl * 1024 + 1024],
                    in0=x_sb[:, tb * 1024:(tb + 1) * 1024],
                    scalar1=nm3[:, k:k + 1], scalar2=iv3[:, k:k + 1],
                    op0=A.add, op1=A.mult), inc=esem)

        def emit_h2T(qc, k):
            tb = 4 * qc + k
            sl = k % 2
            wt("t", "V" if qc == NQC - 1 else "G", m_n3[(qc, tb)])
            if k == 0:
                wt("t", "V", m_yev[(qc, 4 * qc + 3)])
            if (qc, tb - 1) in m_h2E:
                wt("t", "V", m_h2E[(qc, tb - 1)])
            for dc in range(NDC):
                def tr(e, dc=dc, sl=sl):
                    return e.transpose(psw_bf[:, sl * 1024 + dc * 128:sl * 1024 + dc * 128 + 128],
                                       tmbf[:, sl * 1024 + dc * 128:sl * 1024 + dc * 128 + 128],
                                       idb_sb[:, :])
                if dc == NDC - 1:
                    m_h2T[(qc, tb)] = op("t", tr, inc="T")
                else:
                    op("t", tr)
            wt("v", "T", m_h2T[(qc, tb)])
            m_h2E[(qc, tb)] = op("v", lambda e, k=k, sl=sl: e.tensor_copy(
                out=h2fm[:, :].rearrange("p (dc t) -> p dc t", dc=NDC, t=512)[:, :, k * 128:k * 128 + 128],
                in_=psw_bf[:, sl * 1024:sl * 1024 + 1024].rearrange("p (dc c) -> p dc c", dc=NDC, c=128)), inc="V")

        ffn_st = {}

        def emit_ffn_front(qc, part):
            if part == 0:
                wt("t", "V", m_h2E[(qc, 4 * qc + 3)])
                wt("t", "V", m_win)
                wt("t", "V", m_ffnw)
                if qc > 0:
                    wt("t", "V", m_gb[qc - 1])
                m_sp = None
                for dc in range(NDC):
                    for half in range(2):
                        def mm(e, dc=dc, half=half):
                            return e.matmul(
                                ps_w[half * 64:half * 64 + 64, 0:256],
                                win_sb[:, dc * 64:(dc + 1) * 64],
                                h2fm[:, dc * 512 + half * 256:dc * 512 + (half + 1) * 256],
                                start=(dc == 0), stop=(dc == NDC - 1), tile_position=(0, half * 64),
                                skip_group_check=True)
                        if dc == NDC - 1 and half == 1:
                            m_sp = op("t", mm, inc="T")
                        else:
                            op("t", mm)
                wt("v", "T", m_sp)
                if qc > 0:
                    wt("v", "T", m_fpl[qc - 1])
                m_sev = op("v", lambda e: e.tensor_copy(out=s_sbv, in_=ps_w[:, 0:256]), inc="V")
                ffn_st[qc] = {"sev": m_sev, "relu": {}, "acc": {}}
            fst = ffn_st[qc]
            m_sev, m_relu, m_acc = fst["sev"], fst["relu"], fst["acc"]

            fastq = (qc == NQC - 1)

            def step(i, qc=qc, fastq=fastq):
                if fastq:
                    wt("s", "V", m_sev if i < 2 else m_acc[i - 2])
                    m_relu[i] = op("s", lambda e, i=i: e.activation(
                        out=hbuf[i % 2], in_=s_sbv, func=AF.Relu,
                        bias=b1pd[:, i:i + 1], scale=w1d[:, i:i + 1]), inc="S")
                    wt("v", "S", m_relu[i])
                    if i >= 2:
                        vsync(m_acc[i - 2])
                    elif m_accL[0] is not None:
                        vsync(m_accL[0])
                elif i < 2:
                    vsync(m_sev)
                    if m_accL[0] is not None:
                        vsync(m_accL[0])
                else:
                    vsync(m_acc[i - 2])
                if not fastq:
                    m_pre = op("v", lambda e, i=i: e.tensor_scalar(
                        out=hbuf[i % 2], in0=s_sbv, scalar1=w1d[:, i:i + 1],
                        scalar2=b1pd[:, i:i + 1], op0=A.mult, op1=A.add), inc="V")
                    vsync(m_pre)
                    m_relu[i] = op("v", lambda e, i=i: e.tensor_scalar(
                        out=hbuf[i % 2], in0=hbuf[i % 2], scalar1=0.0, scalar2=None,
                        op0=A.max), inc="V")
                    vsync(m_relu[i])
                if i < 2:
                    m_acc[i] = op("v", lambda e, i=i: e.tensor_scalar(
                        out=gsl[i], in0=hbuf[i], scalar1=w2d[:, i:i + 1],
                        scalar2=(b2d[:, 0:1] if i == 0 else zerob[:, 0:1]),
                        op0=A.mult, op1=A.add), inc="V")
                else:
                    m_acc[i] = op("v", lambda e, i=i: e.scalar_tensor_tensor(
                        out=gsl[i % 2], in0=hbuf[i % 2], scalar=w2d[:, i:i + 1], in1=gsl[i % 2],
                        op0=A.mult, op1=A.add), inc="V")
                if i == 15:
                    vsync(m_acc[15])
                    m_gb[qc] = op("v", lambda e: e.tensor_tensor(
                        out=gb, in0=gsl[0], in1=gsl[1], op=A.add), inc="V")
                    m_accL[0] = m_acc[15]
            for i in (range(0, 8) if part == 0 else range(8, 16)):
                strickle.append(lambda i=i: step(i))

        def emit_fproj(qc, tt):
            hb = 64 * (tt // 2)
            gcol = (tt % 2) * 128
            wt("t", "V", m_gb[qc] if tt == 0 else m_fe[(qc, tt - 1)])
            m_fp = None
            for ih in range(2):
                def mm(e, hb=hb, gcol=gcol, ih=ih):
                    return e.matmul(ps_w[:, ih * 512:(ih + 1) * 512],
                                    gb[hb:hb + 64, gcol:gcol + 128],
                                    woutd[hb:hb + 64, ih * 512:(ih + 1) * 512],
                                    start=True, stop=True, tile_position=(hb, 0))
                if ih == 1:
                    m_fp = op("t", mm, inc="T")
                else:
                    op("t", mm)
            m_fpl[qc] = m_fp
            wt("v", "T", m_fp)
            if tt >= 1:
                if qc == NQC - 1:
                    vsync(m_nrm4.get((qc, tt - 1)))
                else:
                    wt("v", "G", m_nrm4.get((qc, tt - 1)))
            for ih in range(2):
                def ev(e, ih=ih):
                    return e.tensor_tensor(
                        out=f_sb[:, ih * 512:(ih + 1) * 512],
                        in0=ps_w[:, ih * 512:(ih + 1) * 512],
                        in1=boutb[:, ih * 512:(ih + 1) * 512], op=A.add)
                if ih == 1:
                    m_fe[(qc, tt)] = op("v", ev, inc="V")
                else:
                    op("v", ev)

        m_nrm4 = {}

        def emit_ln4_single(qc, k):
            fast = (qc == NQC - 1)
            eng, esem, syn = ("v", "V", vsync) if fast else ("g", "G", gsync)
            vsync(m_fe[(qc, k)])
            op("v", lambda e, k=k: e.bn_stats(
                out=stats[:, 96 + k * 12:96 + k * 12 + 6], in_=f_sb[:, 0:512]))
            m_st = op("v", lambda e, k=k: e.bn_stats(
                out=stats[:, 96 + k * 12 + 6:96 + k * 12 + 12], in_=f_sb[:, 512:1024]), inc="V")
            mfv, mg = ln_core((k,), 96, 16, nm4, iv4, 8, m_st)
            tb = 4 * qc + k
            wt(eng, "V", mfv)
            wt(eng, "V", m_fe[(qc, k)])
            wt(eng, "G", mg)
            syn(m_add2[(qc, tb)])
            if k >= 1:
                syn(m_add4[(qc, tb - 1)])
            m_nrm4[(qc, k)] = op(eng, lambda e, k=k: e.tensor_scalar(
                out=lnbuf[:, :], in0=f_sb[:, :],
                scalar1=nm4[:, k:k + 1], scalar2=iv4[:, k:k + 1],
                op0=A.add, op1=A.mult), inc=esem)
            syn(m_nrm4[(qc, k)])
            m_add4[(qc, tb)] = op(eng, lambda e, tb=tb: e.tensor_tensor(
                out=x_sb[:, tb * 1024:(tb + 1) * 1024],
                in0=x_sb[:, tb * 1024:(tb + 1) * 1024],
                in1=lnbuf[:, :], op=A.add), inc=esem)

        def emit_yout(qc):
            asem = "V" if qc == NQC - 1 else "G"
            for hf in range(4):
                streams["y"].append(lambda e, v=m_add4[(qc, 4 * qc + hf)], asem=asem:
                                     e.wait_ge(sems[asem][0], v))
                streams["y"].append(lambda e, v=sems["dO"][1]: (e.wait_ge(sems["dO"][0], v) if v > 0 else None))
                op("y", lambda e, qc=qc, hf=hf: e.dma_start(
                    out=yout[qc * 512 + hf * 128:qc * 512 + (hf + 1) * 128, :].rearrange("(t p) d -> p t d", p=128),
                    in_=x_sb[:, qc * 4096 + hf * 1024:qc * 4096 + (hf + 1) * 1024].rearrange(
                        "p (t d) -> p t d", t=1, d=1024)), inc="dO", inck=16)

        def D_units(qc):
            def u1():
                emit_wo(qc, 0)
                emit_yev(qc, 0)
                emit_wo(qc, 1)
                ms1 = emit_yev(qc, 1)
                emit_ln2_half(qc, (0, 1), ms1)

            def u2():
                emit_wo(qc, 2)
                emit_yev(qc, 2)
                emit_wo(qc, 3)
                ms3 = emit_yev(qc, 3)
                emit_ln2_half(qc, (2, 3), ms3)

            return [
                lambda: emit_bcast_norm(qc),
                u1,
                u2,
                lambda: (emit_ln3_half(qc, (0, 1)), emit_h2T(qc, 0), emit_h2T(qc, 1)),
                lambda: (emit_ln3_half(qc, (2, 3)), emit_h2T(qc, 2), emit_h2T(qc, 3)),
                lambda: emit_ffn_front(qc, 0),
                lambda: emit_ffn_front(qc, 1),
                lambda: (emit_fproj(qc, 0), emit_ln4_single(qc, 0)),
                lambda: (emit_fproj(qc, 1), emit_ln4_single(qc, 1)),
                lambda: (emit_fproj(qc, 2), emit_ln4_single(qc, 2)),
                lambda: (emit_fproj(qc, 3), emit_ln4_single(qc, 3), emit_yout(qc)),
            ]

        if stage <= 3:
            debug_finish()
        nqc_run = NQC if stage >= 5 else (1 if stage == 4 else 0)
        do_D = stage >= 6
        dq = []
        trail = []
        prev_qc = None
        for qc in range(nqc_run):
            nkb = 4 * qc + 4
            for pair in range(NPAIR):
                for kb in range(nkb):
                    if pair == 0 and trail:
                        trail.pop(0)()
                    if pair >= 1:
                        if kb == 0:
                            wt("t", "V", m_ep[qc][pair - 2] if pair >= 2
                               else (m_ep[qc - 1][7] if qc >= 1 else None))
                        emit_pv(qc, pair - 1, kb)
                    m_sc = emit_scores(qc, pair, kb)
                    emit_exp(qc, pair, kb, m_sc)
                    if strickle:
                        strickle.pop(0)()
                    if kb == nkb // 2 - 1:
                        if dqq:
                            dqq.pop(0)()
                        elif do_D and dq and pair >= 1 and not strickle:
                            dq.pop(0)()
                if pair == 0:
                    while trail:
                        trail.pop(0)()
                    if do_D and prev_qc is not None:
                        emit_recip(prev_qc)
                if dqq:
                    dqq.pop(0)()
                elif do_D and dq and (not strickle or pair <= 1):
                    dq.pop(0)()
                if pair >= 1:
                    emit_epilogue(qc, pair - 1)

            def mk_trail(qc=qc, nkb=nkb):
                t = []
                for kb in range(nkb):
                    def f(kb=kb, qc=qc):
                        if kb == 0:
                            wt("t", "V", m_ep[qc][6])
                        emit_pv(qc, 7, kb)
                    t.append(f)
                t.append(lambda qc=qc: emit_epilogue(qc, 7))
                return t
            trail = mk_trail()
            if do_D:
                nu = D_units(qc)
                dq = nu[0:3] + dq + nu[3:]
                prev_qc = qc
        while trail:
            trail.pop(0)()
        if do_D:
            emit_recip(prev_qc)
            for u in dq:
                u()
                while strickle:
                    strickle.pop(0)()
        else:
            debug_finish()

        streams["y"].append(lambda e: e.wait_ge(sems["dO"][0], sems["dO"][1]))

        block = st.enter_context(nc.Block())

        @block.sync
        def _(eng):
            for f in streams["y"]:
                f(eng)

        @block.tensor
        def _(eng):
            for f in streams["t"]:
                f(eng)

        @block.vector
        def _(eng):
            for f in streams["v"]:
                f(eng)

        @block.scalar
        def _(eng):
            for f in streams["s"]:
                f(eng)

        @block.gpsimd
        def _(eng):
            for f in streams["g"]:
                f(eng)

    return nc


_CACHE = {}


def kernel(L0_attn_x, L0_attn_wq, L0_attn_wk, L0_attn_wv, L0_attn_wo,
           L0_ffn_Win, L0_ffn_bin_, L0_ffn_W1, L0_ffn_b1, L0_ffn_W2,
           L0_ffn_b2, L0_ffn_Wout, L0_ffn_bout):
    x = np.ascontiguousarray(np.asarray(L0_attn_x, dtype=np.float32))
    consts = _host_consts()
    if "nc" not in _CACHE:
        _CACHE["nc"] = build_nc()
    nc = _CACHE["nc"]
    shared = {
        "wq": np.ascontiguousarray(np.asarray(L0_attn_wq, dtype=np.float32)),
        "wk": np.ascontiguousarray(np.asarray(L0_attn_wk, dtype=np.float32)),
        "wv": np.ascontiguousarray(np.asarray(L0_attn_wv, dtype=np.float32)),
        "wo": np.ascontiguousarray(np.asarray(L0_attn_wo, dtype=np.float32)),
        "win": np.ascontiguousarray(np.asarray(L0_ffn_Win, dtype=np.float32)),
        "binp": np.ascontiguousarray(np.asarray(L0_ffn_bin_, dtype=np.float32).reshape(64, 1)),
        "w1": np.ascontiguousarray(np.asarray(L0_ffn_W1, dtype=np.float32)),
        "b1": np.ascontiguousarray(np.asarray(L0_ffn_b1, dtype=np.float32)),
        "w2": np.ascontiguousarray(np.asarray(L0_ffn_W2, dtype=np.float32)),
        "b2": np.ascontiguousarray(np.asarray(L0_ffn_b2, dtype=np.float32).reshape(64, 1)),
        "wout": np.ascontiguousarray(np.asarray(L0_ffn_Wout, dtype=np.float32)),
        "bout": np.ascontiguousarray(np.asarray(L0_ffn_bout, dtype=np.float32).reshape(1, D)),
    }
    shared.update(consts)
    in_maps = []
    for i in range(NCORE):
        m = dict(shared)
        m["x"] = np.ascontiguousarray(x[i])
        in_maps.append(m)
    res = run_bass_kernel_spmd(nc, in_maps, core_ids=list(range(NCORE)))
    out = np.stack([np.asarray(r["y"]) for r in res.results], axis=0)
    return out.astype(np.float32)


# version 29
# speedup vs baseline: 1.0163x; 1.0163x over previous
import os
import numpy as np
import ml_dtypes
import concourse.bass as bass
import concourse.mybir as mybir
from concourse.bass_utils import run_bass_kernel_spmd
from concourse.alu_op_type import AluOpType as A
from contextlib import ExitStack

F32 = mybir.dt.float32
BF16 = mybir.dt.bfloat16
AF = mybir.ActivationFunctionType

NCORE = 8
S_TOK = 2048
D = 1024
NTB = 16
NDC = 8
NQC = 4
NPAIR = 8
EPS = 1e-5
ESHIFT = -8.0


def _host_consts():
    ident = np.eye(128, dtype=np.float32)
    identb = np.eye(128, dtype=ml_dtypes.bfloat16)
    j = np.arange(64)
    p = np.arange(128)
    tria = np.zeros((128, 256), dtype=np.float32)
    tria[0:64, 0:128] = (p[None, :] > j[:, None]).astype(np.float32)
    tria[0:64, 128:256] = (p[None, :] > j[:, None] + 64).astype(np.float32)
    tria[64:128] = tria[0:64]
    negi = np.zeros((128, 256), dtype=np.float32)
    negi[j, j] = -1e30
    negi[j, 128 + 64 + j] = -1e30
    negi[64:128] = negi[0:64]
    fsel = np.zeros((128, 128), dtype=np.float32)
    fsel[0, 0:64] = 1.0
    fsel[32, 64:128] = 1.0
    fsel[64:128] = fsel[0:64]
    t = np.arange(S_TOK, dtype=np.float64)[:, None]
    f = 1000.0 ** (-np.arange(32, dtype=np.float64) / 32.0)[None, :]
    tf = t * f
    two_pi = 2.0 * np.pi

    def hilo(theta):
        y = np.mod(theta, two_pi) - np.pi
        hi = y.astype(np.float32)
        lo = (y - hi.astype(np.float64)).astype(np.float32)
        return hi, lo
    thc = np.concatenate([tf, tf], axis=1) + np.pi / 2.0
    ths = np.concatenate([-tf, tf], axis=1)
    yc, ycd = hilo(thc)
    ys, ysd = hilo(ths)
    return {
        "c_ident": ident, "c_identb": identb,
        "c_tria": tria.astype(ml_dtypes.bfloat16), "c_negi": negi.astype(ml_dtypes.bfloat16),
        "c_fsel": fsel,
        "c_yc": yc, "c_ycd": ycd, "c_ys": ys, "c_ysd": ysd,
    }


def build_nc():
    stage = int(os.environ.get("KSTAGE", "6"))
    nc = bass.Bass("TRN2", target_bir_lowering=False)
    xin = nc.dram_tensor("x", [S_TOK, D], F32, kind="ExternalInput").ap()
    wq = nc.dram_tensor("wq", [D, D], F32, kind="ExternalInput").ap()
    wk = nc.dram_tensor("wk", [D, 64], F32, kind="ExternalInput").ap()
    wv = nc.dram_tensor("wv", [D, 64], F32, kind="ExternalInput").ap()
    wo = nc.dram_tensor("wo", [D, D], F32, kind="ExternalInput").ap()
    win = nc.dram_tensor("win", [D, 64], F32, kind="ExternalInput").ap()
    binp = nc.dram_tensor("binp", [64, 1], F32, kind="ExternalInput").ap()
    w1 = nc.dram_tensor("w1", [64, 16], F32, kind="ExternalInput").ap()
    b1 = nc.dram_tensor("b1", [64, 16], F32, kind="ExternalInput").ap()
    w2 = nc.dram_tensor("w2", [64, 16], F32, kind="ExternalInput").ap()
    b2 = nc.dram_tensor("b2", [64, 1], F32, kind="ExternalInput").ap()
    wout = nc.dram_tensor("wout", [64, D], F32, kind="ExternalInput").ap()
    bout = nc.dram_tensor("bout", [1, D], F32, kind="ExternalInput").ap()
    c_ident = nc.dram_tensor("c_ident", [128, 128], F32, kind="ExternalInput").ap()
    c_identb = nc.dram_tensor("c_identb", [128, 128], BF16, kind="ExternalInput").ap()
    c_tria = nc.dram_tensor("c_tria", [128, 256], BF16, kind="ExternalInput").ap()
    c_negi = nc.dram_tensor("c_negi", [128, 256], BF16, kind="ExternalInput").ap()
    c_fsel = nc.dram_tensor("c_fsel", [128, 128], F32, kind="ExternalInput").ap()
    c_yc = nc.dram_tensor("c_yc", [S_TOK, 64], F32, kind="ExternalInput").ap()
    c_ycd = nc.dram_tensor("c_ycd", [S_TOK, 64], F32, kind="ExternalInput").ap()
    c_ys = nc.dram_tensor("c_ys", [S_TOK, 64], F32, kind="ExternalInput").ap()
    c_ysd = nc.dram_tensor("c_ysd", [S_TOK, 64], F32, kind="ExternalInput").ap()
    yout = nc.dram_tensor("y", [S_TOK, D], F32, kind="ExternalOutput").ap()

    with ExitStack() as st:
        def sb(name, shape, dt=F32):
            return st.enter_context(nc.sbuf_tensor(name, shape, dt))

        x_sb = sb("x_sb", [128, NTB * 1024])
        bufA = sb("bufA", [128, 16384], BF16)
        qfm = sb("qfm", [128, 16384], BF16)
        wT = sb("wT", [128, 8192], BF16)
        h2fm = sb("h2fm", [128, 4096], BF16)
        attnq = sb("attnq", [128, 4096], BF16)
        kvd = sb("kvd", [128, 2048], BF16)
        v1 = sb("v1", [128, 16 * 65], BF16)
        wstage = sb("wstage", [128, 512])
        tmroll = sb("tmroll", [128, 1024])
        qtm = sb("qtm", [128, 1024], BF16)
        qrope = sb("qrope", [128, 2048], BF16)
        cs_y = sb("cs_y", [128, 2048], BF16)
        sums = sb("sums", [128, 2048])
        lnbuf = sb("lnbuf", [128, 1024], BF16)
        f_sb = sb("f_sb", [128, 1024], BF16)
        kvw = sb("kvw", [128, 1024], BF16)
        woutd = sb("woutd", [128, 1024], BF16)
        boutb = sb("boutb", [128, 1024], BF16)
        w1d = sb("w1d", [128, 16])
        b1pd = sb("b1pd", [128, 16])
        w2d = sb("w2d", [128, 16])
        b2d = sb("b2d", [128, 1])
        stats = sb("stats", [128, 192])
        bind = stats[:, 174:175]
        b1d = stats[:, 176:192]
        win_sb = kvw[:, 0:512]
        mv = sb("mv", [128, 32])
        nm1 = sb("nm1", [128, 16])
        iv1 = sb("iv1", [128, 16])
        vep = sb("vep", [128, 16])
        nm2 = sb("nm2", [128, 4]); iv2 = sb("iv2", [128, 4])
        nm3 = sb("nm3", [128, 4]); iv3 = sb("iv3", [128, 4])
        nm4 = sb("nm4", [128, 4]); iv4 = sb("iv4", [128, 4])
        neghalf = sb("neghalf", [128, 16])
        negone = sb("negone", [128, 1])
        expb = sb("expb", [128, 1])
        ones1 = sb("ones1", [1, 128])
        zerob = sb("zerob", [128, 1])
        junk = sb("junk", [128, 16])
        junkg = sb("junkg", [128, 16])
        id_sb = sb("id_sb", [128, 128])
        idb_sb = sb("idb_sb", [128, 128], BF16)
        tria_sb = sb("tria_sb", [128, 256], BF16)
        negi_sb = sb("negi_sb", [128, 256], BF16)
        fsel_sb = sb("fsel_sb", [128, 128])

        ps_st = st.enter_context(nc.psum_tensor("ps_st", [128, 2048], F32))
        ps_pv = st.enter_context(nc.psum_tensor("ps_pv", [128, 1024], F32))
        ps_w = st.enter_context(nc.psum_tensor("ps_w", [128, 1024], F32))

        s_sbv = qtm[:, 0:256]
        hbuf = [qtm[:, 256:512], qtm[:, 512:768]]
        gb = qtm[:, 768:1024]
        gsl = [qrope[:, :].bitcast(F32)[:, 0:256], qrope[:, :].bitcast(F32)[:, 256:512]]

        tmbf = tmroll[:, :].bitcast(BF16)
        qfm_f32 = qfm[:, :].bitcast(F32)
        psst_bf = ps_st[:, :].bitcast(BF16)
        psw_bf = ps_w[:, :].bitcast(BF16)
        pspv_bf = ps_pv[:, :].bitcast(BF16)
        wstage_bf = wstage[:, :].bitcast(BF16)

        sems = {}
        for s in ("dX", "dW", "dO", "dQ", "T", "V", "S", "G"):
            sems[s] = [st.enter_context(nc.semaphore(s)), 0]
        streams = {k: [] for k in ("t", "v", "s", "g", "y")}

        def op(eng, build, inc=None, inck=1):
            if inc is not None:
                sems[inc][1] += inck
            val = sems[inc][1] if inc is not None else None

            def f(e, build=build, inc=inc, inck=inck):
                ins = build(e)
                if inc is not None:
                    ins.then_inc(sems[inc][0], inck)
            streams[eng].append(f)
            return val

        def wt(eng, sem, val):
            if val is None or val <= 0:
                return
            streams[eng].append(lambda e, s=sem, v=val: e.wait_ge(sems[s][0], v))

        def vsync(val):
            wt("v", "V", val)

        def gsync(val):
            wt("g", "G", val)

        def ssync(val):
            wt("s", "S", val)

        _fr = [0, 0]

        def vfence(inc=None):
            _fr[0] += 1
            c = _fr[0] % 16
            return op("v", lambda e, c=c: e.memset(junk[:, c:c + 1], 0.0), inc=inc)

        def gfence(inc=None):
            _fr[1] += 1
            c = _fr[1] % 16
            return op("g", lambda e, c=c: e.memset(junkg[:, c:c + 1], 0.0), inc=inc)

        _done = [False]

        def debug_finish():
            if _done[0]:
                return
            _done[0] = True
            for semname in ("T", "V", "S", "G"):
                v = sems[semname][1]
                if v > 0:
                    streams["y"].append(lambda e, sn=semname, v=v: e.wait_ge(sems[sn][0], v))
            op("y", lambda e: e.dma_start(
                out=yout.rearrange("(t p) d -> p t d", p=128),
                in_=x_sb[:, :].rearrange("p (t d) -> p t d", t=NTB, d=1024)),
                inc="dO", inck=16)

        op("v", lambda e: e.memset(expb[:, :], ESHIFT))
        op("v", lambda e: e.memset(zerob[:, :], 0.0))
        op("v", lambda e: e.memset(neghalf[:, :], -0.5))
        m_negone = op("v", lambda e: e.memset(negone[:, :], -1.0), inc="V")
        op("v", lambda e: e.memset(sums[:, :], 1.0))
        m_ones = op("v", lambda e: e.memset(
            v1[:, :].rearrange("p (t c) -> p t c", t=16, c=65)[:, :, 64:65], 1.0), inc="V")

        def dma(build):
            return op("y", build, inc="dW", inck=16)

        m_xg = [None] * 4

        dma(lambda e: e.dma_start(out=id_sb[:, :], in_=c_ident[:, :]))
        m_idB = dma(lambda e: e.dma_start(out=idb_sb[:, :], in_=c_identb[:, :]))
        streams["y"].append(lambda e, v=m_idB: e.wait_ge(sems["dW"][0], v))
        dma(lambda e: e.dma_start(out=tria_sb[:, :], in_=c_tria[:, :]))
        dma(lambda e: e.dma_start(out=negi_sb[:, :], in_=c_negi[:, :]))
        dma(lambda e: e.dma_start(out=fsel_sb[:, :], in_=c_fsel[:, :]))
        dma(lambda e: e.dma_start(out=w1d[0:64, :], in_=w1[:, :]))
        dma(lambda e: e.dma_start(out=b1d[0:64, :], in_=b1[:, :]))
        dma(lambda e: e.dma_start(out=w2d[0:64, :], in_=w2[:, :]))
        dma(lambda e: e.dma_start(out=b2d[0:64, :], in_=b2[:, :]))
        m_small = dma(lambda e: e.dma_start(out=bind[0:64, :], in_=binp[:, :]))

        def tab_view(tab, half=None):
            v = tab.rearrange("(t p) j -> p t j", p=128)
            if half is None:
                return v
            return v[:, half * 8:(half + 1) * 8, :]

        m_tabL = None
        for i, tab in enumerate((c_yc, c_ycd, c_ys, c_ysd)):
            m_tabL = dma(lambda e, tab=tab, i=i: e.dma_start(
                out=qfm_f32[:, i * 1024:(i + 1) * 1024].rearrange("p (t j) -> p t j", t=16, j=64),
                in_=tab_view(tab)))
        streams["y"].append(lambda e, v=m_tabL: e.wait_ge(sems["dW"][0], v))
        wq_r = wq.rearrange("(ic r) d -> r ic d", r=128)
        m_wqL = [None]
        for h2 in range(2):
            for eo in range(2):
                src = wq_r[h2 * 64 + eo:h2 * 64 + 64:2, :, :]
                dst = qfm[h2 * 64 + eo * 32:h2 * 64 + eo * 32 + 32, 8192:16384].rearrange(
                    "p (ic d) -> p ic d", ic=8)
                m_wqL[0] = op("g", lambda e, src=src, dst=dst: e.dma_start(out=dst, in_=src),
                              inc="dQ", inck=16)
        streams["g"].append(lambda e, v=m_wqL[0]: e.wait_ge(sems["dQ"][0], v))
        for g in range(4):
            m_xg[g] = op("g", lambda e, g=g: e.dma_start(
                out=x_sb[:, g * 4096:(g + 1) * 4096].rearrange("p (t d) -> p t d", t=4, d=1024),
                in_=xin.rearrange("(t p) d -> p t d", p=128)[:, g * 4:(g + 1) * 4, :]),
                inc="dX", inck=16)
            streams["g"].append(lambda e, v=m_xg[g]: e.wait_ge(sems["dX"][0], v))
        m_x = m_xg[3]
        def emit_sin(hi_col, lo_col, dstcol, prev_gate):
            if prev_gate is not None:
                ssync(prev_gate)
                wt("v", "S", prev_gate)
            wt("v", "dW", m_tabL)
            m_add = op("v", lambda e, hi_col=hi_col, lo_col=lo_col: e.tensor_tensor(
                out=tmroll[:, :], in0=qfm_f32[:, hi_col:hi_col + 1024],
                in1=qfm_f32[:, lo_col:lo_col + 1024], op=A.add), inc="V")
            wt("s", "V", m_add)
            return op("s", lambda e, dstcol=dstcol: e.activation(
                out=cs_y[:, dstcol:dstcol + 1024], in_=tmroll[:, :],
                func=AF.Sin, bias=zerob[:, 0:1], scale=-1.0), inc="S")

        m_sin1 = emit_sin(0, 1024, 0, None)
        m_sin2 = emit_sin(2048, 3072, 1024, m_sin1)

        m_wkL = dma(lambda e: e.dma_start(
            out=wstage[:, :].rearrange("p (c h) -> p c h", c=8, h=64),
            in_=wk.rearrange("(c p) h -> p c h", p=128)))
        wt("v", "dW", m_wkL)
        wsv = wstage[:, :].rearrange("p (c e) -> p c e", c=512, e=1)
        m_wkC = None
        for dc in range(NDC):
            op("v", lambda e, dc=dc: e.tensor_copy(
                out=kvw[:, dc * 128 + 0:dc * 128 + 32],
                in_=wsv[:, dc * 64 + 0:dc * 64 + 64:2, 0]))
            m_wkC = op("v", lambda e, dc=dc: e.tensor_copy(
                out=kvw[:, dc * 128 + 32:dc * 128 + 64],
                in_=wsv[:, dc * 64 + 1:dc * 64 + 64:2, 0]), inc="V")
        streams["y"].append(lambda e, v=m_wkC: e.wait_ge(sems["V"][0], v))
        m_wvL = dma(lambda e: e.dma_start(
            out=wstage[:, :].rearrange("p (c h) -> p c h", c=8, h=64),
            in_=wv.rearrange("(c p) h -> p c h", p=128)))
        wt("v", "dW", m_wvL)
        m_kvw = None
        for dc in range(NDC):
            m_kvw = op("v", lambda e, dc=dc: e.tensor_copy(
                out=kvw[:, dc * 128 + 64:dc * 128 + 128],
                in_=wstage[:, dc * 64:dc * 64 + 64]), inc="V")
        streams["y"].append(lambda e, v=m_kvw: e.wait_ge(sems["V"][0], v))
        m_wout = None
        for half in range(2):
            m_woutL = dma(lambda e, half=half: e.dma_start(
                out=wstage[0:64, :], in_=wout[:, half * 512:(half + 1) * 512]))
            wt("v", "dW", m_woutL)
            op("v", lambda e, half=half: e.tensor_copy(
                out=woutd[0:64, half * 512:(half + 1) * 512], in_=wstage[0:64, :]))
            m_wout = op("v", lambda e, half=half: e.tensor_copy(
                out=woutd[64:128, half * 512:(half + 1) * 512], in_=wstage[0:64, :]), inc="V")
            streams["y"].append(lambda e, v=m_wout: e.wait_ge(sems["V"][0], v))
        m_ones1 = op("v", lambda e: e.memset(ones1[:, :], 1.0), inc="V")
        m_boutc = None
        m_bmm = None
        for half in range(2):
            m_boutL = dma(lambda e, half=half: e.dma_start(
                out=wstage[0:1, :], in_=bout[:, half * 512:(half + 1) * 512]))
            wt("t", "dW", m_boutL)
            wt("t", "V", m_ones1)
            if half == 1:
                wt("t", "V", m_boutc)
            m_bmm = op("t", lambda e: e.matmul(
                ps_w[:, 0:512], ones1[0:1, :], wstage[0:1, :], start=True, stop=True), inc="T")
            wt("v", "T", m_bmm)
            m_boutc = op("v", lambda e, half=half: e.tensor_copy(
                out=boutb[:, half * 512:(half + 1) * 512], in_=ps_w[:, 0:512]), inc="V")
            streams["y"].append(lambda e, v=m_boutc: e.wait_ge(sems["V"][0], v))
        op("v", lambda e: e.tensor_copy(out=w1d[64:128, :], in_=w1d[0:64, :]))
        op("v", lambda e: e.tensor_copy(out=b1d[64:128, :], in_=b1d[0:64, :]))
        op("v", lambda e: e.tensor_copy(out=w2d[64:128, :], in_=w2d[0:64, :]))
        op("v", lambda e: e.tensor_copy(out=b2d[64:128, :], in_=b2d[0:64, :]))
        m_dups = op("v", lambda e: e.tensor_copy(out=bind[64:128, :], in_=bind[0:64, :]), inc="V")
        vsync(m_dups)
        m_ffnw = op("v", lambda e: e.scalar_tensor_tensor(
            out=b1pd[:, :], in0=w1d[:, :], scalar=bind[:, 0:1], in1=b1d[:, :],
            op0=A.mult, op1=A.add), inc="V")

        m_ln1g = [None] * 4
        for g in range(4):
            wt("v", "dX", m_xg[g])
            if g == 0:
                vsync(m_ffnw)
            m_st1 = None
            for tb in range(4 * g, 4 * g + 4):
                op("v", lambda e, tb=tb: e.bn_stats(out=stats[:, tb * 12:tb * 12 + 6],
                                                    in_=x_sb[:, tb * 1024:tb * 1024 + 512]))
                m_st1 = op("v", lambda e, tb=tb: e.bn_stats(out=stats[:, tb * 12 + 6:tb * 12 + 12],
                                                            in_=x_sb[:, tb * 1024 + 512:tb * 1024 + 1024]),
                           inc="V")
            vsync(m_st1)
            m_ag1 = None
            for tb in range(4 * g, 4 * g + 4):
                m_ag1 = op("v", lambda e, tb=tb: e.bn_aggr(out=mv[:, tb * 2:tb * 2 + 2],
                                                           in_=stats[:, tb * 12:tb * 12 + 12]), inc="V")
            vsync(m_ag1)
            op("v", lambda e, g=g: e.tensor_scalar(
                out=nm1[:, 4 * g:4 * g + 4], in0=mv[:, 8 * g:8 * g + 8:2],
                scalar1=-1.0, scalar2=None, op0=A.mult))
            m_ln1v = op("v", lambda e, g=g: e.tensor_scalar(
                out=vep[:, 4 * g:4 * g + 4], in0=mv[:, 8 * g + 1:8 * g + 8:2],
                scalar1=EPS, scalar2=None, op0=A.add), inc="V")
            wt("g", "V", m_ln1v)
            m_ln1g[g] = op("g", lambda e, g=g: e.tensor_tensor(
                out=iv1[:, 4 * g:4 * g + 4], in0=vep[:, 4 * g:4 * g + 4],
                in1=neghalf[:, 4 * g:4 * g + 4], op=A.pow), inc="G")

        m_wqT = [None] * NDC
        m_wqE = [None] * NDC
        for ic in range(NDC):
            wt("t", "dQ", m_wqL[0])
            if ic == 0:
                wt("t", "dW", m_idB)
            if ic >= 1:
                wt("t", "S", m_wqE[ic - 1])
            for dc in range(NDC):
                def tr(e, ic=ic, dc=dc):
                    return e.transpose(
                        psst_bf[:, 2048 + dc * 128:2048 + dc * 128 + 128],
                        qfm[:, 8192 + ic * 1024 + dc * 128:8192 + ic * 1024 + (dc + 1) * 128],
                        idb_sb[:, :])
                if dc == NDC - 1:
                    m_wqT[ic] = op("t", tr, inc="T")
                else:
                    op("t", tr)
            wt("s", "T", m_wqT[ic])
            m_wqE[ic] = op("s", lambda e, ic=ic: e.activation(
                out=wT[:, :].rearrange("p (dc i) -> p dc i", dc=NDC, i=1024)[:, :, ic * 128:ic * 128 + 128],
                in_=psst_bf[:, 2048:3072].rearrange("p (s c) -> p s c", s=8, c=128),
                func=AF.Copy, bias=0.0), inc="S")

        m_qp = [None] * NTB
        m_qtmE = [None] * NTB
        m_qT = [None] * NTB
        m_qfm = [None] * NTB
        m_rope = [None] * NTB

        def emit_qproj(tb):
            par = tb % 2
            wt("t", "S", m_h1fm[tb])
            if tb == 1:
                wt("t", "S", m_wqE[NDC - 1])
            if tb >= 2:
                wt("t", "V", m_qtmE[tb - 2])
            for dc in range(NDC):
                for ih in range(2):
                    def mm(e, tb=tb, dc=dc, ih=ih, par=par):
                        out = ps_st[:, par * 1024 + ih * 512:par * 1024 + (ih + 1) * 512]
                        return e.matmul(out,
                                        bufA[:, dc * 2048 + tb * 128:dc * 2048 + tb * 128 + 128],
                                        wT[:, dc * 1024 + ih * 512:dc * 1024 + (ih + 1) * 512],
                                        start=(dc == 0), stop=(dc == NDC - 1))
                    if dc == NDC - 1 and ih == 1:
                        m_qp[tb] = op("t", mm, inc="T")
                    else:
                        op("t", mm)

        def emit_qrope(tb):
            par = tb % 2
            csc = cs_y[:, tb * 64:(tb + 1) * 64]
            css = cs_y[:, 1024 + tb * 64:1024 + (tb + 1) * 64]
            wt("v", "T", m_qp[tb])
            psq = ps_st[:, par * 1024:par * 1024 + 1024]
            if tb >= 1:
                vsync(m_rope[tb - 1])
            if tb >= 2 and m_qT[tb - 2] is not None:
                wt("v", "T", m_qT[tb - 2])
            op("v", lambda e, css=css, psq=psq: e.tensor_tensor(
                out=qtm[:, :].rearrange("p (h a j) -> p h a j", h=16, a=2, j=32),
                in0=psq.rearrange("p (h a j) -> p h a j", h=16, a=2, j=32)[:, :, ::-1, :],
                in1=css.rearrange("p (a j) -> p a j", a=2, j=32).unsqueeze(1).broadcast_to((128, 16, 2, 32)),
                op=A.mult))
            m_qtmE[tb] = op("v", lambda e, csc=csc, psq=psq, par=par: e.tensor_tensor(
                out=qrope[:, par * 1024:par * 1024 + 1024].rearrange("p (h f) -> p h f", h=16, f=64),
                in0=psq.rearrange("p (h f) -> p h f", h=16, f=64),
                in1=csc.unsqueeze(1).broadcast_to((128, 16, 64)), op=A.mult), inc="V")
            vsync(m_qtmE[tb])
            m_rope[tb] = op("v", lambda e, par=par: e.tensor_tensor(
                out=qrope[:, par * 1024:par * 1024 + 1024],
                in0=qrope[:, par * 1024:par * 1024 + 1024],
                in1=qtm[:, :], op=A.add), inc="V")

        def emit_qT(tb):
            par = tb % 2
            pso = par
            wt("t", "V", m_rope[tb])
            if tb < 2:
                wt("t", "V", m_boutc)
            else:
                wt("t", "S", m_qfm[tb - 2])
            for pr in range(NPAIR):
                def tr(e, pr=pr, par=par, pso=pso):
                    return e.transpose(psw_bf[:, pso * 1024 + pr * 128:pso * 1024 + pr * 128 + 128],
                                       qrope[:, par * 1024 + pr * 128:par * 1024 + (pr + 1) * 128], idb_sb[:, :])
                if pr == NPAIR - 1:
                    m_qT[tb] = op("t", tr, inc="T")
                else:
                    op("t", tr)
            wt("s", "T", m_qT[tb])
            m_qfm[tb] = op("s", lambda e, tb=tb, pso=pso: e.activation(
                out=qfm[:, :].rearrange("p (pr t) -> p pr t", pr=NPAIR, t=2048)[:, :, tb * 128:tb * 128 + 128],
                in_=psw_bf[:, pso * 1024:pso * 1024 + 1024].rearrange("p (pr c) -> p pr c", pr=NPAIR, c=128),
                func=AF.Copy, bias=0.0), inc="S")

        m_h1T = [None] * NTB
        m_h1fm = [None] * NTB
        wt("g", "S", m_sin2)
        for tb in range(NTB):
            slot = tb % 2
            if tb % 4 == 0:
                gsync(m_ln1g[tb // 4])
            if tb >= 2:
                wt("g", "T", m_h1T[tb - 2])
            m_n1 = op("g", lambda e, tb=tb, slot=slot: e.tensor_scalar(
                out=tmbf[:, slot * 1024:slot * 1024 + 1024],
                in0=x_sb[:, tb * 1024:(tb + 1) * 1024],
                scalar1=nm1[:, tb:tb + 1], scalar2=iv1[:, tb:tb + 1],
                op0=A.add, op1=A.mult), inc="G")
            wt("t", "G", m_n1)
            if tb >= 2:
                wt("t", "S", m_h1fm[tb - 2])
            for dc in range(NDC):
                def tr(e, dc=dc, slot=slot):
                    return e.transpose(
                        pspv_bf[:, slot * 1024 + dc * 128:slot * 1024 + dc * 128 + 128],
                        tmbf[:, slot * 1024 + dc * 128:slot * 1024 + dc * 128 + 128],
                        idb_sb[:, :])
                if dc == NDC - 1:
                    m_h1T[tb] = op("t", tr, inc="T")
                else:
                    op("t", tr)
            wt("s", "T", m_h1T[tb])
            m_h1fm[tb] = op("s", lambda e, tb=tb, slot=slot: e.activation(
                out=bufA[:, :].rearrange("p (dc t) -> p dc t", dc=NDC, t=2048)[:, :, tb * 128:tb * 128 + 128],
                in_=pspv_bf[:, slot * 1024:slot * 1024 + 1024].rearrange("p (dc c) -> p dc c", dc=NDC, c=128),
                func=AF.Copy, bias=0.0), inc="S")
            if tb >= 1:
                if tb >= 3:
                    emit_qT(tb - 3)
                emit_qproj(tb - 1)
                emit_qrope(tb - 1)
            if tb == NTB - 1:
                emit_qT(tb - 2)
                emit_qproj(tb)
                emit_qrope(tb)
                emit_qT(tb - 1)
                emit_qT(tb)

        if stage <= 1:
            debug_finish()
        m_kvT = [None] * 4
        m_kvE = [None] * 4
        for tc in range(4):
            if tc == 0:
                wt("t", "S", m_qfm[NTB - 1])
            wt("t", "V", m_kvw)
            wt("t", "S", m_h1fm[4 * tc + 3])
            if tc >= 1:
                wt("t", "S", m_kvE[tc - 1])
            for dc in range(NDC):
                def mm(e, tc=tc, dc=dc):
                    return e.matmul(ps_w[:, 0:512], kvw[:, dc * 128:(dc + 1) * 128],
                                    bufA[:, dc * 2048 + tc * 512:dc * 2048 + (tc + 1) * 512],
                                    start=(dc == 0), stop=(dc == NDC - 1))
                if dc == NDC - 1:
                    m_kvT[tc] = op("t", mm, inc="T")
                else:
                    op("t", mm)
            wt("s", "T", m_kvT[tc])
            m_kvE[tc] = op("s", lambda e, tc=tc: e.activation(
                out=h2fm[:, tc * 512:(tc + 1) * 512], in_=ps_w[:, 0:512],
                func=AF.Copy, bias=0.0), inc="S")


        streams["y"].append(lambda e, v=m_kvT[3]: e.wait_ge(sems["T"][0], v))
        m_winL = dma(lambda e: e.dma_start(
            out=wstage[:, :].rearrange("p (c h) -> p c h", c=8, h=64),
            in_=win.rearrange("(c p) h -> p c h", p=128)))
        wt("v", "dW", m_winL)
        wt("v", "T", m_kvT[3])
        m_win = op("v", lambda e: e.tensor_copy(out=win_sb, in_=wstage[:, :]), inc="V")

        m_vT = [None] * 2
        m_vE = [None] * 2
        for gs in range(2):
            wt("t", "S", m_kvE[3])
            if gs == 1:
                wt("t", "S", m_vE[0])
            for i in range(8):
                tb = gs * 8 + i
                def tr(e, tb=tb, i=i):
                    return e.transpose(psw_bf[:, i * 128:i * 128 + 64],
                                       h2fm[64:128, tb * 128:(tb + 1) * 128],
                                       idb_sb[64:128, 64:128])
                if i == 7:
                    m_vT[gs] = op("t", tr, inc="T")
                else:
                    op("t", tr)
            wt("s", "T", m_vT[gs])
            m_vE[gs] = op("s", lambda e, gs=gs: e.activation(
                out=v1[:, :].rearrange("p (t c) -> p t c", t=16, c=65)[:, gs * 8:gs * 8 + 8, 0:64],
                in_=psw_bf[:, 0:1024].rearrange("p (i c) -> p i c", i=8, c=128)[:, :, 0:64],
                func=AF.Copy, bias=0.0), inc="S")

        m_kT = [None] * 2
        m_ktm = [None] * 2
        for gs in range(2):
            wt("t", "S", m_vE[gs] if gs == 0 else m_ktm[0])
            if gs == 0:
                wt("t", "S", m_vE[1])
            for i in range(8):
                tb = gs * 8 + i
                def tr(e, tb=tb, i=i):
                    return e.transpose(psw_bf[:, i * 128:i * 128 + 64],
                                       h2fm[0:64, tb * 128:(tb + 1) * 128],
                                       idb_sb[0:64, 0:64])
                if i == 7:
                    m_kT[gs] = op("t", tr, inc="T")
                else:
                    op("t", tr)
            wt("s", "T", m_kT[gs])
            if gs == 0:
                wt("s", "V", m_rope[NTB - 1])
            m_ktm[gs] = op("s", lambda e, gs=gs: e.activation(
                out=qtm[:, gs * 512:gs * 512 + 512].rearrange("p (t c) -> p t c", t=8, c=64),
                in_=psw_bf[:, 0:1024].rearrange("p (i c) -> p i c", i=8, c=128)[:, :, 0:64],
                func=AF.Copy, bias=0.0), inc="S")

        m_krope = [None] * 2
        m_kback = [None] * NTB
        m_kvdE = {}
        m_kbackL = None
        for gs in range(2):
            wt("v", "S", m_ktm[gs])
            csc = cs_y[:, gs * 512:(gs + 1) * 512]
            css = cs_y[:, 1024 + gs * 512:1024 + (gs + 1) * 512]
            ktg = qtm[:, gs * 512:(gs + 1) * 512]
            m_sw = op("v", lambda e, ktg=ktg, css=css, gs=gs: e.tensor_tensor(
                out=tmbf[:, 1024 + gs * 512:1024 + gs * 512 + 512].rearrange(
                    "p (t a j) -> p t a j", t=8, a=2, j=32),
                in0=ktg.rearrange("p (t a j) -> p t a j", t=8, a=2, j=32)[:, :, ::-1, :],
                in1=css.rearrange("p (t a j) -> p t a j", t=8, a=2, j=32), op=A.mult), inc="V")
            vsync(m_sw)
            m_kc = op("v", lambda e, ktg=ktg, csc=csc: e.tensor_tensor(
                out=ktg, in0=ktg, in1=csc, op=A.mult), inc="V")
            vsync(m_kc)
            m_krope[gs] = op("v", lambda e, ktg=ktg, gs=gs: e.tensor_tensor(
                out=ktg, in0=ktg,
                in1=tmbf[:, 1024 + gs * 512:1024 + gs * 512 + 512], op=A.add), inc="V")
        for tb in range(NTB):
            if tb % 4 == 0:
                wt("t", "V", m_krope[(tb + 3) // 8])
                if tb == 0:
                    wt("t", "S", m_ktm[1])
                if tb >= 4:
                    wt("t", "S", m_kvdE.get(tb - 4))
            m_kback[tb] = op("t", lambda e, tb=tb: e.transpose(
                psw_bf[0:64, (tb % 4) * 128:(tb % 4) * 128 + 128],
                qtm[:, tb * 64:tb * 64 + 64], idb_sb[:, :]), inc="T")
            if tb % 4 == 3:
                wt("s", "T", m_kback[tb])
                mm_ = op("s", lambda e, tb=tb: e.activation(
                    out=kvd[0:64, (tb - 3) * 128:(tb + 1) * 128],
                    in_=psw_bf[0:64, 0:512], func=AF.Copy, bias=0.0), inc="S")
                for q_ in range(4):
                    m_kvdE[tb - q_] = mm_
        m_kbackL = m_kback[NTB - 1]
        wt("s", "S", m_kvdE[15])
        m_kdup = op("s", lambda e: e.activation(
            out=kvd[64:128, :], in_=kvd[0:64, :], func=AF.Copy, bias=0.0), inc="S")

        if stage <= 2:
            debug_finish()


        m_woTh = {}
        m_woT = [None] * NDC
        m_woE = [None] * NDC

        def emit_woT(ics):
            if ics[0] == 0:
                streams["g"].append(lambda e, v=m_qp[NTB - 1]: e.wait_ge(sems["T"][0], v))
                streams["g"].append(lambda e, v=m_win: e.wait_ge(sems["V"][0], v))
            for ic in ics:
                for cg in range(2):
                    ch = ic * 2 + cg
                    sl = ch % 2
                    prev2 = m_woTh.get((ic, cg - 2)) or m_woTh.get((ic - 1, cg))
                    if prev2 is not None:
                        streams["g"].append(lambda e, v=prev2: e.wait_ge(sems["T"][0], v))
                    m_load = op("g", lambda e, ic=ic, cg=cg, sl=sl: e.dma_start(
                        out=wstage_bf[:, sl * 512:sl * 512 + 512],
                        in_=wo[ic * 128:(ic + 1) * 128, cg * 512:(cg + 1) * 512]),
                        inc="dQ", inck=16)
                    wt("t", "dQ", m_load + (16 if ch % 2 == 0 else 0))
                    if ic == 0 and cg == 0:
                        wt("t", "S", m_kdup)
                    if ic >= 1 and cg == 0:
                        wt("t", "V", m_woE[ic - 1])
                    for j in range(4):
                        def tr(e, j=j, cg=cg, sl=sl):
                            return e.transpose(
                                psw_bf[:, cg * 512 + j * 128:cg * 512 + j * 128 + 128],
                                wstage_bf[:, sl * 512 + j * 128:sl * 512 + (j + 1) * 128], idb_sb[:, :])
                        if j == 3:
                            m_woTh[(ic, cg)] = op("t", tr, inc="T")
                        else:
                            op("t", tr)
                m_woT[ic] = m_woTh[(ic, 1)]
                wt("v", "T", m_woT[ic])
                if ic == 0:
                    wt("v", "S", m_qfm[NTB - 1])
                m_woE[ic] = op("v", lambda e, ic=ic: e.tensor_copy(
                    out=wT[:, :].rearrange("p (dc i) -> p dc i", dc=NDC, i=1024)[:, :, ic * 128:ic * 128 + 128],
                    in_=psw_bf[:, 0:1024].rearrange("p (s c) -> p s c", s=8, c=128)), inc="V")

        emit_woT(list(range(NDC)))
        strickle = []
        dqq = []

        m_ep = [[None] * NPAIR for _ in range(NQC)]
        m_bc = [[None] * NPAIR for _ in range(NQC)]
        m_nm = [[None] * NPAIR for _ in range(NQC)]
        m_recip = [None] * NQC
        m_wo = {}
        m_yev = {}
        m_nrm = {}
        m_add2 = {}
        m_add4 = {}
        m_n3 = {}
        m_h2T = {}
        m_h2E = {}
        m_fe = {}
        m_gb = [None] * NQC
        m_fpl = [None] * NQC
        m_accL = [None]
        exp_hist = []
        pv_km = {}
        exp_m = {}

        def emit_scores(qc, pair, kb):
            sub = kb % 2
            di = kb - 4 * qc
            col0 = 128 * di if di >= 0 else 0
            if qc == 0 and pair == 0 and kb < 2:
                wt("t", "S", m_qfm[NTB - 1])
                wt("t", "V", m_qtmE[NTB - 1])
                wt("t", "S", m_kdup)
                wt("t", "V", m_ones)
            if qc >= 2 and pair == 0 and kb == 0:
                wt("t", "S", m_qfm[4 * qc + 3])
            if len(exp_hist) >= 2:
                wt("t", "S", exp_hist[-2])
            mkey = None
            for hb in (0, 64):
                base = (0 if hb == 0 else 1024) + sub * 512

                def mm(e, qc=qc, pair=pair, kb=kb, hb=hb, base=base, col0=col0, di=di):
                    return e.matmul(
                        ps_st[:, base + col0:base + 512],
                        kvd[hb:hb + 64, kb * 128:(kb + 1) * 128],
                        qfm[hb:hb + 64, pair * 2048 + qc * 512 + col0:pair * 2048 + (qc + 1) * 512],
                        start=True, stop=(di < 0), tile_position=(hb, 0))
                if di < 0 and hb == 64:
                    mkey = op("t", mm, inc="T")
                else:
                    op("t", mm)
                if di >= 0:
                    op("t", lambda e, hb=hb, base=base, col0=col0: e.matmul(
                        ps_st[:, base + col0:base + col0 + 128],
                        tria_sb[hb:hb + 64, 0:128], negi_sb[hb:hb + 64, 0:128],
                        start=False, stop=False, tile_position=(hb, 0)))

                    def tr2(e, hb=hb, base=base, col0=col0):
                        return e.matmul(
                            ps_st[:, base + col0:base + col0 + 128],
                            tria_sb[hb:hb + 64, 128:256], negi_sb[hb:hb + 64, 128:256],
                            start=False, stop=True, tile_position=(hb, 0))
                    if hb == 64:
                        mkey = op("t", tr2, inc="T")
                    else:
                        op("t", tr2)
            return mkey

        def emit_exp(qc, pair, kb, m_scores):
            sub = kb % 2
            di = kb - 4 * qc
            col0 = 128 * di if di >= 0 else 0
            HS = 2048 if qc == 0 else 8192
            PT = h2fm if qc == 0 else bufA
            wt("s", "T", m_scores)
            if qc == 0 and pair == 0 and kb == 0:
                wt("s", "T", m_kT[1])
            if pair > 0:
                wt("s", "T", pv_km[(qc, pair - 1, kb)])
            elif qc > 0:
                prev_last = 4 * (qc - 1) + 3
                wt("s", "T", pv_km[(qc - 1, 7, min(kb, prev_last))])
            if qc == 1 and pair == 0 and kb == 0:
                wt("s", "T", m_qp[NTB - 1])
            in_ap = ps_st[:, :].rearrange("p (h c) -> p h c", h=2, c=1024)[:, :, sub * 512 + col0:sub * 512 + 512]
            out_ap = PT[:, :].rearrange("p (h k) -> p h k", h=2, k=HS)[:, :, kb * 512 + col0:kb * 512 + 512]
            m = op("s", lambda e, in_ap=in_ap, out_ap=out_ap: e.activation(
                out=out_ap, in_=in_ap, func=AF.Exp, bias=expb[:, 0:1], scale=0.125), inc="S")
            exp_hist.append(m)
            exp_m[(qc, pair, kb)] = m

        def emit_pv(qc, pair, kb):
            nkb = 4 * qc + 4
            di = kb - 4 * qc
            col0 = 128 * di if di >= 0 else 0
            HS = 2048 if qc == 0 else 8192
            PT = h2fm if qc == 0 else bufA
            wt("t", "S", exp_m[(qc, pair, kb)])
            mkey = None
            for h in range(2):
                def mm(e, pair=pair, kb=kb, h=h, col0=col0, HS=HS, PT=PT, nkb=nkb):
                    return e.matmul(
                        ps_pv[0:65, h * 512 + col0:h * 512 + 512],
                        v1[:, kb * 65:kb * 65 + 65],
                        PT[:, h * HS + kb * 512 + col0:h * HS + kb * 512 + 512],
                        start=(kb == 0), stop=(kb == nkb - 1))
                if h == 1:
                    mkey = op("t", mm, inc="T")
                else:
                    op("t", mm)
            pv_km[(qc, pair, kb)] = mkey

        def emit_epilogue(qc, pair):
            nkb = 4 * qc + 4
            wt("v", "T", pv_km[(qc, pair, nkb - 1)])
            if qc > 0:
                wt("v", "T", m_wo.get((qc - 1, 4 * (qc - 1) + 3)))
                wt("v", "S", m_recipH[(qc - 1, 0 if pair < 4 else 1)])
            op("v", lambda e, pair=pair: e.tensor_copy(
                out=attnq[0:64, pair * 512:(pair + 1) * 512], in_=ps_pv[0:64, 0:512]))
            op("v", lambda e, pair=pair: e.tensor_copy(
                out=attnq[64:128, pair * 512:(pair + 1) * 512], in_=ps_pv[0:64, 512:1024]))
            for hh in range(2):
                h = 2 * pair + hh
                b, c = h % 4, h // 4

                def cp(e, hh=hh, b=b, c=c):
                    return e.tensor_copy(out=sums[32 * b:32 * b + 1, 512 * c:512 * c + 512],
                                         in_=ps_pv[64:65, hh * 512:hh * 512 + 512])
                if hh == 1:
                    m_ep[qc][pair] = op("v", cp, inc="V")
                else:
                    op("v", cp)

        m_recipH = {}

        def emit_recip_half(qc, half):
            wt("s", "V", m_ep[qc][3 if half == 0 else 7])
            c0 = half * 1024
            m_ln = op("s", lambda e, c0=c0: e.activation(
                out=sums[:, c0:c0 + 1024], in_=sums[:, c0:c0 + 1024], func=AF.Ln,
                bias=zerob[:, 0:1]), inc="S")
            ssync(m_ln)
            m_recipH[(qc, half)] = op("s", lambda e, c0=c0: e.activation(
                out=sums[:, c0:c0 + 1024], in_=sums[:, c0:c0 + 1024], func=AF.Exp,
                bias=zerob[:, 0:1], scale=-1.0), inc="S")

        def emit_recip(qc):
            if (qc, 0) not in m_recipH:
                emit_recip_half(qc, 0)
            emit_recip_half(qc, 1)
            m_recip[qc] = m_recipH[(qc, 1)]

        def emit_bcast_norm(qc):
            vsync(m_ep[qc][7])
            for pair in range(NPAIR):
                b, c = (2 * pair) % 4, (2 * pair) // 4
                wt("t", "S", m_recipH[(qc, 0 if pair < 4 else 1)])
                if pair >= 2:
                    wt("t", "V", m_nm[qc][pair - 2])
                if pair < 2:
                    wt("t", "V", m_woE[NDC - 1])
                m_bc[qc][pair] = op("t", lambda e, b=b, c=c, pair=pair: e.matmul(
                    ps_w[:, (pair % 2) * 512:(pair % 2) * 512 + 512],
                    fsel_sb[32 * b:32 * b + 64, :], sums[32 * b:32 * b + 64, 512 * c:512 * c + 512],
                    start=True, stop=True), inc="T")
                wt("v", "T", m_bc[qc][pair])
                m_nm[qc][pair] = op("v", lambda e, pair=pair: e.tensor_tensor(
                    out=attnq[:, pair * 512:(pair + 1) * 512],
                    in0=attnq[:, pair * 512:(pair + 1) * 512],
                    in1=ps_w[:, (pair % 2) * 512:(pair % 2) * 512 + 512], op=A.mult), inc="V")

        def emit_wo(qc, k):
            tb = 4 * qc + k
            wt("t", "V", m_nm[qc][7])
            if k >= 1:
                wt("t", "V", m_yev[(qc, tb - 1)])
            for dc in range(NDC):
                for ih in range(2):
                    def mm(e, qc=qc, k=k, dc=dc, ih=ih):
                        return e.matmul(ps_w[:, ih * 512:(ih + 1) * 512],
                                        attnq[:, dc * 512 + k * 128:dc * 512 + k * 128 + 128],
                                        wT[:, dc * 1024 + ih * 512:dc * 1024 + (ih + 1) * 512],
                                        start=(dc == 0), stop=(dc == NDC - 1))
                    if dc == NDC - 1 and ih == 1:
                        m_wo[(qc, tb)] = op("t", mm, inc="T")
                    else:
                        op("t", mm)

        def emit_yev(qc, k):
            tb = 4 * qc + k
            sl = k % 2
            wt("v", "T", m_wo[(qc, tb)])
            if k >= 2:
                if qc == NQC - 1:
                    vsync(m_nrm[(qc, tb - 2)])
                else:
                    wt("v", "G", m_nrm[(qc, tb - 2)])
            m_yev[(qc, tb)] = op("v", lambda e, sl=sl: e.tensor_copy(
                out=cs_y[:, sl * 1024:sl * 1024 + 1024], in_=ps_w[:, :]), inc="V")
            vsync(m_yev[(qc, tb)])
            op("v", lambda e, k=k, sl=sl: e.bn_stats(
                out=stats[:, k * 12:k * 12 + 6], in_=cs_y[:, sl * 1024:sl * 1024 + 512]))
            return op("v", lambda e, k=k, sl=sl: e.bn_stats(
                out=stats[:, k * 12 + 6:k * 12 + 12], in_=cs_y[:, sl * 1024 + 512:sl * 1024 + 1024]),
                inc="V")

        def ln_core(ks, statbase, mvbase, nmx, ivx, vepbase, mstats):
            vsync(mstats)
            m_ag = None
            for k in ks:
                m_ag = op("v", lambda e, k=k, statbase=statbase, mvbase=mvbase: e.bn_aggr(
                    out=mv[:, mvbase + k * 2:mvbase + k * 2 + 2],
                    in_=stats[:, statbase + k * 12:statbase + k * 12 + 12]), inc="V")
            vsync(m_ag)
            k0, w = ks[0], len(ks)
            op("v", lambda e, k0=k0, w=w, mvbase=mvbase, nmx=nmx: e.tensor_scalar(
                out=nmx[:, k0:k0 + w], in0=mv[:, mvbase + 2 * k0:mvbase + 2 * k0 + 2 * w:2],
                scalar1=-1.0, scalar2=None, op0=A.mult))
            mfv = op("v", lambda e, k0=k0, w=w, mvbase=mvbase, vepbase=vepbase: e.tensor_scalar(
                out=vep[:, vepbase + k0:vepbase + k0 + w],
                in0=mv[:, mvbase + 2 * k0 + 1:mvbase + 2 * k0 + 2 * w:2],
                scalar1=EPS, scalar2=None, op0=A.add), inc="V")
            wt("g", "V", mfv)
            mg = op("g", lambda e, k0=k0, w=w, ivx=ivx, vepbase=vepbase: e.tensor_tensor(
                out=ivx[:, k0:k0 + w], in0=vep[:, vepbase + k0:vepbase + k0 + w],
                in1=neghalf[:, k0:k0 + w], op=A.pow), inc="G")
            return mfv, mg

        def emit_ln2_half(qc, ks, mstats):
            fast = (qc == NQC - 1)
            eng, esem, syn = ("v", "V", vsync) if fast else ("g", "G", gsync)
            mfv, mg = ln_core(ks, 0, 0, nm2, iv2, 0, mstats)
            for k in ks:
                tb = 4 * qc + k
                sl = k % 2
                wt(eng, "V", mfv)
                wt(eng, "V", m_yev[(qc, tb)])
                wt(eng, "G", mg)
                if k != ks[0]:
                    syn(m_add2[(qc, tb - 1)])
                if fast and qc >= 1:
                    wt("v", "G", m_add4.get((qc - 1, 4 * (qc - 1) + 3)))
                m_nrm[(qc, tb)] = op(eng, lambda e, k=k, sl=sl: e.tensor_scalar(
                    out=lnbuf[:, :], in0=cs_y[:, sl * 1024:sl * 1024 + 1024],
                    scalar1=nm2[:, k:k + 1], scalar2=iv2[:, k:k + 1],
                    op0=A.add, op1=A.mult), inc=esem)
                syn(m_nrm[(qc, tb)])
                m_add2[(qc, tb)] = op(eng, lambda e, tb=tb: e.tensor_tensor(
                    out=x_sb[:, tb * 1024:(tb + 1) * 1024],
                    in0=x_sb[:, tb * 1024:(tb + 1) * 1024],
                    in1=lnbuf[:, :], op=A.add), inc=esem)

        def emit_ln3_half(qc, ks):
            fast = (qc == NQC - 1)
            eng, esem, syn = ("v", "V", vsync) if fast else ("g", "G", gsync)
            m_st = None
            for k in ks:
                tb = 4 * qc + k
                if fast:
                    vsync(m_add2[(qc, tb)])
                else:
                    wt("v", "G", m_add2[(qc, tb)])
                op("v", lambda e, k=k, tb=tb: e.bn_stats(
                    out=stats[:, 48 + k * 12:48 + k * 12 + 6], in_=x_sb[:, tb * 1024:tb * 1024 + 512]))
                m_st = op("v", lambda e, k=k, tb=tb: e.bn_stats(
                    out=stats[:, 48 + k * 12 + 6:48 + k * 12 + 12],
                    in_=x_sb[:, tb * 1024 + 512:tb * 1024 + 1024]), inc="V")
            mfv, mg = ln_core(ks, 48, 8, nm3, iv3, 4, m_st)
            for k in ks:
                tb = 4 * qc + k
                sl = k % 2
                wt(eng, "G", mg)
                syn(m_add2[(qc, tb)])
                if (qc, tb - 2) in m_h2T:
                    wt(eng, "T", m_h2T[(qc, tb - 2)])
                wt(eng, "V", mfv)
                m_n3[(qc, tb)] = op(eng, lambda e, k=k, tb=tb, sl=sl: e.tensor_scalar(
                    out=tmbf[:, sl * 1024:sl * 1024 + 1024],
                    in0=x_sb[:, tb * 1024:(tb + 1) * 1024],
                    scalar1=nm3[:, k:k + 1], scalar2=iv3[:, k:k + 1],
                    op0=A.add, op1=A.mult), inc=esem)

        def emit_h2T(qc, k):
            tb = 4 * qc + k
            sl = k % 2
            wt("t", "V" if qc == NQC - 1 else "G", m_n3[(qc, tb)])
            if k == 0:
                wt("t", "V", m_yev[(qc, 4 * qc + 3)])
            if (qc, tb - 1) in m_h2E:
                wt("t", "V", m_h2E[(qc, tb - 1)])
            for dc in range(NDC):
                def tr(e, dc=dc, sl=sl):
                    return e.transpose(psw_bf[:, sl * 1024 + dc * 128:sl * 1024 + dc * 128 + 128],
                                       tmbf[:, sl * 1024 + dc * 128:sl * 1024 + dc * 128 + 128],
                                       idb_sb[:, :])
                if dc == NDC - 1:
                    m_h2T[(qc, tb)] = op("t", tr, inc="T")
                else:
                    op("t", tr)
            wt("v", "T", m_h2T[(qc, tb)])
            m_h2E[(qc, tb)] = op("v", lambda e, k=k, sl=sl: e.tensor_copy(
                out=h2fm[:, :].rearrange("p (dc t) -> p dc t", dc=NDC, t=512)[:, :, k * 128:k * 128 + 128],
                in_=psw_bf[:, sl * 1024:sl * 1024 + 1024].rearrange("p (dc c) -> p dc c", dc=NDC, c=128)), inc="V")

        ffn_st = {}

        def emit_ffn_front(qc, part):
            if part == 0:
                wt("t", "V", m_h2E[(qc, 4 * qc + 3)])
                wt("t", "V", m_win)
                wt("t", "V", m_ffnw)
                if qc > 0:
                    wt("t", "V", m_gb[qc - 1])
                m_sp = None
                for dc in range(NDC):
                    for half in range(2):
                        def mm(e, dc=dc, half=half):
                            return e.matmul(
                                ps_w[half * 64:half * 64 + 64, 0:256],
                                win_sb[:, dc * 64:(dc + 1) * 64],
                                h2fm[:, dc * 512 + half * 256:dc * 512 + (half + 1) * 256],
                                start=(dc == 0), stop=(dc == NDC - 1), tile_position=(0, half * 64),
                                skip_group_check=True)
                        if dc == NDC - 1 and half == 1:
                            m_sp = op("t", mm, inc="T")
                        else:
                            op("t", mm)
                wt("v", "T", m_sp)
                if qc > 0:
                    wt("v", "T", m_fpl[qc - 1])
                m_sev = op("v", lambda e: e.tensor_copy(out=s_sbv, in_=ps_w[:, 0:256]), inc="V")
                ffn_st[qc] = {"sev": m_sev, "relu": {}, "acc": {}}
            fst = ffn_st[qc]
            m_sev, m_relu, m_acc = fst["sev"], fst["relu"], fst["acc"]

            fastq = (qc == NQC - 1)

            def step(i, qc=qc, fastq=fastq):
                if fastq:
                    wt("s", "V", m_sev if i < 2 else m_acc[i - 2])
                    m_relu[i] = op("s", lambda e, i=i: e.activation(
                        out=hbuf[i % 2], in_=s_sbv, func=AF.Relu,
                        bias=b1pd[:, i:i + 1], scale=w1d[:, i:i + 1]), inc="S")
                    wt("v", "S", m_relu[i])
                    if i >= 2:
                        vsync(m_acc[i - 2])
                    elif m_accL[0] is not None:
                        vsync(m_accL[0])
                elif i < 2:
                    vsync(m_sev)
                    if m_accL[0] is not None:
                        vsync(m_accL[0])
                else:
                    vsync(m_acc[i - 2])
                if not fastq:
                    m_pre = op("v", lambda e, i=i: e.tensor_scalar(
                        out=hbuf[i % 2], in0=s_sbv, scalar1=w1d[:, i:i + 1],
                        scalar2=b1pd[:, i:i + 1], op0=A.mult, op1=A.add), inc="V")
                    vsync(m_pre)
                    m_relu[i] = op("v", lambda e, i=i: e.tensor_scalar(
                        out=hbuf[i % 2], in0=hbuf[i % 2], scalar1=0.0, scalar2=None,
                        op0=A.max), inc="V")
                    vsync(m_relu[i])
                if i < 2:
                    m_acc[i] = op("v", lambda e, i=i: e.tensor_scalar(
                        out=gsl[i], in0=hbuf[i], scalar1=w2d[:, i:i + 1],
                        scalar2=(b2d[:, 0:1] if i == 0 else zerob[:, 0:1]),
                        op0=A.mult, op1=A.add), inc="V")
                else:
                    m_acc[i] = op("v", lambda e, i=i: e.scalar_tensor_tensor(
                        out=gsl[i % 2], in0=hbuf[i % 2], scalar=w2d[:, i:i + 1], in1=gsl[i % 2],
                        op0=A.mult, op1=A.add), inc="V")
                if i == 15:
                    vsync(m_acc[15])
                    m_gb[qc] = op("v", lambda e: e.tensor_tensor(
                        out=gb, in0=gsl[0], in1=gsl[1], op=A.add), inc="V")
                    m_accL[0] = m_acc[15]
            for i in (range(0, 8) if part == 0 else range(8, 16)):
                strickle.append(lambda i=i: step(i))

        def emit_fproj(qc, tt):
            hb = 64 * (tt // 2)
            gcol = (tt % 2) * 128
            wt("t", "V", m_gb[qc] if tt == 0 else m_fe[(qc, tt - 1)])
            m_fp = None
            for ih in range(2):
                def mm(e, hb=hb, gcol=gcol, ih=ih):
                    return e.matmul(ps_w[:, ih * 512:(ih + 1) * 512],
                                    gb[hb:hb + 64, gcol:gcol + 128],
                                    woutd[hb:hb + 64, ih * 512:(ih + 1) * 512],
                                    start=True, stop=True, tile_position=(hb, 0))
                if ih == 1:
                    m_fp = op("t", mm, inc="T")
                else:
                    op("t", mm)
            m_fpl[qc] = m_fp
            wt("v", "T", m_fp)
            if tt >= 1:
                if qc == NQC - 1:
                    vsync(m_nrm4.get((qc, tt - 1)))
                else:
                    wt("v", "G", m_nrm4.get((qc, tt - 1)))
            for ih in range(2):
                def ev(e, ih=ih):
                    return e.tensor_tensor(
                        out=f_sb[:, ih * 512:(ih + 1) * 512],
                        in0=ps_w[:, ih * 512:(ih + 1) * 512],
                        in1=boutb[:, ih * 512:(ih + 1) * 512], op=A.add)
                if ih == 1:
                    m_fe[(qc, tt)] = op("v", ev, inc="V")
                else:
                    op("v", ev)

        m_nrm4 = {}

        def emit_ln4_single(qc, k):
            fast = (qc == NQC - 1)
            eng, esem, syn = ("v", "V", vsync) if fast else ("g", "G", gsync)
            vsync(m_fe[(qc, k)])
            op("v", lambda e, k=k: e.bn_stats(
                out=stats[:, 96 + k * 12:96 + k * 12 + 6], in_=f_sb[:, 0:512]))
            m_st = op("v", lambda e, k=k: e.bn_stats(
                out=stats[:, 96 + k * 12 + 6:96 + k * 12 + 12], in_=f_sb[:, 512:1024]), inc="V")
            mfv, mg = ln_core((k,), 96, 16, nm4, iv4, 8, m_st)
            tb = 4 * qc + k
            wt(eng, "V", mfv)
            wt(eng, "V", m_fe[(qc, k)])
            wt(eng, "G", mg)
            syn(m_add2[(qc, tb)])
            if k >= 1:
                syn(m_add4[(qc, tb - 1)])
            m_nrm4[(qc, k)] = op(eng, lambda e, k=k: e.tensor_scalar(
                out=lnbuf[:, :], in0=f_sb[:, :],
                scalar1=nm4[:, k:k + 1], scalar2=iv4[:, k:k + 1],
                op0=A.add, op1=A.mult), inc=esem)
            syn(m_nrm4[(qc, k)])
            m_add4[(qc, tb)] = op(eng, lambda e, tb=tb: e.tensor_tensor(
                out=x_sb[:, tb * 1024:(tb + 1) * 1024],
                in0=x_sb[:, tb * 1024:(tb + 1) * 1024],
                in1=lnbuf[:, :], op=A.add), inc=esem)

        def emit_yout(qc):
            asem = "V" if qc == NQC - 1 else "G"
            for hf in range(2):
                streams["y"].append(lambda e, v=m_add4[(qc, 4 * qc + 2 * hf + 1)], asem=asem:
                                     e.wait_ge(sems[asem][0], v))
                streams["y"].append(lambda e, v=sems["dO"][1]: (e.wait_ge(sems["dO"][0], v) if v > 0 else None))
                op("y", lambda e, qc=qc, hf=hf: e.dma_start(
                    out=yout[qc * 512 + hf * 256:qc * 512 + (hf + 1) * 256, :].rearrange("(t p) d -> p t d", p=128),
                    in_=x_sb[:, qc * 4096 + hf * 2048:qc * 4096 + (hf + 1) * 2048].rearrange(
                        "p (t d) -> p t d", t=2, d=1024)), inc="dO", inck=16)

        def D_units(qc):
            def u1():
                emit_wo(qc, 0)
                emit_yev(qc, 0)
                emit_wo(qc, 1)
                ms1 = emit_yev(qc, 1)
                emit_ln2_half(qc, (0, 1), ms1)

            def u2():
                emit_wo(qc, 2)
                emit_yev(qc, 2)
                emit_wo(qc, 3)
                ms3 = emit_yev(qc, 3)
                emit_ln2_half(qc, (2, 3), ms3)

            return [
                lambda: emit_bcast_norm(qc),
                u1,
                u2,
                lambda: (emit_ln3_half(qc, (0, 1)), emit_h2T(qc, 0), emit_h2T(qc, 1)),
                lambda: (emit_ln3_half(qc, (2, 3)), emit_h2T(qc, 2), emit_h2T(qc, 3)),
                lambda: emit_ffn_front(qc, 0),
                lambda: emit_ffn_front(qc, 1),
                lambda: (emit_fproj(qc, 0), emit_ln4_single(qc, 0)),
                lambda: (emit_fproj(qc, 1), emit_ln4_single(qc, 1)),
                lambda: (emit_fproj(qc, 2), emit_ln4_single(qc, 2)),
                lambda: (emit_fproj(qc, 3), emit_ln4_single(qc, 3), emit_yout(qc)),
            ]

        if stage <= 3:
            debug_finish()
        nqc_run = NQC if stage >= 5 else (1 if stage == 4 else 0)
        do_D = stage >= 6
        dq = []
        trail = []
        prev_qc = None
        for qc in range(nqc_run):
            nkb = 4 * qc + 4
            for pair in range(NPAIR):
                for kb in range(nkb):
                    if pair == 0 and trail:
                        trail.pop(0)()
                    if pair >= 1:
                        if kb == 0:
                            wt("t", "V", m_ep[qc][pair - 2] if pair >= 2
                               else (m_ep[qc - 1][7] if qc >= 1 else None))
                        emit_pv(qc, pair - 1, kb)
                    m_sc = emit_scores(qc, pair, kb)
                    emit_exp(qc, pair, kb, m_sc)
                    if strickle:
                        strickle.pop(0)()
                    if (kb == nkb // 2 - 1 and pair == 0 and do_D and prev_qc is not None
                            and (prev_qc, 0) not in m_recipH):
                        emit_recip_half(prev_qc, 0)
                    if kb == nkb // 2 - 1:
                        if dqq:
                            dqq.pop(0)()
                        elif do_D and dq and pair >= 1 and not strickle:
                            dq.pop(0)()
                if pair == 0:
                    while trail:
                        trail.pop(0)()
                    if do_D and prev_qc is not None:
                        emit_recip(prev_qc)

                if dqq:
                    dqq.pop(0)()
                elif do_D and dq and (not strickle or pair <= 1):
                    dq.pop(0)()
                if pair >= 1:
                    emit_epilogue(qc, pair - 1)

            def mk_trail(qc=qc, nkb=nkb):
                t = []
                for kb in range(nkb):
                    def f(kb=kb, qc=qc):
                        if kb == 0:
                            wt("t", "V", m_ep[qc][6])
                        emit_pv(qc, 7, kb)
                    t.append(f)
                t.append(lambda qc=qc: emit_epilogue(qc, 7))
                return t
            trail = mk_trail()
            if do_D:
                nu = D_units(qc)
                dq = nu[0:3] + dq + nu[3:]
                prev_qc = qc
        while trail:
            trail.pop(0)()
        if do_D:
            emit_recip(prev_qc)
            for u in dq:
                u()
                while strickle:
                    strickle.pop(0)()
        else:
            debug_finish()

        streams["y"].append(lambda e: e.wait_ge(sems["dO"][0], sems["dO"][1]))

        block = st.enter_context(nc.Block())

        @block.sync
        def _(eng):
            for f in streams["y"]:
                f(eng)

        @block.tensor
        def _(eng):
            for f in streams["t"]:
                f(eng)

        @block.vector
        def _(eng):
            for f in streams["v"]:
                f(eng)

        @block.scalar
        def _(eng):
            for f in streams["s"]:
                f(eng)

        @block.gpsimd
        def _(eng):
            for f in streams["g"]:
                f(eng)

    return nc


_CACHE = {}


def kernel(L0_attn_x, L0_attn_wq, L0_attn_wk, L0_attn_wv, L0_attn_wo,
           L0_ffn_Win, L0_ffn_bin_, L0_ffn_W1, L0_ffn_b1, L0_ffn_W2,
           L0_ffn_b2, L0_ffn_Wout, L0_ffn_bout):
    x = np.ascontiguousarray(np.asarray(L0_attn_x, dtype=np.float32))
    consts = _host_consts()
    if "nc" not in _CACHE:
        _CACHE["nc"] = build_nc()
    nc = _CACHE["nc"]
    shared = {
        "wq": np.ascontiguousarray(np.asarray(L0_attn_wq, dtype=np.float32)),
        "wk": np.ascontiguousarray(np.asarray(L0_attn_wk, dtype=np.float32)),
        "wv": np.ascontiguousarray(np.asarray(L0_attn_wv, dtype=np.float32)),
        "wo": np.ascontiguousarray(np.asarray(L0_attn_wo, dtype=np.float32)),
        "win": np.ascontiguousarray(np.asarray(L0_ffn_Win, dtype=np.float32)),
        "binp": np.ascontiguousarray(np.asarray(L0_ffn_bin_, dtype=np.float32).reshape(64, 1)),
        "w1": np.ascontiguousarray(np.asarray(L0_ffn_W1, dtype=np.float32)),
        "b1": np.ascontiguousarray(np.asarray(L0_ffn_b1, dtype=np.float32)),
        "w2": np.ascontiguousarray(np.asarray(L0_ffn_W2, dtype=np.float32)),
        "b2": np.ascontiguousarray(np.asarray(L0_ffn_b2, dtype=np.float32).reshape(64, 1)),
        "wout": np.ascontiguousarray(np.asarray(L0_ffn_Wout, dtype=np.float32)),
        "bout": np.ascontiguousarray(np.asarray(L0_ffn_bout, dtype=np.float32).reshape(1, D)),
    }
    shared.update(consts)
    in_maps = []
    for i in range(NCORE):
        m = dict(shared)
        m["x"] = np.ascontiguousarray(x[i])
        in_maps.append(m)
    res = run_bass_kernel_spmd(nc, in_maps, core_ids=list(range(NCORE)))
    out = np.stack([np.asarray(r["y"]) for r in res.results], axis=0)
    return out.astype(np.float32)


# version 30
# speedup vs baseline: 1.0208x; 1.0044x over previous
import os
import numpy as np
import ml_dtypes
import concourse.bass as bass
import concourse.mybir as mybir
from concourse.bass_utils import run_bass_kernel_spmd
from concourse.alu_op_type import AluOpType as A
from contextlib import ExitStack

F32 = mybir.dt.float32
BF16 = mybir.dt.bfloat16
AF = mybir.ActivationFunctionType

NCORE = 8
S_TOK = 2048
D = 1024
NTB = 16
NDC = 8
NQC = 4
NPAIR = 8
EPS = 1e-5
ESHIFT = -8.0


def _host_consts():
    ident = np.eye(128, dtype=np.float32)
    identb = np.eye(128, dtype=ml_dtypes.bfloat16)
    j = np.arange(64)
    p = np.arange(128)
    tria = np.zeros((128, 256), dtype=np.float32)
    tria[0:64, 0:128] = (p[None, :] > j[:, None]).astype(np.float32)
    tria[0:64, 128:256] = (p[None, :] > j[:, None] + 64).astype(np.float32)
    tria[64:128] = tria[0:64]
    negi = np.zeros((128, 256), dtype=np.float32)
    negi[j, j] = -1e30
    negi[j, 128 + 64 + j] = -1e30
    negi[64:128] = negi[0:64]
    fsel = np.zeros((128, 128), dtype=np.float32)
    fsel[0, 0:64] = 1.0
    fsel[32, 64:128] = 1.0
    fsel[64:128] = fsel[0:64]
    t = np.arange(S_TOK, dtype=np.float64)[:, None]
    f = 1000.0 ** (-np.arange(32, dtype=np.float64) / 32.0)[None, :]
    tf = t * f
    two_pi = 2.0 * np.pi

    def hilo(theta):
        y = np.mod(theta, two_pi) - np.pi
        hi = y.astype(np.float32)
        lo = (y - hi.astype(np.float64)).astype(np.float32)
        return hi, lo
    thc = np.concatenate([tf, tf], axis=1) + np.pi / 2.0
    ths = np.concatenate([-tf, tf], axis=1)
    yc, ycd = hilo(thc)
    ys, ysd = hilo(ths)
    return {
        "c_ident": ident, "c_identb": identb,
        "c_tria": tria.astype(ml_dtypes.bfloat16), "c_negi": negi.astype(ml_dtypes.bfloat16),
        "c_fsel": fsel,
        "c_yc": yc, "c_ycd": ycd, "c_ys": ys, "c_ysd": ysd,
    }


def build_nc():
    stage = int(os.environ.get("KSTAGE", "6"))
    nc = bass.Bass("TRN2", target_bir_lowering=False)
    xin = nc.dram_tensor("x", [S_TOK, D], F32, kind="ExternalInput").ap()
    wq = nc.dram_tensor("wq", [D, D], F32, kind="ExternalInput").ap()
    wk = nc.dram_tensor("wk", [D, 64], F32, kind="ExternalInput").ap()
    wv = nc.dram_tensor("wv", [D, 64], F32, kind="ExternalInput").ap()
    wo = nc.dram_tensor("wo", [D, D], F32, kind="ExternalInput").ap()
    win = nc.dram_tensor("win", [D, 64], F32, kind="ExternalInput").ap()
    binp = nc.dram_tensor("binp", [64, 1], F32, kind="ExternalInput").ap()
    w1 = nc.dram_tensor("w1", [64, 16], F32, kind="ExternalInput").ap()
    b1 = nc.dram_tensor("b1", [64, 16], F32, kind="ExternalInput").ap()
    w2 = nc.dram_tensor("w2", [64, 16], F32, kind="ExternalInput").ap()
    b2 = nc.dram_tensor("b2", [64, 1], F32, kind="ExternalInput").ap()
    wout = nc.dram_tensor("wout", [64, D], F32, kind="ExternalInput").ap()
    bout = nc.dram_tensor("bout", [1, D], F32, kind="ExternalInput").ap()
    c_ident = nc.dram_tensor("c_ident", [128, 128], F32, kind="ExternalInput").ap()
    c_identb = nc.dram_tensor("c_identb", [128, 128], BF16, kind="ExternalInput").ap()
    c_tria = nc.dram_tensor("c_tria", [128, 256], BF16, kind="ExternalInput").ap()
    c_negi = nc.dram_tensor("c_negi", [128, 256], BF16, kind="ExternalInput").ap()
    c_fsel = nc.dram_tensor("c_fsel", [128, 128], F32, kind="ExternalInput").ap()
    c_yc = nc.dram_tensor("c_yc", [S_TOK, 64], F32, kind="ExternalInput").ap()
    c_ycd = nc.dram_tensor("c_ycd", [S_TOK, 64], F32, kind="ExternalInput").ap()
    c_ys = nc.dram_tensor("c_ys", [S_TOK, 64], F32, kind="ExternalInput").ap()
    c_ysd = nc.dram_tensor("c_ysd", [S_TOK, 64], F32, kind="ExternalInput").ap()
    yout = nc.dram_tensor("y", [S_TOK, D], F32, kind="ExternalOutput").ap()

    with ExitStack() as st:
        def sb(name, shape, dt=F32):
            return st.enter_context(nc.sbuf_tensor(name, shape, dt))

        x_sb = sb("x_sb", [128, NTB * 1024])
        bufA = sb("bufA", [128, 16384], BF16)
        qfm = sb("qfm", [128, 16384], BF16)
        wT = sb("wT", [128, 8192], BF16)
        h2fm = sb("h2fm", [128, 4096], BF16)
        attnq = sb("attnq", [128, 4096], BF16)
        kvd = sb("kvd", [128, 2048], BF16)
        v1 = sb("v1", [128, 16 * 65], BF16)
        wstage = sb("wstage", [128, 512])
        tmroll = sb("tmroll", [128, 1024])
        qtm = sb("qtm", [128, 1024], BF16)
        qrope = sb("qrope", [128, 2048], BF16)
        cs_y = sb("cs_y", [128, 2048], BF16)
        sums = sb("sums", [128, 2048])
        lnbuf = sb("lnbuf", [128, 1024], BF16)
        f_sb = sb("f_sb", [128, 1024], BF16)
        kvw = sb("kvw", [128, 1024], BF16)
        woutd = sb("woutd", [128, 1024], BF16)
        boutb = sb("boutb", [128, 1024], BF16)
        w1d = sb("w1d", [128, 16])
        b1pd = sb("b1pd", [128, 16])
        w2d = sb("w2d", [128, 16])
        b2d = sb("b2d", [128, 1])
        stats = sb("stats", [128, 192])
        bind = stats[:, 174:175]
        b1d = stats[:, 176:192]
        win_sb = kvw[:, 0:512]
        mv = sb("mv", [128, 32])
        nm1 = sb("nm1", [128, 16])
        iv1 = sb("iv1", [128, 16])
        vep = sb("vep", [128, 16])
        nm2 = sb("nm2", [128, 4]); iv2 = sb("iv2", [128, 4])
        nm3 = sb("nm3", [128, 4]); iv3 = sb("iv3", [128, 4])
        nm4 = sb("nm4", [128, 4]); iv4 = sb("iv4", [128, 4])
        neghalf = sb("neghalf", [128, 16])
        negone = sb("negone", [128, 1])
        expb = sb("expb", [128, 1])
        ones1 = sb("ones1", [1, 128])
        zerob = sb("zerob", [128, 1])
        junk = sb("junk", [128, 16])
        junkg = sb("junkg", [128, 16])
        id_sb = sb("id_sb", [128, 128])
        idb_sb = sb("idb_sb", [128, 128], BF16)
        tria_sb = sb("tria_sb", [128, 256], BF16)
        negi_sb = sb("negi_sb", [128, 256], BF16)
        fsel_sb = sb("fsel_sb", [128, 128])

        ps_st = st.enter_context(nc.psum_tensor("ps_st", [128, 2048], F32))
        ps_pv = st.enter_context(nc.psum_tensor("ps_pv", [128, 1024], F32))
        ps_w = st.enter_context(nc.psum_tensor("ps_w", [128, 1024], F32))

        s_sbv = qtm[:, 0:256]
        hbuf = [qtm[:, 256:512], qtm[:, 512:768]]
        gb = qtm[:, 768:1024]
        gsl = [qrope[:, :].bitcast(F32)[:, 0:256], qrope[:, :].bitcast(F32)[:, 256:512]]

        tmbf = tmroll[:, :].bitcast(BF16)
        qfm_f32 = qfm[:, :].bitcast(F32)
        psst_bf = ps_st[:, :].bitcast(BF16)
        psw_bf = ps_w[:, :].bitcast(BF16)
        pspv_bf = ps_pv[:, :].bitcast(BF16)
        wstage_bf = wstage[:, :].bitcast(BF16)

        sems = {}
        for s in ("dX", "dW", "dO", "dQ", "T", "V", "S", "G"):
            sems[s] = [st.enter_context(nc.semaphore(s)), 0]
        streams = {k: [] for k in ("t", "v", "s", "g", "y")}

        def op(eng, build, inc=None, inck=1):
            if inc is not None:
                sems[inc][1] += inck
            val = sems[inc][1] if inc is not None else None

            def f(e, build=build, inc=inc, inck=inck):
                ins = build(e)
                if inc is not None:
                    ins.then_inc(sems[inc][0], inck)
            streams[eng].append(f)
            return val

        def wt(eng, sem, val):
            if val is None or val <= 0:
                return
            streams[eng].append(lambda e, s=sem, v=val: e.wait_ge(sems[s][0], v))

        def vsync(val):
            wt("v", "V", val)

        def gsync(val):
            wt("g", "G", val)

        def ssync(val):
            wt("s", "S", val)

        _fr = [0, 0]

        def vfence(inc=None):
            _fr[0] += 1
            c = _fr[0] % 16
            return op("v", lambda e, c=c: e.memset(junk[:, c:c + 1], 0.0), inc=inc)

        def gfence(inc=None):
            _fr[1] += 1
            c = _fr[1] % 16
            return op("g", lambda e, c=c: e.memset(junkg[:, c:c + 1], 0.0), inc=inc)

        _done = [False]

        def debug_finish():
            if _done[0]:
                return
            _done[0] = True
            for semname in ("T", "V", "S", "G"):
                v = sems[semname][1]
                if v > 0:
                    streams["y"].append(lambda e, sn=semname, v=v: e.wait_ge(sems[sn][0], v))
            op("y", lambda e: e.dma_start(
                out=yout.rearrange("(t p) d -> p t d", p=128),
                in_=x_sb[:, :].rearrange("p (t d) -> p t d", t=NTB, d=1024)),
                inc="dO", inck=16)

        op("v", lambda e: e.memset(expb[:, :], ESHIFT))
        op("v", lambda e: e.memset(zerob[:, :], 0.0))
        op("v", lambda e: e.memset(neghalf[:, :], -0.5))
        m_negone = op("v", lambda e: e.memset(negone[:, :], -1.0), inc="V")
        op("v", lambda e: e.memset(sums[:, :], 1.0))
        m_ones = op("v", lambda e: e.memset(
            v1[:, :].rearrange("p (t c) -> p t c", t=16, c=65)[:, :, 64:65], 1.0), inc="V")

        def dma(build):
            return op("y", build, inc="dW", inck=16)

        m_xg = [None] * 4

        dma(lambda e: e.dma_start(out=id_sb[:, :], in_=c_ident[:, :]))
        m_idB = dma(lambda e: e.dma_start(out=idb_sb[:, :], in_=c_identb[:, :]))
        streams["y"].append(lambda e, v=m_idB: e.wait_ge(sems["dW"][0], v))
        dma(lambda e: e.dma_start(out=tria_sb[:, :], in_=c_tria[:, :]))
        dma(lambda e: e.dma_start(out=negi_sb[:, :], in_=c_negi[:, :]))
        dma(lambda e: e.dma_start(out=fsel_sb[:, :], in_=c_fsel[:, :]))
        dma(lambda e: e.dma_start(out=w1d[0:64, :], in_=w1[:, :]))
        dma(lambda e: e.dma_start(out=b1d[0:64, :], in_=b1[:, :]))
        dma(lambda e: e.dma_start(out=w2d[0:64, :], in_=w2[:, :]))
        dma(lambda e: e.dma_start(out=b2d[0:64, :], in_=b2[:, :]))
        m_small = dma(lambda e: e.dma_start(out=bind[0:64, :], in_=binp[:, :]))

        def tab_view(tab, half=None):
            v = tab.rearrange("(t p) j -> p t j", p=128)
            if half is None:
                return v
            return v[:, half * 8:(half + 1) * 8, :]

        m_tabL = None
        for i, tab in enumerate((c_yc, c_ycd, c_ys, c_ysd)):
            m_tabL = dma(lambda e, tab=tab, i=i: e.dma_start(
                out=qfm_f32[:, i * 1024:(i + 1) * 1024].rearrange("p (t j) -> p t j", t=16, j=64),
                in_=tab_view(tab)))
        streams["y"].append(lambda e, v=m_tabL: e.wait_ge(sems["dW"][0], v))
        wq_r = wq.rearrange("(ic r) d -> r ic d", r=128)
        m_wqL = [None]
        for h2 in range(2):
            for eo in range(2):
                src = wq_r[h2 * 64 + eo:h2 * 64 + 64:2, :, :]
                dst = qfm[h2 * 64 + eo * 32:h2 * 64 + eo * 32 + 32, 8192:16384].rearrange(
                    "p (ic d) -> p ic d", ic=8)
                m_wqL[0] = op("g", lambda e, src=src, dst=dst: e.dma_start(out=dst, in_=src),
                              inc="dQ", inck=16)
        streams["g"].append(lambda e, v=m_wqL[0]: e.wait_ge(sems["dQ"][0], v))
        for g in range(4):
            m_xg[g] = op("g", lambda e, g=g: e.dma_start(
                out=x_sb[:, g * 4096:(g + 1) * 4096].rearrange("p (t d) -> p t d", t=4, d=1024),
                in_=xin.rearrange("(t p) d -> p t d", p=128)[:, g * 4:(g + 1) * 4, :]),
                inc="dX", inck=16)
            streams["g"].append(lambda e, v=m_xg[g]: e.wait_ge(sems["dX"][0], v))
        m_x = m_xg[3]
        def emit_sin(hi_col, lo_col, dstcol, prev_gate):
            if prev_gate is not None:
                ssync(prev_gate)
                wt("v", "S", prev_gate)
            wt("v", "dW", m_tabL)
            m_add = op("v", lambda e, hi_col=hi_col, lo_col=lo_col: e.tensor_tensor(
                out=tmroll[:, :], in0=qfm_f32[:, hi_col:hi_col + 1024],
                in1=qfm_f32[:, lo_col:lo_col + 1024], op=A.add), inc="V")
            wt("s", "V", m_add)
            return op("s", lambda e, dstcol=dstcol: e.activation(
                out=cs_y[:, dstcol:dstcol + 1024], in_=tmroll[:, :],
                func=AF.Sin, bias=zerob[:, 0:1], scale=-1.0), inc="S")

        m_sin1 = emit_sin(0, 1024, 0, None)
        m_sin2 = emit_sin(2048, 3072, 1024, m_sin1)

        m_wkL = dma(lambda e: e.dma_start(
            out=wstage[:, :].rearrange("p (c h) -> p c h", c=8, h=64),
            in_=wk.rearrange("(c p) h -> p c h", p=128)))
        wt("v", "dW", m_wkL)
        wsv = wstage[:, :].rearrange("p (c e) -> p c e", c=512, e=1)
        m_wkC = None
        for dc in range(NDC):
            op("v", lambda e, dc=dc: e.tensor_copy(
                out=kvw[:, dc * 128 + 0:dc * 128 + 32],
                in_=wsv[:, dc * 64 + 0:dc * 64 + 64:2, 0]))
            m_wkC = op("v", lambda e, dc=dc: e.tensor_copy(
                out=kvw[:, dc * 128 + 32:dc * 128 + 64],
                in_=wsv[:, dc * 64 + 1:dc * 64 + 64:2, 0]), inc="V")
        streams["y"].append(lambda e, v=m_wkC: e.wait_ge(sems["V"][0], v))
        m_wvL = dma(lambda e: e.dma_start(
            out=wstage[:, :].rearrange("p (c h) -> p c h", c=8, h=64),
            in_=wv.rearrange("(c p) h -> p c h", p=128)))
        wt("v", "dW", m_wvL)
        m_kvw = None
        for dc in range(NDC):
            m_kvw = op("v", lambda e, dc=dc: e.tensor_copy(
                out=kvw[:, dc * 128 + 64:dc * 128 + 128],
                in_=wstage[:, dc * 64:dc * 64 + 64]), inc="V")
        streams["y"].append(lambda e, v=m_kvw: e.wait_ge(sems["V"][0], v))
        m_wout = None
        for half in range(2):
            m_woutL = dma(lambda e, half=half: e.dma_start(
                out=wstage[0:64, :], in_=wout[:, half * 512:(half + 1) * 512]))
            wt("v", "dW", m_woutL)
            op("v", lambda e, half=half: e.tensor_copy(
                out=woutd[0:64, half * 512:(half + 1) * 512], in_=wstage[0:64, :]))
            m_wout = op("v", lambda e, half=half: e.tensor_copy(
                out=woutd[64:128, half * 512:(half + 1) * 512], in_=wstage[0:64, :]), inc="V")
            streams["y"].append(lambda e, v=m_wout: e.wait_ge(sems["V"][0], v))
        m_ones1 = op("v", lambda e: e.memset(ones1[:, :], 1.0), inc="V")
        m_boutc = None
        m_bmm = None
        for half in range(2):
            m_boutL = dma(lambda e, half=half: e.dma_start(
                out=wstage[0:1, :], in_=bout[:, half * 512:(half + 1) * 512]))
            wt("t", "dW", m_boutL)
            wt("t", "V", m_ones1)
            if half == 1:
                wt("t", "V", m_boutc)
            m_bmm = op("t", lambda e: e.matmul(
                ps_w[:, 0:512], ones1[0:1, :], wstage[0:1, :], start=True, stop=True), inc="T")
            wt("v", "T", m_bmm)
            m_boutc = op("v", lambda e, half=half: e.tensor_copy(
                out=boutb[:, half * 512:(half + 1) * 512], in_=ps_w[:, 0:512]), inc="V")
            streams["y"].append(lambda e, v=m_boutc: e.wait_ge(sems["V"][0], v))
        op("v", lambda e: e.tensor_copy(out=w1d[64:128, :], in_=w1d[0:64, :]))
        op("v", lambda e: e.tensor_copy(out=b1d[64:128, :], in_=b1d[0:64, :]))
        op("v", lambda e: e.tensor_copy(out=w2d[64:128, :], in_=w2d[0:64, :]))
        op("v", lambda e: e.tensor_copy(out=b2d[64:128, :], in_=b2d[0:64, :]))
        m_dups = op("v", lambda e: e.tensor_copy(out=bind[64:128, :], in_=bind[0:64, :]), inc="V")
        vsync(m_dups)
        m_ffnw = op("v", lambda e: e.scalar_tensor_tensor(
            out=b1pd[:, :], in0=w1d[:, :], scalar=bind[:, 0:1], in1=b1d[:, :],
            op0=A.mult, op1=A.add), inc="V")

        m_ln1g = [None] * 4
        for g in range(4):
            wt("v", "dX", m_xg[g])
            if g == 0:
                vsync(m_ffnw)
            m_st1 = None
            for tb in range(4 * g, 4 * g + 4):
                op("v", lambda e, tb=tb: e.bn_stats(out=stats[:, tb * 12:tb * 12 + 6],
                                                    in_=x_sb[:, tb * 1024:tb * 1024 + 512]))
                m_st1 = op("v", lambda e, tb=tb: e.bn_stats(out=stats[:, tb * 12 + 6:tb * 12 + 12],
                                                            in_=x_sb[:, tb * 1024 + 512:tb * 1024 + 1024]),
                           inc="V")
            vsync(m_st1)
            m_ag1 = None
            for tb in range(4 * g, 4 * g + 4):
                m_ag1 = op("v", lambda e, tb=tb: e.bn_aggr(out=mv[:, tb * 2:tb * 2 + 2],
                                                           in_=stats[:, tb * 12:tb * 12 + 12]), inc="V")
            vsync(m_ag1)
            op("v", lambda e, g=g: e.tensor_scalar(
                out=nm1[:, 4 * g:4 * g + 4], in0=mv[:, 8 * g:8 * g + 8:2],
                scalar1=-1.0, scalar2=None, op0=A.mult))
            m_ln1v = op("v", lambda e, g=g: e.tensor_scalar(
                out=vep[:, 4 * g:4 * g + 4], in0=mv[:, 8 * g + 1:8 * g + 8:2],
                scalar1=EPS, scalar2=None, op0=A.add), inc="V")
            wt("g", "V", m_ln1v)
            m_ln1g[g] = op("g", lambda e, g=g: e.tensor_tensor(
                out=iv1[:, 4 * g:4 * g + 4], in0=vep[:, 4 * g:4 * g + 4],
                in1=neghalf[:, 4 * g:4 * g + 4], op=A.pow), inc="G")

        m_wqT = [None] * NDC
        m_wqE = [None] * NDC
        for ic in range(NDC):
            wt("t", "dQ", m_wqL[0])
            if ic == 0:
                wt("t", "dW", m_idB)
            if ic >= 1:
                wt("t", "S", m_wqE[ic - 1])
            for dc in range(NDC):
                def tr(e, ic=ic, dc=dc):
                    return e.transpose(
                        psst_bf[:, 2048 + dc * 128:2048 + dc * 128 + 128],
                        qfm[:, 8192 + ic * 1024 + dc * 128:8192 + ic * 1024 + (dc + 1) * 128],
                        idb_sb[:, :])
                if dc == NDC - 1:
                    m_wqT[ic] = op("t", tr, inc="T")
                else:
                    op("t", tr)
            wt("s", "T", m_wqT[ic])
            m_wqE[ic] = op("s", lambda e, ic=ic: e.activation(
                out=wT[:, :].rearrange("p (dc i) -> p dc i", dc=NDC, i=1024)[:, :, ic * 128:ic * 128 + 128],
                in_=psst_bf[:, 2048:3072].rearrange("p (s c) -> p s c", s=8, c=128),
                func=AF.Copy, bias=0.0), inc="S")

        m_qp = [None] * NTB
        m_qtmE = [None] * NTB
        m_qT = [None] * NTB
        m_qfm = [None] * NTB
        m_rope = [None] * NTB

        def emit_qproj(tb):
            par = tb % 2
            wt("t", "S", m_h1fm[tb])
            if tb == 1:
                wt("t", "S", m_wqE[NDC - 1])
            if tb >= 2:
                wt("t", "V", m_qtmE[tb - 2])
            for dc in range(NDC):
                for ih in range(2):
                    def mm(e, tb=tb, dc=dc, ih=ih, par=par):
                        out = ps_st[:, par * 1024 + ih * 512:par * 1024 + (ih + 1) * 512]
                        return e.matmul(out,
                                        bufA[:, dc * 2048 + tb * 128:dc * 2048 + tb * 128 + 128],
                                        wT[:, dc * 1024 + ih * 512:dc * 1024 + (ih + 1) * 512],
                                        start=(dc == 0), stop=(dc == NDC - 1))
                    if dc == NDC - 1 and ih == 1:
                        m_qp[tb] = op("t", mm, inc="T")
                    else:
                        op("t", mm)

        def emit_qrope(tb):
            par = tb % 2
            csc = cs_y[:, tb * 64:(tb + 1) * 64]
            css = cs_y[:, 1024 + tb * 64:1024 + (tb + 1) * 64]
            wt("v", "T", m_qp[tb])
            psq = ps_st[:, par * 1024:par * 1024 + 1024]
            if tb >= 1:
                vsync(m_rope[tb - 1])
            if tb >= 2 and m_qT[tb - 2] is not None:
                wt("v", "T", m_qT[tb - 2])
            op("v", lambda e, css=css, psq=psq: e.tensor_tensor(
                out=qtm[:, :].rearrange("p (h a j) -> p h a j", h=16, a=2, j=32),
                in0=psq.rearrange("p (h a j) -> p h a j", h=16, a=2, j=32)[:, :, ::-1, :],
                in1=css.rearrange("p (a j) -> p a j", a=2, j=32).unsqueeze(1).broadcast_to((128, 16, 2, 32)),
                op=A.mult))
            m_qtmE[tb] = op("v", lambda e, csc=csc, psq=psq, par=par: e.tensor_tensor(
                out=qrope[:, par * 1024:par * 1024 + 1024].rearrange("p (h f) -> p h f", h=16, f=64),
                in0=psq.rearrange("p (h f) -> p h f", h=16, f=64),
                in1=csc.unsqueeze(1).broadcast_to((128, 16, 64)), op=A.mult), inc="V")
            vsync(m_qtmE[tb])
            m_rope[tb] = op("v", lambda e, par=par: e.tensor_tensor(
                out=qrope[:, par * 1024:par * 1024 + 1024],
                in0=qrope[:, par * 1024:par * 1024 + 1024],
                in1=qtm[:, :], op=A.add), inc="V")

        def emit_qT(tb):
            par = tb % 2
            pso = par
            wt("t", "V", m_rope[tb])
            if tb < 2:
                wt("t", "V", m_boutc)
            else:
                wt("t", "S", m_qfm[tb - 2])
            for pr in range(NPAIR):
                def tr(e, pr=pr, par=par, pso=pso):
                    return e.transpose(psw_bf[:, pso * 1024 + pr * 128:pso * 1024 + pr * 128 + 128],
                                       qrope[:, par * 1024 + pr * 128:par * 1024 + (pr + 1) * 128], idb_sb[:, :])
                if pr == NPAIR - 1:
                    m_qT[tb] = op("t", tr, inc="T")
                else:
                    op("t", tr)
            wt("s", "T", m_qT[tb])
            m_qfm[tb] = op("s", lambda e, tb=tb, pso=pso: e.activation(
                out=qfm[:, :].rearrange("p (pr t) -> p pr t", pr=NPAIR, t=2048)[:, :, tb * 128:tb * 128 + 128],
                in_=psw_bf[:, pso * 1024:pso * 1024 + 1024].rearrange("p (pr c) -> p pr c", pr=NPAIR, c=128),
                func=AF.Copy, bias=0.0), inc="S")

        m_h1T = [None] * NTB
        m_h1fm = [None] * NTB
        wt("g", "S", m_sin2)
        for tb in range(NTB):
            slot = tb % 2
            if tb % 4 == 0:
                gsync(m_ln1g[tb // 4])
            if tb >= 2:
                wt("g", "T", m_h1T[tb - 2])
            m_n1 = op("g", lambda e, tb=tb, slot=slot: e.tensor_scalar(
                out=tmbf[:, slot * 1024:slot * 1024 + 1024],
                in0=x_sb[:, tb * 1024:(tb + 1) * 1024],
                scalar1=nm1[:, tb:tb + 1], scalar2=iv1[:, tb:tb + 1],
                op0=A.add, op1=A.mult), inc="G")
            wt("t", "G", m_n1)
            if tb >= 2:
                wt("t", "S", m_h1fm[tb - 2])
            for dc in range(NDC):
                def tr(e, dc=dc, slot=slot):
                    return e.transpose(
                        pspv_bf[:, slot * 1024 + dc * 128:slot * 1024 + dc * 128 + 128],
                        tmbf[:, slot * 1024 + dc * 128:slot * 1024 + dc * 128 + 128],
                        idb_sb[:, :])
                if dc == NDC - 1:
                    m_h1T[tb] = op("t", tr, inc="T")
                else:
                    op("t", tr)
            wt("s", "T", m_h1T[tb])
            m_h1fm[tb] = op("s", lambda e, tb=tb, slot=slot: e.activation(
                out=bufA[:, :].rearrange("p (dc t) -> p dc t", dc=NDC, t=2048)[:, :, tb * 128:tb * 128 + 128],
                in_=pspv_bf[:, slot * 1024:slot * 1024 + 1024].rearrange("p (dc c) -> p dc c", dc=NDC, c=128),
                func=AF.Copy, bias=0.0), inc="S")
            if tb >= 1:
                if tb >= 3:
                    emit_qT(tb - 3)
                emit_qproj(tb - 1)
                emit_qrope(tb - 1)
            if tb == NTB - 1:
                emit_qT(tb - 2)
                emit_qproj(tb)
                emit_qrope(tb)
                emit_qT(tb - 1)
                emit_qT(tb)

        if stage <= 1:
            debug_finish()
        m_kvT = [None] * 4
        m_kvE = [None] * 4
        for tc in range(4):
            if tc == 0:
                wt("t", "S", m_qfm[NTB - 1])
            wt("t", "V", m_kvw)
            wt("t", "S", m_h1fm[4 * tc + 3])
            if tc >= 1:
                wt("t", "S", m_kvE[tc - 1])
            for dc in range(NDC):
                def mm(e, tc=tc, dc=dc):
                    return e.matmul(ps_w[:, 0:512], kvw[:, dc * 128:(dc + 1) * 128],
                                    bufA[:, dc * 2048 + tc * 512:dc * 2048 + (tc + 1) * 512],
                                    start=(dc == 0), stop=(dc == NDC - 1))
                if dc == NDC - 1:
                    m_kvT[tc] = op("t", mm, inc="T")
                else:
                    op("t", mm)
            wt("s", "T", m_kvT[tc])
            m_kvE[tc] = op("s", lambda e, tc=tc: e.activation(
                out=h2fm[:, tc * 512:(tc + 1) * 512], in_=ps_w[:, 0:512],
                func=AF.Copy, bias=0.0), inc="S")


        streams["y"].append(lambda e, v=m_kvT[3]: e.wait_ge(sems["T"][0], v))
        m_winL = dma(lambda e: e.dma_start(
            out=wstage[:, :].rearrange("p (c h) -> p c h", c=8, h=64),
            in_=win.rearrange("(c p) h -> p c h", p=128)))
        wt("v", "dW", m_winL)
        wt("v", "T", m_kvT[3])
        m_win = op("v", lambda e: e.tensor_copy(out=win_sb, in_=wstage[:, :]), inc="V")

        m_vT = [None] * 2
        m_vE = [None] * 2
        for gs in range(2):
            wt("t", "S", m_kvE[3])
            if gs == 1:
                wt("t", "S", m_vE[0])
            for i in range(8):
                tb = gs * 8 + i
                def tr(e, tb=tb, i=i):
                    return e.transpose(psw_bf[:, i * 128:i * 128 + 64],
                                       h2fm[64:128, tb * 128:(tb + 1) * 128],
                                       idb_sb[64:128, 64:128])
                if i == 7:
                    m_vT[gs] = op("t", tr, inc="T")
                else:
                    op("t", tr)
            wt("s", "T", m_vT[gs])
            m_vE[gs] = op("s", lambda e, gs=gs: e.activation(
                out=v1[:, :].rearrange("p (t c) -> p t c", t=16, c=65)[:, gs * 8:gs * 8 + 8, 0:64],
                in_=psw_bf[:, 0:1024].rearrange("p (i c) -> p i c", i=8, c=128)[:, :, 0:64],
                func=AF.Copy, bias=0.0), inc="S")

        m_kT = [None] * 2
        m_ktm = [None] * 2
        for gs in range(2):
            wt("t", "S", m_vE[gs] if gs == 0 else m_ktm[0])
            if gs == 0:
                wt("t", "S", m_vE[1])
            for i in range(8):
                tb = gs * 8 + i
                def tr(e, tb=tb, i=i):
                    return e.transpose(psw_bf[:, i * 128:i * 128 + 64],
                                       h2fm[0:64, tb * 128:(tb + 1) * 128],
                                       idb_sb[0:64, 0:64])
                if i == 7:
                    m_kT[gs] = op("t", tr, inc="T")
                else:
                    op("t", tr)
            wt("s", "T", m_kT[gs])
            if gs == 0:
                wt("s", "V", m_rope[NTB - 1])
            m_ktm[gs] = op("s", lambda e, gs=gs: e.activation(
                out=qtm[:, gs * 512:gs * 512 + 512].rearrange("p (t c) -> p t c", t=8, c=64),
                in_=psw_bf[:, 0:1024].rearrange("p (i c) -> p i c", i=8, c=128)[:, :, 0:64],
                func=AF.Copy, bias=0.0), inc="S")

        m_krope = [None] * 2
        m_kback = [None] * NTB
        m_kvdE = {}
        m_kbackL = None
        for gs in range(2):
            wt("v", "S", m_ktm[gs])
            csc = cs_y[:, gs * 512:(gs + 1) * 512]
            css = cs_y[:, 1024 + gs * 512:1024 + (gs + 1) * 512]
            ktg = qtm[:, gs * 512:(gs + 1) * 512]
            m_sw = op("v", lambda e, ktg=ktg, css=css, gs=gs: e.tensor_tensor(
                out=tmbf[:, 1024 + gs * 512:1024 + gs * 512 + 512].rearrange(
                    "p (t a j) -> p t a j", t=8, a=2, j=32),
                in0=ktg.rearrange("p (t a j) -> p t a j", t=8, a=2, j=32)[:, :, ::-1, :],
                in1=css.rearrange("p (t a j) -> p t a j", t=8, a=2, j=32), op=A.mult), inc="V")
            vsync(m_sw)
            m_kc = op("v", lambda e, ktg=ktg, csc=csc: e.tensor_tensor(
                out=ktg, in0=ktg, in1=csc, op=A.mult), inc="V")
            vsync(m_kc)
            m_krope[gs] = op("v", lambda e, ktg=ktg, gs=gs: e.tensor_tensor(
                out=ktg, in0=ktg,
                in1=tmbf[:, 1024 + gs * 512:1024 + gs * 512 + 512], op=A.add), inc="V")
        for tb in range(NTB):
            if tb % 4 == 0:
                wt("t", "V", m_krope[(tb + 3) // 8])
                if tb == 0:
                    wt("t", "S", m_ktm[1])
                if tb >= 4:
                    wt("t", "S", m_kvdE.get(tb - 4))
            m_kback[tb] = op("t", lambda e, tb=tb: e.transpose(
                psw_bf[0:64, (tb % 4) * 128:(tb % 4) * 128 + 128],
                qtm[:, tb * 64:tb * 64 + 64], idb_sb[:, :]), inc="T")
            if tb % 4 == 3:
                wt("s", "T", m_kback[tb])
                mm_ = op("s", lambda e, tb=tb: e.activation(
                    out=kvd[0:64, (tb - 3) * 128:(tb + 1) * 128],
                    in_=psw_bf[0:64, 0:512], func=AF.Copy, bias=0.0), inc="S")
                for q_ in range(4):
                    m_kvdE[tb - q_] = mm_
        m_kbackL = m_kback[NTB - 1]
        wt("s", "S", m_kvdE[15])
        m_kdup = op("s", lambda e: e.activation(
            out=kvd[64:128, :], in_=kvd[0:64, :], func=AF.Copy, bias=0.0), inc="S")

        if stage <= 2:
            debug_finish()


        m_woTh = {}
        m_woT = [None] * NDC
        m_woE = [None] * NDC

        def emit_woT(ics):
            if ics[0] == 0:
                streams["g"].append(lambda e, v=m_qp[NTB - 1]: e.wait_ge(sems["T"][0], v))
                streams["g"].append(lambda e, v=m_win: e.wait_ge(sems["V"][0], v))
            for ic in ics:
                for cg in range(2):
                    ch = ic * 2 + cg
                    sl = ch % 2
                    prev2 = m_woTh.get((ic, cg - 2)) or m_woTh.get((ic - 1, cg))
                    if prev2 is not None:
                        streams["g"].append(lambda e, v=prev2: e.wait_ge(sems["T"][0], v))
                    m_load = op("g", lambda e, ic=ic, cg=cg, sl=sl: e.dma_start(
                        out=wstage_bf[:, sl * 512:sl * 512 + 512],
                        in_=wo[ic * 128:(ic + 1) * 128, cg * 512:(cg + 1) * 512]),
                        inc="dQ", inck=16)
                    wt("t", "dQ", m_load + (16 if ch % 2 == 0 else 0))
                    if ic == 0 and cg == 0:
                        wt("t", "S", m_kdup)
                    if ic >= 1 and cg == 0:
                        wt("t", "V", m_woE[ic - 1])
                    for j in range(4):
                        def tr(e, j=j, cg=cg, sl=sl):
                            return e.transpose(
                                psw_bf[:, cg * 512 + j * 128:cg * 512 + j * 128 + 128],
                                wstage_bf[:, sl * 512 + j * 128:sl * 512 + (j + 1) * 128], idb_sb[:, :])
                        if j == 3:
                            m_woTh[(ic, cg)] = op("t", tr, inc="T")
                        else:
                            op("t", tr)
                m_woT[ic] = m_woTh[(ic, 1)]
                wt("v", "T", m_woT[ic])
                if ic == 0:
                    wt("v", "S", m_qfm[NTB - 1])
                m_woE[ic] = op("v", lambda e, ic=ic: e.tensor_copy(
                    out=wT[:, :].rearrange("p (dc i) -> p dc i", dc=NDC, i=1024)[:, :, ic * 128:ic * 128 + 128],
                    in_=psw_bf[:, 0:1024].rearrange("p (s c) -> p s c", s=8, c=128)), inc="V")

        emit_woT(list(range(NDC)))
        strickle = []
        dqq = []

        m_ep = [[None] * NPAIR for _ in range(NQC)]
        m_bc = [[None] * NPAIR for _ in range(NQC)]
        m_nm = [[None] * NPAIR for _ in range(NQC)]
        m_recip = [None] * NQC
        m_wo = {}
        m_yev = {}
        m_nrm = {}
        m_add2 = {}
        m_add4 = {}
        m_n3 = {}
        m_h2T = {}
        m_h2E = {}
        m_fe = {}
        m_gb = [None] * NQC
        m_fpl = [None] * NQC
        m_accL = [None]
        exp_hist = []
        pv_km = {}
        exp_m = {}

        def emit_scores(qc, pair, kb):
            sub = kb % 2
            di = kb - 4 * qc
            col0 = 128 * di if di >= 0 else 0
            if qc == 0 and pair == 0 and kb < 2:
                wt("t", "S", m_qfm[NTB - 1])
                wt("t", "V", m_qtmE[NTB - 1])
                wt("t", "S", m_kdup)
                wt("t", "V", m_ones)
            if qc >= 2 and pair == 0 and kb == 0:
                wt("t", "S", m_qfm[4 * qc + 3])
            if len(exp_hist) >= 2:
                wt("t", "S", exp_hist[-2])
            mkey = None
            for hb in (0, 64):
                base = (0 if hb == 0 else 1024) + sub * 512

                def mm(e, qc=qc, pair=pair, kb=kb, hb=hb, base=base, col0=col0, di=di):
                    return e.matmul(
                        ps_st[:, base + col0:base + 512],
                        kvd[hb:hb + 64, kb * 128:(kb + 1) * 128],
                        qfm[hb:hb + 64, pair * 2048 + qc * 512 + col0:pair * 2048 + (qc + 1) * 512],
                        start=True, stop=(di < 0), tile_position=(hb, 0))
                if di < 0 and hb == 64:
                    mkey = op("t", mm, inc="T")
                else:
                    op("t", mm)
                if di >= 0:
                    op("t", lambda e, hb=hb, base=base, col0=col0: e.matmul(
                        ps_st[:, base + col0:base + col0 + 128],
                        tria_sb[hb:hb + 64, 0:128], negi_sb[hb:hb + 64, 0:128],
                        start=False, stop=False, tile_position=(hb, 0)))

                    def tr2(e, hb=hb, base=base, col0=col0):
                        return e.matmul(
                            ps_st[:, base + col0:base + col0 + 128],
                            tria_sb[hb:hb + 64, 128:256], negi_sb[hb:hb + 64, 128:256],
                            start=False, stop=True, tile_position=(hb, 0))
                    if hb == 64:
                        mkey = op("t", tr2, inc="T")
                    else:
                        op("t", tr2)
            return mkey

        def emit_exp(qc, pair, kb, m_scores):
            sub = kb % 2
            di = kb - 4 * qc
            col0 = 128 * di if di >= 0 else 0
            HS = 2048 if qc == 0 else 8192
            PT = h2fm if qc == 0 else bufA
            wt("s", "T", m_scores)
            if qc == 0 and pair == 0 and kb == 0:
                wt("s", "T", m_kT[1])
            if pair > 0:
                wt("s", "T", pv_km[(qc, pair - 1, kb)])
            elif qc > 0:
                prev_last = 4 * (qc - 1) + 3
                wt("s", "T", pv_km[(qc - 1, 7, min(kb, prev_last))])
            if qc == 1 and pair == 0 and kb == 0:
                wt("s", "T", m_qp[NTB - 1])
            in_ap = ps_st[:, :].rearrange("p (h c) -> p h c", h=2, c=1024)[:, :, sub * 512 + col0:sub * 512 + 512]
            out_ap = PT[:, :].rearrange("p (h k) -> p h k", h=2, k=HS)[:, :, kb * 512 + col0:kb * 512 + 512]
            m = op("s", lambda e, in_ap=in_ap, out_ap=out_ap: e.activation(
                out=out_ap, in_=in_ap, func=AF.Exp, bias=expb[:, 0:1], scale=0.125), inc="S")
            exp_hist.append(m)
            exp_m[(qc, pair, kb)] = m

        def emit_pv(qc, pair, kb):
            nkb = 4 * qc + 4
            di = kb - 4 * qc
            col0 = 128 * di if di >= 0 else 0
            HS = 2048 if qc == 0 else 8192
            PT = h2fm if qc == 0 else bufA
            wt("t", "S", exp_m[(qc, pair, kb)])
            mkey = None
            for h in range(2):
                def mm(e, pair=pair, kb=kb, h=h, col0=col0, HS=HS, PT=PT, nkb=nkb):
                    return e.matmul(
                        ps_pv[0:65, h * 512 + col0:h * 512 + 512],
                        v1[:, kb * 65:kb * 65 + 65],
                        PT[:, h * HS + kb * 512 + col0:h * HS + kb * 512 + 512],
                        start=(kb == 0), stop=(kb == nkb - 1))
                if h == 1:
                    mkey = op("t", mm, inc="T")
                else:
                    op("t", mm)
            pv_km[(qc, pair, kb)] = mkey

        def emit_epilogue(qc, pair):
            nkb = 4 * qc + 4
            wt("v", "T", pv_km[(qc, pair, nkb - 1)])
            if qc > 0:
                wt("v", "T", m_wo.get((qc - 1, 4 * (qc - 1) + 3)))
                wt("v", "S", m_recipH[(qc - 1, 0 if pair < 4 else 1)])
            op("v", lambda e, pair=pair: e.tensor_copy(
                out=attnq[0:64, pair * 512:(pair + 1) * 512], in_=ps_pv[0:64, 0:512]))
            op("v", lambda e, pair=pair: e.tensor_copy(
                out=attnq[64:128, pair * 512:(pair + 1) * 512], in_=ps_pv[0:64, 512:1024]))
            for hh in range(2):
                h = 2 * pair + hh
                b, c = h % 4, h // 4

                def cp(e, hh=hh, b=b, c=c):
                    return e.tensor_copy(out=sums[32 * b:32 * b + 1, 512 * c:512 * c + 512],
                                         in_=ps_pv[64:65, hh * 512:hh * 512 + 512])
                if hh == 1:
                    m_ep[qc][pair] = op("v", cp, inc="V")
                else:
                    op("v", cp)

        m_recipH = {}

        def emit_recip_half(qc, half):
            wt("s", "V", m_ep[qc][3 if half == 0 else 7])
            c0 = half * 1024
            m_ln = op("s", lambda e, c0=c0: e.activation(
                out=sums[:, c0:c0 + 1024], in_=sums[:, c0:c0 + 1024], func=AF.Ln,
                bias=zerob[:, 0:1]), inc="S")
            ssync(m_ln)
            m_recipH[(qc, half)] = op("s", lambda e, c0=c0: e.activation(
                out=sums[:, c0:c0 + 1024], in_=sums[:, c0:c0 + 1024], func=AF.Exp,
                bias=zerob[:, 0:1], scale=-1.0), inc="S")

        def emit_recip(qc):
            if (qc, 0) not in m_recipH:
                emit_recip_half(qc, 0)
            emit_recip_half(qc, 1)
            m_recip[qc] = m_recipH[(qc, 1)]

        def emit_bcast_norm(qc):
            vsync(m_ep[qc][7])
            for pair in range(NPAIR):
                b, c = (2 * pair) % 4, (2 * pair) // 4
                wt("t", "S", m_recipH[(qc, 0 if pair < 4 else 1)])
                if pair >= 2:
                    wt("t", "V", m_nm[qc][pair - 2])
                if pair < 2:
                    wt("t", "V", m_woE[NDC - 1])
                m_bc[qc][pair] = op("t", lambda e, b=b, c=c, pair=pair: e.matmul(
                    ps_w[:, (pair % 2) * 512:(pair % 2) * 512 + 512],
                    fsel_sb[32 * b:32 * b + 64, :], sums[32 * b:32 * b + 64, 512 * c:512 * c + 512],
                    start=True, stop=True), inc="T")
                wt("v", "T", m_bc[qc][pair])
                m_nm[qc][pair] = op("v", lambda e, pair=pair: e.tensor_tensor(
                    out=attnq[:, pair * 512:(pair + 1) * 512],
                    in0=attnq[:, pair * 512:(pair + 1) * 512],
                    in1=ps_w[:, (pair % 2) * 512:(pair % 2) * 512 + 512], op=A.mult), inc="V")

        def emit_wo(qc, k):
            tb = 4 * qc + k
            wt("t", "V", m_nm[qc][7])
            if k >= 1:
                wt("t", "V", m_yev[(qc, tb - 1)])
            for dc in range(NDC):
                for ih in range(2):
                    def mm(e, qc=qc, k=k, dc=dc, ih=ih):
                        return e.matmul(ps_w[:, ih * 512:(ih + 1) * 512],
                                        attnq[:, dc * 512 + k * 128:dc * 512 + k * 128 + 128],
                                        wT[:, dc * 1024 + ih * 512:dc * 1024 + (ih + 1) * 512],
                                        start=(dc == 0), stop=(dc == NDC - 1))
                    if dc == NDC - 1 and ih == 1:
                        m_wo[(qc, tb)] = op("t", mm, inc="T")
                    else:
                        op("t", mm)

        def emit_yev(qc, k):
            tb = 4 * qc + k
            sl = k % 2
            wt("v", "T", m_wo[(qc, tb)])
            if k >= 2:
                if qc == NQC - 1:
                    vsync(m_nrm[(qc, tb - 2)])
                else:
                    wt("v", "G", m_nrm[(qc, tb - 2)])
            m_yev[(qc, tb)] = op("v", lambda e, sl=sl: e.tensor_copy(
                out=cs_y[:, sl * 1024:sl * 1024 + 1024], in_=ps_w[:, :]), inc="V")
            vsync(m_yev[(qc, tb)])
            op("v", lambda e, k=k, sl=sl: e.bn_stats(
                out=stats[:, k * 12:k * 12 + 6], in_=cs_y[:, sl * 1024:sl * 1024 + 512]))
            return op("v", lambda e, k=k, sl=sl: e.bn_stats(
                out=stats[:, k * 12 + 6:k * 12 + 12], in_=cs_y[:, sl * 1024 + 512:sl * 1024 + 1024]),
                inc="V")

        def ln_core(ks, statbase, mvbase, nmx, ivx, vepbase, mstats):
            vsync(mstats)
            m_ag = None
            for k in ks:
                m_ag = op("v", lambda e, k=k, statbase=statbase, mvbase=mvbase: e.bn_aggr(
                    out=mv[:, mvbase + k * 2:mvbase + k * 2 + 2],
                    in_=stats[:, statbase + k * 12:statbase + k * 12 + 12]), inc="V")
            vsync(m_ag)
            k0, w = ks[0], len(ks)
            op("v", lambda e, k0=k0, w=w, mvbase=mvbase, nmx=nmx: e.tensor_scalar(
                out=nmx[:, k0:k0 + w], in0=mv[:, mvbase + 2 * k0:mvbase + 2 * k0 + 2 * w:2],
                scalar1=-1.0, scalar2=None, op0=A.mult))
            mfv = op("v", lambda e, k0=k0, w=w, mvbase=mvbase, vepbase=vepbase: e.tensor_scalar(
                out=vep[:, vepbase + k0:vepbase + k0 + w],
                in0=mv[:, mvbase + 2 * k0 + 1:mvbase + 2 * k0 + 2 * w:2],
                scalar1=EPS, scalar2=None, op0=A.add), inc="V")
            wt("g", "V", mfv)
            mg = op("g", lambda e, k0=k0, w=w, ivx=ivx, vepbase=vepbase: e.tensor_tensor(
                out=ivx[:, k0:k0 + w], in0=vep[:, vepbase + k0:vepbase + k0 + w],
                in1=neghalf[:, k0:k0 + w], op=A.pow), inc="G")
            return mfv, mg

        def emit_ln2_half(qc, ks, mstats):
            fast = (qc == NQC - 1)
            eng, esem, syn = ("v", "V", vsync) if fast else ("g", "G", gsync)
            mfv, mg = ln_core(ks, 0, 0, nm2, iv2, 0, mstats)
            for k in ks:
                tb = 4 * qc + k
                sl = k % 2
                wt(eng, "V", mfv)
                wt(eng, "V", m_yev[(qc, tb)])
                wt(eng, "G", mg)
                if k != ks[0]:
                    syn(m_add2[(qc, tb - 1)])
                if fast and qc >= 1:
                    wt("v", "G", m_add4.get((qc - 1, 4 * (qc - 1) + 3)))
                m_nrm[(qc, tb)] = op(eng, lambda e, k=k, sl=sl: e.tensor_scalar(
                    out=lnbuf[:, :], in0=cs_y[:, sl * 1024:sl * 1024 + 1024],
                    scalar1=nm2[:, k:k + 1], scalar2=iv2[:, k:k + 1],
                    op0=A.add, op1=A.mult), inc=esem)
                syn(m_nrm[(qc, tb)])
                m_add2[(qc, tb)] = op(eng, lambda e, tb=tb: e.tensor_tensor(
                    out=x_sb[:, tb * 1024:(tb + 1) * 1024],
                    in0=x_sb[:, tb * 1024:(tb + 1) * 1024],
                    in1=lnbuf[:, :], op=A.add), inc=esem)

        def emit_ln3_half(qc, ks):
            fast = (qc == NQC - 1)
            eng, esem, syn = ("v", "V", vsync) if fast else ("g", "G", gsync)
            m_st = None
            for k in ks:
                tb = 4 * qc + k
                if fast:
                    vsync(m_add2[(qc, tb)])
                else:
                    wt("v", "G", m_add2[(qc, tb)])
                op("v", lambda e, k=k, tb=tb: e.bn_stats(
                    out=stats[:, 48 + k * 12:48 + k * 12 + 6], in_=x_sb[:, tb * 1024:tb * 1024 + 512]))
                m_st = op("v", lambda e, k=k, tb=tb: e.bn_stats(
                    out=stats[:, 48 + k * 12 + 6:48 + k * 12 + 12],
                    in_=x_sb[:, tb * 1024 + 512:tb * 1024 + 1024]), inc="V")
            mfv, mg = ln_core(ks, 48, 8, nm3, iv3, 4, m_st)
            for k in ks:
                tb = 4 * qc + k
                sl = k % 2
                wt(eng, "G", mg)
                syn(m_add2[(qc, tb)])
                if (qc, tb - 2) in m_h2T:
                    wt(eng, "T", m_h2T[(qc, tb - 2)])
                wt(eng, "V", mfv)
                m_n3[(qc, tb)] = op(eng, lambda e, k=k, tb=tb, sl=sl: e.tensor_scalar(
                    out=tmbf[:, sl * 1024:sl * 1024 + 1024],
                    in0=x_sb[:, tb * 1024:(tb + 1) * 1024],
                    scalar1=nm3[:, k:k + 1], scalar2=iv3[:, k:k + 1],
                    op0=A.add, op1=A.mult), inc=esem)

        def emit_h2T(qc, k):
            tb = 4 * qc + k
            sl = k % 2
            wt("t", "V" if qc == NQC - 1 else "G", m_n3[(qc, tb)])
            if k == 0:
                wt("t", "V", m_yev[(qc, 4 * qc + 3)])
            if (qc, tb - 1) in m_h2E:
                wt("t", "V", m_h2E[(qc, tb - 1)])
            for dc in range(NDC):
                def tr(e, dc=dc, sl=sl):
                    return e.transpose(psw_bf[:, sl * 1024 + dc * 128:sl * 1024 + dc * 128 + 128],
                                       tmbf[:, sl * 1024 + dc * 128:sl * 1024 + dc * 128 + 128],
                                       idb_sb[:, :])
                if dc == NDC - 1:
                    m_h2T[(qc, tb)] = op("t", tr, inc="T")
                else:
                    op("t", tr)
            wt("v", "T", m_h2T[(qc, tb)])
            m_h2E[(qc, tb)] = op("v", lambda e, k=k, sl=sl: e.tensor_copy(
                out=h2fm[:, :].rearrange("p (dc t) -> p dc t", dc=NDC, t=512)[:, :, k * 128:k * 128 + 128],
                in_=psw_bf[:, sl * 1024:sl * 1024 + 1024].rearrange("p (dc c) -> p dc c", dc=NDC, c=128)), inc="V")

        ffn_st = {}

        def emit_ffn_front(qc, part):
            if part == 0:
                wt("t", "V", m_h2E[(qc, 4 * qc + 3)])
                wt("t", "V", m_win)
                wt("t", "V", m_ffnw)
                if qc > 0:
                    wt("t", "V", m_gb[qc - 1])
                m_sp = None
                for dc in range(NDC):
                    for half in range(2):
                        def mm(e, dc=dc, half=half):
                            return e.matmul(
                                ps_w[half * 64:half * 64 + 64, 0:256],
                                win_sb[:, dc * 64:(dc + 1) * 64],
                                h2fm[:, dc * 512 + half * 256:dc * 512 + (half + 1) * 256],
                                start=(dc == 0), stop=(dc == NDC - 1), tile_position=(0, half * 64),
                                skip_group_check=True)
                        if dc == NDC - 1 and half == 1:
                            m_sp = op("t", mm, inc="T")
                        else:
                            op("t", mm)
                wt("v", "T", m_sp)
                if qc > 0:
                    wt("v", "T", m_fpl[qc - 1])
                m_sev = op("v", lambda e: e.tensor_copy(out=s_sbv, in_=ps_w[:, 0:256]), inc="V")
                ffn_st[qc] = {"sev": m_sev, "relu": {}, "acc": {}}
            fst = ffn_st[qc]
            m_sev, m_relu, m_acc = fst["sev"], fst["relu"], fst["acc"]

            fastq = (qc == NQC - 1)

            def step(i, qc=qc, fastq=fastq):
                if fastq:
                    wt("s", "V", m_sev if i < 2 else m_acc[i - 2])
                    m_relu[i] = op("s", lambda e, i=i: e.activation(
                        out=hbuf[i % 2], in_=s_sbv, func=AF.Relu,
                        bias=b1pd[:, i:i + 1], scale=w1d[:, i:i + 1]), inc="S")
                    wt("v", "S", m_relu[i])
                    if i >= 2:
                        vsync(m_acc[i - 2])
                    elif m_accL[0] is not None:
                        vsync(m_accL[0])
                elif i < 2:
                    vsync(m_sev)
                    if m_accL[0] is not None:
                        vsync(m_accL[0])
                else:
                    vsync(m_acc[i - 2])
                if not fastq:
                    m_pre = op("v", lambda e, i=i: e.tensor_scalar(
                        out=hbuf[i % 2], in0=s_sbv, scalar1=w1d[:, i:i + 1],
                        scalar2=b1pd[:, i:i + 1], op0=A.mult, op1=A.add), inc="V")
                    vsync(m_pre)
                    m_relu[i] = op("v", lambda e, i=i: e.tensor_scalar(
                        out=hbuf[i % 2], in0=hbuf[i % 2], scalar1=0.0, scalar2=None,
                        op0=A.max), inc="V")
                    vsync(m_relu[i])
                if i < 2:
                    m_acc[i] = op("v", lambda e, i=i: e.tensor_scalar(
                        out=gsl[i], in0=hbuf[i], scalar1=w2d[:, i:i + 1],
                        scalar2=(b2d[:, 0:1] if i == 0 else zerob[:, 0:1]),
                        op0=A.mult, op1=A.add), inc="V")
                else:
                    m_acc[i] = op("v", lambda e, i=i: e.scalar_tensor_tensor(
                        out=gsl[i % 2], in0=hbuf[i % 2], scalar=w2d[:, i:i + 1], in1=gsl[i % 2],
                        op0=A.mult, op1=A.add), inc="V")
                if i == 15:
                    vsync(m_acc[15])
                    m_gb[qc] = op("v", lambda e: e.tensor_tensor(
                        out=gb, in0=gsl[0], in1=gsl[1], op=A.add), inc="V")
                    m_accL[0] = m_acc[15]
            for i in (range(0, 8) if part == 0 else range(8, 16)):
                strickle.append(lambda i=i: step(i))

        def emit_fproj(qc, tt):
            hb = 64 * (tt // 2)
            gcol = (tt % 2) * 128
            wt("t", "V", m_gb[qc] if tt == 0 else m_fe[(qc, tt - 1)])
            m_fp = None
            for ih in range(2):
                def mm(e, hb=hb, gcol=gcol, ih=ih):
                    return e.matmul(ps_w[:, ih * 512:(ih + 1) * 512],
                                    gb[hb:hb + 64, gcol:gcol + 128],
                                    woutd[hb:hb + 64, ih * 512:(ih + 1) * 512],
                                    start=True, stop=True, tile_position=(hb, 0))
                if ih == 1:
                    m_fp = op("t", mm, inc="T")
                else:
                    op("t", mm)
            m_fpl[qc] = m_fp
            wt("v", "T", m_fp)
            if tt >= 1:
                if qc == NQC - 1:
                    vsync(m_nrm4.get((qc, tt - 1)))
                else:
                    wt("v", "G", m_nrm4.get((qc, tt - 1)))
            for ih in range(2):
                def ev(e, ih=ih):
                    return e.tensor_tensor(
                        out=f_sb[:, ih * 512:(ih + 1) * 512],
                        in0=ps_w[:, ih * 512:(ih + 1) * 512],
                        in1=boutb[:, ih * 512:(ih + 1) * 512], op=A.add)
                if ih == 1:
                    m_fe[(qc, tt)] = op("v", ev, inc="V")
                else:
                    op("v", ev)

        m_nrm4 = {}

        def emit_ln4_single(qc, k):
            fast = (qc == NQC - 1)
            eng, esem, syn = ("v", "V", vsync) if fast else ("g", "G", gsync)
            vsync(m_fe[(qc, k)])
            op("v", lambda e, k=k: e.bn_stats(
                out=stats[:, 96 + k * 12:96 + k * 12 + 6], in_=f_sb[:, 0:512]))
            m_st = op("v", lambda e, k=k: e.bn_stats(
                out=stats[:, 96 + k * 12 + 6:96 + k * 12 + 12], in_=f_sb[:, 512:1024]), inc="V")
            mfv, mg = ln_core((k,), 96, 16, nm4, iv4, 8, m_st)
            tb = 4 * qc + k
            wt(eng, "V", mfv)
            wt(eng, "V", m_fe[(qc, k)])
            wt(eng, "G", mg)
            syn(m_add2[(qc, tb)])
            if k >= 1:
                syn(m_add4[(qc, tb - 1)])
            m_nrm4[(qc, k)] = op(eng, lambda e, k=k: e.tensor_scalar(
                out=lnbuf[:, :], in0=f_sb[:, :],
                scalar1=nm4[:, k:k + 1], scalar2=iv4[:, k:k + 1],
                op0=A.add, op1=A.mult), inc=esem)
            syn(m_nrm4[(qc, k)])
            m_add4[(qc, tb)] = op(eng, lambda e, tb=tb: e.tensor_tensor(
                out=x_sb[:, tb * 1024:(tb + 1) * 1024],
                in0=x_sb[:, tb * 1024:(tb + 1) * 1024],
                in1=lnbuf[:, :], op=A.add), inc=esem)

        def emit_yout(qc):
            asem = "V" if qc == NQC - 1 else "G"
            for hf in range(2):
                streams["y"].append(lambda e, v=m_add4[(qc, 4 * qc + 2 * hf + 1)], asem=asem:
                                     e.wait_ge(sems[asem][0], v))
                streams["y"].append(lambda e, v=sems["dO"][1]: (e.wait_ge(sems["dO"][0], v) if v > 0 else None))
                op("y", lambda e, qc=qc, hf=hf: e.dma_start(
                    out=yout[qc * 512 + hf * 256:qc * 512 + (hf + 1) * 256, :].rearrange("(t p) d -> p t d", p=128),
                    in_=x_sb[:, qc * 4096 + hf * 2048:qc * 4096 + (hf + 1) * 2048].rearrange(
                        "p (t d) -> p t d", t=2, d=1024)), inc="dO", inck=16)

        def D_units(qc):
            def u1():
                emit_wo(qc, 0)
                emit_yev(qc, 0)
                emit_wo(qc, 1)
                ms1 = emit_yev(qc, 1)
                emit_ln2_half(qc, (0, 1), ms1)

            def u2():
                emit_wo(qc, 2)
                emit_yev(qc, 2)
                emit_wo(qc, 3)
                ms3 = emit_yev(qc, 3)
                emit_ln2_half(qc, (2, 3), ms3)

            return [
                lambda: emit_bcast_norm(qc),
                u1,
                u2,
                lambda: (emit_ln3_half(qc, (0, 1)), emit_h2T(qc, 0), emit_h2T(qc, 1)),
                lambda: (emit_ln3_half(qc, (2, 3)), emit_h2T(qc, 2), emit_h2T(qc, 3)),
                lambda: emit_ffn_front(qc, 0),
                lambda: emit_ffn_front(qc, 1),
                lambda: (emit_fproj(qc, 0), emit_ln4_single(qc, 0)),
                lambda: (emit_fproj(qc, 1), emit_ln4_single(qc, 1)),
                lambda: (emit_fproj(qc, 2), emit_ln4_single(qc, 2)),
                lambda: (emit_fproj(qc, 3), emit_ln4_single(qc, 3), emit_yout(qc)),
            ]

        if stage <= 3:
            debug_finish()
        nqc_run = NQC if stage >= 5 else (1 if stage == 4 else 0)
        do_D = stage >= 6
        dq = []
        trail = []
        prev_qc = None
        for qc in range(nqc_run):
            nkb = 4 * qc + 4
            for pair in range(NPAIR):
                for kb in range(nkb):
                    if pair == 0 and trail:
                        trail.pop(0)()
                    if pair >= 1:
                        if kb == 0:
                            wt("t", "V", m_ep[qc][pair - 2] if pair >= 2
                               else (m_ep[qc - 1][7] if qc >= 1 else None))
                        emit_pv(qc, pair - 1, kb)
                    m_sc = emit_scores(qc, pair, kb)
                    emit_exp(qc, pair, kb, m_sc)
                    if strickle:
                        strickle.pop(0)()
                    if (kb == nkb // 2 - 1 and pair == 7 and do_D
                            and (qc, 0) not in m_recipH):
                        emit_recip_half(qc, 0)
                    if (kb == nkb // 2 - 1 and pair == 0 and do_D and prev_qc is not None
                            and (prev_qc, 0) not in m_recipH):
                        emit_recip_half(prev_qc, 0)
                    if kb == nkb // 2 - 1:
                        if dqq:
                            dqq.pop(0)()
                        elif do_D and dq and pair >= 1 and not strickle:
                            dq.pop(0)()
                if pair == 0:
                    while trail:
                        trail.pop(0)()
                    if do_D and prev_qc is not None:
                        emit_recip(prev_qc)

                if dqq:
                    dqq.pop(0)()
                elif do_D and dq and (not strickle or pair <= 1):
                    dq.pop(0)()
                if pair >= 1:
                    emit_epilogue(qc, pair - 1)

            def mk_trail(qc=qc, nkb=nkb):
                t = []
                for kb in range(nkb):
                    def f(kb=kb, qc=qc):
                        if kb == 0:
                            wt("t", "V", m_ep[qc][6])
                        emit_pv(qc, 7, kb)
                    t.append(f)
                t.append(lambda qc=qc: emit_epilogue(qc, 7))
                return t
            trail = mk_trail()
            if do_D:
                nu = D_units(qc)
                dq = nu[0:3] + dq + nu[3:]
                prev_qc = qc
        while trail:
            trail.pop(0)()
        if do_D:
            emit_recip(prev_qc)
            for u in dq:
                u()
                while strickle:
                    strickle.pop(0)()
        else:
            debug_finish()

        streams["y"].append(lambda e: e.wait_ge(sems["dO"][0], sems["dO"][1]))

        block = st.enter_context(nc.Block())

        @block.sync
        def _(eng):
            for f in streams["y"]:
                f(eng)

        @block.tensor
        def _(eng):
            for f in streams["t"]:
                f(eng)

        @block.vector
        def _(eng):
            for f in streams["v"]:
                f(eng)

        @block.scalar
        def _(eng):
            for f in streams["s"]:
                f(eng)

        @block.gpsimd
        def _(eng):
            for f in streams["g"]:
                f(eng)

    return nc


_CACHE = {}


def kernel(L0_attn_x, L0_attn_wq, L0_attn_wk, L0_attn_wv, L0_attn_wo,
           L0_ffn_Win, L0_ffn_bin_, L0_ffn_W1, L0_ffn_b1, L0_ffn_W2,
           L0_ffn_b2, L0_ffn_Wout, L0_ffn_bout):
    x = np.ascontiguousarray(np.asarray(L0_attn_x, dtype=np.float32))
    consts = _host_consts()
    if "nc" not in _CACHE:
        _CACHE["nc"] = build_nc()
    nc = _CACHE["nc"]
    shared = {
        "wq": np.ascontiguousarray(np.asarray(L0_attn_wq, dtype=np.float32)),
        "wk": np.ascontiguousarray(np.asarray(L0_attn_wk, dtype=np.float32)),
        "wv": np.ascontiguousarray(np.asarray(L0_attn_wv, dtype=np.float32)),
        "wo": np.ascontiguousarray(np.asarray(L0_attn_wo, dtype=np.float32)),
        "win": np.ascontiguousarray(np.asarray(L0_ffn_Win, dtype=np.float32)),
        "binp": np.ascontiguousarray(np.asarray(L0_ffn_bin_, dtype=np.float32).reshape(64, 1)),
        "w1": np.ascontiguousarray(np.asarray(L0_ffn_W1, dtype=np.float32)),
        "b1": np.ascontiguousarray(np.asarray(L0_ffn_b1, dtype=np.float32)),
        "w2": np.ascontiguousarray(np.asarray(L0_ffn_W2, dtype=np.float32)),
        "b2": np.ascontiguousarray(np.asarray(L0_ffn_b2, dtype=np.float32).reshape(64, 1)),
        "wout": np.ascontiguousarray(np.asarray(L0_ffn_Wout, dtype=np.float32)),
        "bout": np.ascontiguousarray(np.asarray(L0_ffn_bout, dtype=np.float32).reshape(1, D)),
    }
    shared.update(consts)
    in_maps = []
    for i in range(NCORE):
        m = dict(shared)
        m["x"] = np.ascontiguousarray(x[i])
        in_maps.append(m)
    res = run_bass_kernel_spmd(nc, in_maps, core_ids=list(range(NCORE)))
    out = np.stack([np.asarray(r["y"]) for r in res.results], axis=0)
    return out.astype(np.float32)


# version 31 (final)
# speedup vs baseline: 1.0283x; 1.0074x over previous
import os
import numpy as np
import ml_dtypes
import concourse.bass as bass
import concourse.mybir as mybir
from concourse.bass_utils import run_bass_kernel_spmd
from concourse.alu_op_type import AluOpType as A
from contextlib import ExitStack

F32 = mybir.dt.float32
BF16 = mybir.dt.bfloat16
AF = mybir.ActivationFunctionType

NCORE = 8
S_TOK = 2048
D = 1024
NTB = 16
NDC = 8
NQC = 4
NPAIR = 8
EPS = 1e-5
ESHIFT = -8.0


def _host_consts():
    ident = np.eye(128, dtype=np.float32)
    identb = np.eye(128, dtype=ml_dtypes.bfloat16)
    j = np.arange(64)
    p = np.arange(128)
    tria = np.zeros((128, 256), dtype=np.float32)
    tria[0:64, 0:128] = (p[None, :] > j[:, None]).astype(np.float32)
    tria[0:64, 128:256] = (p[None, :] > j[:, None] + 64).astype(np.float32)
    tria[64:128] = tria[0:64]
    negi = np.zeros((128, 256), dtype=np.float32)
    negi[j, j] = -1e30
    negi[j, 128 + 64 + j] = -1e30
    negi[64:128] = negi[0:64]
    fsel = np.zeros((128, 128), dtype=np.float32)
    fsel[0, 0:64] = 1.0
    fsel[32, 64:128] = 1.0
    fsel[64:128] = fsel[0:64]
    t = np.arange(S_TOK, dtype=np.float64)[:, None]
    f = 1000.0 ** (-np.arange(32, dtype=np.float64) / 32.0)[None, :]
    tf = t * f
    two_pi = 2.0 * np.pi

    def hilo(theta):
        y = np.mod(theta, two_pi) - np.pi
        hi = y.astype(np.float32)
        lo = (y - hi.astype(np.float64)).astype(np.float32)
        return hi, lo
    thc = np.concatenate([tf, tf], axis=1) + np.pi / 2.0
    ths = np.concatenate([-tf, tf], axis=1)
    yc, ycd = hilo(thc)
    ys, ysd = hilo(ths)
    return {
        "c_ident": ident, "c_identb": identb,
        "c_tria": tria.astype(ml_dtypes.bfloat16), "c_negi": negi.astype(ml_dtypes.bfloat16),
        "c_fsel": fsel,
        "c_yc": yc, "c_ycd": ycd, "c_ys": ys, "c_ysd": ysd,
    }


def build_nc():
    stage = int(os.environ.get("KSTAGE", "6"))
    nc = bass.Bass("TRN2", target_bir_lowering=False)
    xin = nc.dram_tensor("x", [S_TOK, D], F32, kind="ExternalInput").ap()
    wq = nc.dram_tensor("wq", [D, D], F32, kind="ExternalInput").ap()
    wk = nc.dram_tensor("wk", [D, 64], F32, kind="ExternalInput").ap()
    wv = nc.dram_tensor("wv", [D, 64], F32, kind="ExternalInput").ap()
    wo = nc.dram_tensor("wo", [D, D], F32, kind="ExternalInput").ap()
    win = nc.dram_tensor("win", [D, 64], F32, kind="ExternalInput").ap()
    binp = nc.dram_tensor("binp", [64, 1], F32, kind="ExternalInput").ap()
    w1 = nc.dram_tensor("w1", [64, 16], F32, kind="ExternalInput").ap()
    b1 = nc.dram_tensor("b1", [64, 16], F32, kind="ExternalInput").ap()
    w2 = nc.dram_tensor("w2", [64, 16], F32, kind="ExternalInput").ap()
    b2 = nc.dram_tensor("b2", [64, 1], F32, kind="ExternalInput").ap()
    wout = nc.dram_tensor("wout", [64, D], F32, kind="ExternalInput").ap()
    bout = nc.dram_tensor("bout", [1, D], F32, kind="ExternalInput").ap()
    c_ident = nc.dram_tensor("c_ident", [128, 128], F32, kind="ExternalInput").ap()
    c_identb = nc.dram_tensor("c_identb", [128, 128], BF16, kind="ExternalInput").ap()
    c_tria = nc.dram_tensor("c_tria", [128, 256], BF16, kind="ExternalInput").ap()
    c_negi = nc.dram_tensor("c_negi", [128, 256], BF16, kind="ExternalInput").ap()
    c_fsel = nc.dram_tensor("c_fsel", [128, 128], F32, kind="ExternalInput").ap()
    c_yc = nc.dram_tensor("c_yc", [S_TOK, 64], F32, kind="ExternalInput").ap()
    c_ycd = nc.dram_tensor("c_ycd", [S_TOK, 64], F32, kind="ExternalInput").ap()
    c_ys = nc.dram_tensor("c_ys", [S_TOK, 64], F32, kind="ExternalInput").ap()
    c_ysd = nc.dram_tensor("c_ysd", [S_TOK, 64], F32, kind="ExternalInput").ap()
    yout = nc.dram_tensor("y", [S_TOK, D], F32, kind="ExternalOutput").ap()

    with ExitStack() as st:
        def sb(name, shape, dt=F32):
            return st.enter_context(nc.sbuf_tensor(name, shape, dt))

        x_sb = sb("x_sb", [128, NTB * 1024])
        bufA = sb("bufA", [128, 16384], BF16)
        qfm = sb("qfm", [128, 16384], BF16)
        wT = sb("wT", [128, 8192], BF16)
        h2fm = sb("h2fm", [128, 4096], BF16)
        attnq = sb("attnq", [128, 4096], BF16)
        kvd = sb("kvd", [128, 2048], BF16)
        v1 = sb("v1", [128, 16 * 65], BF16)
        wstage = sb("wstage", [128, 512])
        tmroll = sb("tmroll", [128, 1024])
        qtm = sb("qtm", [128, 1024], BF16)
        qrope = sb("qrope", [128, 2048], BF16)
        cs_y = sb("cs_y", [128, 2048], BF16)
        sums = sb("sums", [128, 2048])
        lnbuf = sb("lnbuf", [128, 1024], BF16)
        f_sb = sb("f_sb", [128, 1024], BF16)
        kvw = sb("kvw", [128, 1024], BF16)
        woutd = sb("woutd", [128, 1024], BF16)
        boutb = sb("boutb", [128, 1024], BF16)
        w1d = sb("w1d", [128, 16])
        b1pd = sb("b1pd", [128, 16])
        w2d = sb("w2d", [128, 16])
        b2d = sb("b2d", [128, 1])
        stats = sb("stats", [128, 192])
        bind = stats[:, 174:175]
        b1d = stats[:, 176:192]
        win_sb = kvw[:, 0:512]
        mv = sb("mv", [128, 32])
        nm1 = sb("nm1", [128, 16])
        iv1 = sb("iv1", [128, 16])
        vep = sb("vep", [128, 16])
        nm2 = sb("nm2", [128, 4]); iv2 = sb("iv2", [128, 4])
        nm3 = sb("nm3", [128, 4]); iv3 = sb("iv3", [128, 4])
        nm4 = sb("nm4", [128, 4]); iv4 = sb("iv4", [128, 4])
        neghalf = sb("neghalf", [128, 16])
        negone = sb("negone", [128, 1])
        expb = sb("expb", [128, 1])
        ones1 = sb("ones1", [1, 128])
        zerob = sb("zerob", [128, 1])
        junk = sb("junk", [128, 16])
        junkg = sb("junkg", [128, 16])
        id_sb = sb("id_sb", [128, 128])
        idb_sb = sb("idb_sb", [128, 128], BF16)
        tria_sb = sb("tria_sb", [128, 256], BF16)
        negi_sb = sb("negi_sb", [128, 256], BF16)
        fsel_sb = sb("fsel_sb", [128, 128])

        ps_st = st.enter_context(nc.psum_tensor("ps_st", [128, 2048], F32))
        ps_pv = st.enter_context(nc.psum_tensor("ps_pv", [128, 1024], F32))
        ps_w = st.enter_context(nc.psum_tensor("ps_w", [128, 1024], F32))

        s_sbv = qtm[:, 0:256]
        hbuf = [qtm[:, 256:512], qtm[:, 512:768]]
        gb = qtm[:, 768:1024]
        gsl = [qrope[:, :].bitcast(F32)[:, 0:256], qrope[:, :].bitcast(F32)[:, 256:512]]

        tmbf = tmroll[:, :].bitcast(BF16)
        qfm_f32 = qfm[:, :].bitcast(F32)
        psst_bf = ps_st[:, :].bitcast(BF16)
        psw_bf = ps_w[:, :].bitcast(BF16)
        pspv_bf = ps_pv[:, :].bitcast(BF16)
        wstage_bf = wstage[:, :].bitcast(BF16)

        sems = {}
        for s in ("dX", "dW", "dO", "dQ", "T", "V", "S", "G"):
            sems[s] = [st.enter_context(nc.semaphore(s)), 0]
        streams = {k: [] for k in ("t", "v", "s", "g", "y")}

        def op(eng, build, inc=None, inck=1):
            if inc is not None:
                sems[inc][1] += inck
            val = sems[inc][1] if inc is not None else None

            def f(e, build=build, inc=inc, inck=inck):
                ins = build(e)
                if inc is not None:
                    ins.then_inc(sems[inc][0], inck)
            streams[eng].append(f)
            return val

        def wt(eng, sem, val):
            if val is None or val <= 0:
                return
            streams[eng].append(lambda e, s=sem, v=val: e.wait_ge(sems[s][0], v))

        def vsync(val):
            wt("v", "V", val)

        def gsync(val):
            wt("g", "G", val)

        def ssync(val):
            wt("s", "S", val)

        _fr = [0, 0]

        def vfence(inc=None):
            _fr[0] += 1
            c = _fr[0] % 16
            return op("v", lambda e, c=c: e.memset(junk[:, c:c + 1], 0.0), inc=inc)

        def gfence(inc=None):
            _fr[1] += 1
            c = _fr[1] % 16
            return op("g", lambda e, c=c: e.memset(junkg[:, c:c + 1], 0.0), inc=inc)

        _done = [False]

        def debug_finish():
            if _done[0]:
                return
            _done[0] = True
            for semname in ("T", "V", "S", "G"):
                v = sems[semname][1]
                if v > 0:
                    streams["y"].append(lambda e, sn=semname, v=v: e.wait_ge(sems[sn][0], v))
            op("y", lambda e: e.dma_start(
                out=yout.rearrange("(t p) d -> p t d", p=128),
                in_=x_sb[:, :].rearrange("p (t d) -> p t d", t=NTB, d=1024)),
                inc="dO", inck=16)

        op("v", lambda e: e.memset(expb[:, :], ESHIFT))
        op("v", lambda e: e.memset(zerob[:, :], 0.0))
        op("v", lambda e: e.memset(neghalf[:, :], -0.5))
        m_negone = op("v", lambda e: e.memset(negone[:, :], -1.0), inc="V")
        op("v", lambda e: e.memset(sums[:, :], 1.0))
        m_ones = op("v", lambda e: e.memset(
            v1[:, :].rearrange("p (t c) -> p t c", t=16, c=65)[:, :, 64:65], 1.0), inc="V")

        def dma(build):
            return op("y", build, inc="dW", inck=16)

        m_xg = [None] * 4

        dma(lambda e: e.dma_start(out=id_sb[:, :], in_=c_ident[:, :]))
        m_idB = dma(lambda e: e.dma_start(out=idb_sb[:, :], in_=c_identb[:, :]))
        streams["y"].append(lambda e, v=m_idB: e.wait_ge(sems["dW"][0], v))
        dma(lambda e: e.dma_start(out=tria_sb[:, :], in_=c_tria[:, :]))
        dma(lambda e: e.dma_start(out=negi_sb[:, :], in_=c_negi[:, :]))
        dma(lambda e: e.dma_start(out=fsel_sb[:, :], in_=c_fsel[:, :]))
        dma(lambda e: e.dma_start(out=w1d[0:64, :], in_=w1[:, :]))
        dma(lambda e: e.dma_start(out=b1d[0:64, :], in_=b1[:, :]))
        dma(lambda e: e.dma_start(out=w2d[0:64, :], in_=w2[:, :]))
        dma(lambda e: e.dma_start(out=b2d[0:64, :], in_=b2[:, :]))
        m_small = dma(lambda e: e.dma_start(out=bind[0:64, :], in_=binp[:, :]))

        def tab_view(tab, half=None):
            v = tab.rearrange("(t p) j -> p t j", p=128)
            if half is None:
                return v
            return v[:, half * 8:(half + 1) * 8, :]

        m_tabL = None
        for i, tab in enumerate((c_yc, c_ycd, c_ys, c_ysd)):
            m_tabL = dma(lambda e, tab=tab, i=i: e.dma_start(
                out=qfm_f32[:, i * 1024:(i + 1) * 1024].rearrange("p (t j) -> p t j", t=16, j=64),
                in_=tab_view(tab)))
        streams["y"].append(lambda e, v=m_tabL: e.wait_ge(sems["dW"][0], v))
        wq_r = wq.rearrange("(ic r) d -> r ic d", r=128)
        m_wqL = [None]
        for h2 in range(2):
            for eo in range(2):
                src = wq_r[h2 * 64 + eo:h2 * 64 + 64:2, :, :]
                dst = qfm[h2 * 64 + eo * 32:h2 * 64 + eo * 32 + 32, 8192:16384].rearrange(
                    "p (ic d) -> p ic d", ic=8)
                m_wqL[0] = op("g", lambda e, src=src, dst=dst: e.dma_start(out=dst, in_=src),
                              inc="dQ", inck=16)
        streams["g"].append(lambda e, v=m_wqL[0]: e.wait_ge(sems["dQ"][0], v))
        for g in range(4):
            m_xg[g] = op("g", lambda e, g=g: e.dma_start(
                out=x_sb[:, g * 4096:(g + 1) * 4096].rearrange("p (t d) -> p t d", t=4, d=1024),
                in_=xin.rearrange("(t p) d -> p t d", p=128)[:, g * 4:(g + 1) * 4, :]),
                inc="dX", inck=16)
            streams["g"].append(lambda e, v=m_xg[g]: e.wait_ge(sems["dX"][0], v))
        m_x = m_xg[3]
        def emit_sin(hi_col, lo_col, dstcol, prev_gate):
            if prev_gate is not None:
                ssync(prev_gate)
                wt("v", "S", prev_gate)
            wt("v", "dW", m_tabL)
            m_add = op("v", lambda e, hi_col=hi_col, lo_col=lo_col: e.tensor_tensor(
                out=tmroll[:, :], in0=qfm_f32[:, hi_col:hi_col + 1024],
                in1=qfm_f32[:, lo_col:lo_col + 1024], op=A.add), inc="V")
            wt("s", "V", m_add)
            return op("s", lambda e, dstcol=dstcol: e.activation(
                out=cs_y[:, dstcol:dstcol + 1024], in_=tmroll[:, :],
                func=AF.Sin, bias=zerob[:, 0:1], scale=-1.0), inc="S")

        m_sin1 = emit_sin(0, 1024, 0, None)
        m_sin2 = emit_sin(2048, 3072, 1024, m_sin1)

        m_wkL = dma(lambda e: e.dma_start(
            out=wstage[:, :].rearrange("p (c h) -> p c h", c=8, h=64),
            in_=wk.rearrange("(c p) h -> p c h", p=128)))
        wt("v", "dW", m_wkL)
        wsv = wstage[:, :].rearrange("p (c e) -> p c e", c=512, e=1)
        m_wkC = None
        for dc in range(NDC):
            op("v", lambda e, dc=dc: e.tensor_copy(
                out=kvw[:, dc * 128 + 0:dc * 128 + 32],
                in_=wsv[:, dc * 64 + 0:dc * 64 + 64:2, 0]))
            m_wkC = op("v", lambda e, dc=dc: e.tensor_copy(
                out=kvw[:, dc * 128 + 32:dc * 128 + 64],
                in_=wsv[:, dc * 64 + 1:dc * 64 + 64:2, 0]), inc="V")
        streams["y"].append(lambda e, v=m_wkC: e.wait_ge(sems["V"][0], v))
        m_wvL = dma(lambda e: e.dma_start(
            out=wstage[:, :].rearrange("p (c h) -> p c h", c=8, h=64),
            in_=wv.rearrange("(c p) h -> p c h", p=128)))
        wt("v", "dW", m_wvL)
        m_kvw = None
        for dc in range(NDC):
            m_kvw = op("v", lambda e, dc=dc: e.tensor_copy(
                out=kvw[:, dc * 128 + 64:dc * 128 + 128],
                in_=wstage[:, dc * 64:dc * 64 + 64]), inc="V")
        streams["y"].append(lambda e, v=m_kvw: e.wait_ge(sems["V"][0], v))
        m_wout = None
        for half in range(2):
            m_woutL = dma(lambda e, half=half: e.dma_start(
                out=wstage[0:64, :], in_=wout[:, half * 512:(half + 1) * 512]))
            wt("v", "dW", m_woutL)
            op("v", lambda e, half=half: e.tensor_copy(
                out=woutd[0:64, half * 512:(half + 1) * 512], in_=wstage[0:64, :]))
            m_wout = op("v", lambda e, half=half: e.tensor_copy(
                out=woutd[64:128, half * 512:(half + 1) * 512], in_=wstage[0:64, :]), inc="V")
            streams["y"].append(lambda e, v=m_wout: e.wait_ge(sems["V"][0], v))
        m_ones1 = op("v", lambda e: e.memset(ones1[:, :], 1.0), inc="V")
        m_boutc = None
        m_bmm = None
        for half in range(2):
            m_boutL = dma(lambda e, half=half: e.dma_start(
                out=wstage[0:1, :], in_=bout[:, half * 512:(half + 1) * 512]))
            wt("t", "dW", m_boutL)
            wt("t", "V", m_ones1)
            if half == 1:
                wt("t", "V", m_boutc)
            m_bmm = op("t", lambda e: e.matmul(
                ps_w[:, 0:512], ones1[0:1, :], wstage[0:1, :], start=True, stop=True), inc="T")
            wt("v", "T", m_bmm)
            m_boutc = op("v", lambda e, half=half: e.tensor_copy(
                out=boutb[:, half * 512:(half + 1) * 512], in_=ps_w[:, 0:512]), inc="V")
            streams["y"].append(lambda e, v=m_boutc: e.wait_ge(sems["V"][0], v))
        op("v", lambda e: e.tensor_copy(out=w1d[64:128, :], in_=w1d[0:64, :]))
        op("v", lambda e: e.tensor_copy(out=b1d[64:128, :], in_=b1d[0:64, :]))
        op("v", lambda e: e.tensor_copy(out=w2d[64:128, :], in_=w2d[0:64, :]))
        op("v", lambda e: e.tensor_copy(out=b2d[64:128, :], in_=b2d[0:64, :]))
        m_dups = op("v", lambda e: e.tensor_copy(out=bind[64:128, :], in_=bind[0:64, :]), inc="V")
        vsync(m_dups)
        m_ffnw = op("v", lambda e: e.scalar_tensor_tensor(
            out=b1pd[:, :], in0=w1d[:, :], scalar=bind[:, 0:1], in1=b1d[:, :],
            op0=A.mult, op1=A.add), inc="V")

        m_ln1g = [None] * 4
        for g in range(4):
            wt("v", "dX", m_xg[g])
            if g == 0:
                vsync(m_ffnw)
            m_st1 = None
            for tb in range(4 * g, 4 * g + 4):
                op("v", lambda e, tb=tb: e.bn_stats(out=stats[:, tb * 12:tb * 12 + 6],
                                                    in_=x_sb[:, tb * 1024:tb * 1024 + 512]))
                m_st1 = op("v", lambda e, tb=tb: e.bn_stats(out=stats[:, tb * 12 + 6:tb * 12 + 12],
                                                            in_=x_sb[:, tb * 1024 + 512:tb * 1024 + 1024]),
                           inc="V")
            vsync(m_st1)
            m_ag1 = None
            for tb in range(4 * g, 4 * g + 4):
                m_ag1 = op("v", lambda e, tb=tb: e.bn_aggr(out=mv[:, tb * 2:tb * 2 + 2],
                                                           in_=stats[:, tb * 12:tb * 12 + 12]), inc="V")
            vsync(m_ag1)
            op("v", lambda e, g=g: e.tensor_scalar(
                out=nm1[:, 4 * g:4 * g + 4], in0=mv[:, 8 * g:8 * g + 8:2],
                scalar1=-1.0, scalar2=None, op0=A.mult))
            m_ln1v = op("v", lambda e, g=g: e.tensor_scalar(
                out=vep[:, 4 * g:4 * g + 4], in0=mv[:, 8 * g + 1:8 * g + 8:2],
                scalar1=EPS, scalar2=None, op0=A.add), inc="V")
            wt("g", "V", m_ln1v)
            m_ln1g[g] = op("g", lambda e, g=g: e.tensor_tensor(
                out=iv1[:, 4 * g:4 * g + 4], in0=vep[:, 4 * g:4 * g + 4],
                in1=neghalf[:, 4 * g:4 * g + 4], op=A.pow), inc="G")

        m_wqT = [None] * NDC
        m_wqE = [None] * NDC
        for ic in range(NDC):
            wt("t", "dQ", m_wqL[0])
            if ic == 0:
                wt("t", "dW", m_idB)
            if ic >= 1:
                wt("t", "S", m_wqE[ic - 1])
            for dc in range(NDC):
                def tr(e, ic=ic, dc=dc):
                    return e.transpose(
                        psst_bf[:, 2048 + dc * 128:2048 + dc * 128 + 128],
                        qfm[:, 8192 + ic * 1024 + dc * 128:8192 + ic * 1024 + (dc + 1) * 128],
                        idb_sb[:, :])
                if dc == NDC - 1:
                    m_wqT[ic] = op("t", tr, inc="T")
                else:
                    op("t", tr)
            wt("s", "T", m_wqT[ic])
            m_wqE[ic] = op("s", lambda e, ic=ic: e.activation(
                out=wT[:, :].rearrange("p (dc i) -> p dc i", dc=NDC, i=1024)[:, :, ic * 128:ic * 128 + 128],
                in_=psst_bf[:, 2048:3072].rearrange("p (s c) -> p s c", s=8, c=128),
                func=AF.Copy, bias=0.0), inc="S")

        m_qp = [None] * NTB
        m_qtmE = [None] * NTB
        m_qT = [None] * NTB
        m_qfm = [None] * NTB
        m_rope = [None] * NTB

        def emit_qproj(tb):
            par = tb % 2
            wt("t", "S", m_h1fm[tb])
            if tb == 1:
                wt("t", "S", m_wqE[NDC - 1])
            if tb >= 2:
                wt("t", "V", m_qtmE[tb - 2])
            for dc in range(NDC):
                for ih in range(2):
                    def mm(e, tb=tb, dc=dc, ih=ih, par=par):
                        out = ps_st[:, par * 1024 + ih * 512:par * 1024 + (ih + 1) * 512]
                        return e.matmul(out,
                                        bufA[:, dc * 2048 + tb * 128:dc * 2048 + tb * 128 + 128],
                                        wT[:, dc * 1024 + ih * 512:dc * 1024 + (ih + 1) * 512],
                                        start=(dc == 0), stop=(dc == NDC - 1))
                    if dc == NDC - 1 and ih == 1:
                        m_qp[tb] = op("t", mm, inc="T")
                    else:
                        op("t", mm)

        def emit_qrope(tb):
            par = tb % 2
            csc = cs_y[:, tb * 64:(tb + 1) * 64]
            css = cs_y[:, 1024 + tb * 64:1024 + (tb + 1) * 64]
            wt("v", "T", m_qp[tb])
            psq = ps_st[:, par * 1024:par * 1024 + 1024]
            if tb >= 1:
                vsync(m_rope[tb - 1])
            if tb >= 2 and m_qT[tb - 2] is not None:
                wt("v", "T", m_qT[tb - 2])
            op("v", lambda e, css=css, psq=psq: e.tensor_tensor(
                out=qtm[:, :].rearrange("p (h a j) -> p h a j", h=16, a=2, j=32),
                in0=psq.rearrange("p (h a j) -> p h a j", h=16, a=2, j=32)[:, :, ::-1, :],
                in1=css.rearrange("p (a j) -> p a j", a=2, j=32).unsqueeze(1).broadcast_to((128, 16, 2, 32)),
                op=A.mult))
            m_qtmE[tb] = op("v", lambda e, csc=csc, psq=psq, par=par: e.tensor_tensor(
                out=qrope[:, par * 1024:par * 1024 + 1024].rearrange("p (h f) -> p h f", h=16, f=64),
                in0=psq.rearrange("p (h f) -> p h f", h=16, f=64),
                in1=csc.unsqueeze(1).broadcast_to((128, 16, 64)), op=A.mult), inc="V")
            vsync(m_qtmE[tb])
            m_rope[tb] = op("v", lambda e, par=par: e.tensor_tensor(
                out=qrope[:, par * 1024:par * 1024 + 1024],
                in0=qrope[:, par * 1024:par * 1024 + 1024],
                in1=qtm[:, :], op=A.add), inc="V")

        def emit_qT(tb):
            par = tb % 2
            pso = par
            wt("t", "V", m_rope[tb])
            if tb < 2:
                wt("t", "V", m_boutc)
            else:
                wt("t", "S", m_qfm[tb - 2])
            for pr in range(NPAIR):
                def tr(e, pr=pr, par=par, pso=pso):
                    return e.transpose(psw_bf[:, pso * 1024 + pr * 128:pso * 1024 + pr * 128 + 128],
                                       qrope[:, par * 1024 + pr * 128:par * 1024 + (pr + 1) * 128], idb_sb[:, :])
                if pr == NPAIR - 1:
                    m_qT[tb] = op("t", tr, inc="T")
                else:
                    op("t", tr)
            wt("s", "T", m_qT[tb])
            m_qfm[tb] = op("s", lambda e, tb=tb, pso=pso: e.activation(
                out=qfm[:, :].rearrange("p (pr t) -> p pr t", pr=NPAIR, t=2048)[:, :, tb * 128:tb * 128 + 128],
                in_=psw_bf[:, pso * 1024:pso * 1024 + 1024].rearrange("p (pr c) -> p pr c", pr=NPAIR, c=128),
                func=AF.Copy, bias=0.0), inc="S")

        m_h1T = [None] * NTB
        m_h1fm = [None] * NTB
        wt("g", "S", m_sin2)
        for tb in range(NTB):
            slot = tb % 2
            if tb % 4 == 0:
                gsync(m_ln1g[tb // 4])
            if tb >= 2:
                wt("g", "T", m_h1T[tb - 2])
            m_n1 = op("g", lambda e, tb=tb, slot=slot: e.tensor_scalar(
                out=tmbf[:, slot * 1024:slot * 1024 + 1024],
                in0=x_sb[:, tb * 1024:(tb + 1) * 1024],
                scalar1=nm1[:, tb:tb + 1], scalar2=iv1[:, tb:tb + 1],
                op0=A.add, op1=A.mult), inc="G")
            wt("t", "G", m_n1)
            if tb >= 2:
                wt("t", "S", m_h1fm[tb - 2])
            for dc in range(NDC):
                def tr(e, dc=dc, slot=slot):
                    return e.transpose(
                        pspv_bf[:, slot * 1024 + dc * 128:slot * 1024 + dc * 128 + 128],
                        tmbf[:, slot * 1024 + dc * 128:slot * 1024 + dc * 128 + 128],
                        idb_sb[:, :])
                if dc == NDC - 1:
                    m_h1T[tb] = op("t", tr, inc="T")
                else:
                    op("t", tr)
            wt("s", "T", m_h1T[tb])
            m_h1fm[tb] = op("s", lambda e, tb=tb, slot=slot: e.activation(
                out=bufA[:, :].rearrange("p (dc t) -> p dc t", dc=NDC, t=2048)[:, :, tb * 128:tb * 128 + 128],
                in_=pspv_bf[:, slot * 1024:slot * 1024 + 1024].rearrange("p (dc c) -> p dc c", dc=NDC, c=128),
                func=AF.Copy, bias=0.0), inc="S")
            if tb >= 1:
                if tb >= 3:
                    emit_qT(tb - 3)
                emit_qproj(tb - 1)
                emit_qrope(tb - 1)
            if tb == NTB - 1:
                emit_qT(tb - 2)
                emit_qproj(tb)
                emit_qrope(tb)
                emit_qT(tb - 1)
                emit_qT(tb)

        if stage <= 1:
            debug_finish()
        m_kvT = [None] * 4
        m_kvE = [None] * 4
        for tc in range(4):
            if tc == 0:
                wt("t", "S", m_qfm[NTB - 1])
            wt("t", "V", m_kvw)
            wt("t", "S", m_h1fm[4 * tc + 3])
            if tc >= 1:
                wt("t", "S", m_kvE[tc - 1])
            for dc in range(NDC):
                def mm(e, tc=tc, dc=dc):
                    return e.matmul(ps_w[:, 0:512], kvw[:, dc * 128:(dc + 1) * 128],
                                    bufA[:, dc * 2048 + tc * 512:dc * 2048 + (tc + 1) * 512],
                                    start=(dc == 0), stop=(dc == NDC - 1))
                if dc == NDC - 1:
                    m_kvT[tc] = op("t", mm, inc="T")
                else:
                    op("t", mm)
            wt("s", "T", m_kvT[tc])
            m_kvE[tc] = op("s", lambda e, tc=tc: e.activation(
                out=h2fm[:, tc * 512:(tc + 1) * 512], in_=ps_w[:, 0:512],
                func=AF.Copy, bias=0.0), inc="S")


        streams["y"].append(lambda e, v=m_kvT[3]: e.wait_ge(sems["T"][0], v))
        m_winL = dma(lambda e: e.dma_start(
            out=wstage[:, :].rearrange("p (c h) -> p c h", c=8, h=64),
            in_=win.rearrange("(c p) h -> p c h", p=128)))
        wt("v", "dW", m_winL)
        wt("v", "T", m_kvT[3])
        m_win = op("v", lambda e: e.tensor_copy(out=win_sb, in_=wstage[:, :]), inc="V")

        m_vT = [None] * 2
        m_vE = [None] * 2
        for gs in range(2):
            wt("t", "S", m_kvE[3])
            if gs == 1:
                wt("t", "S", m_vE[0])
            for i in range(8):
                tb = gs * 8 + i
                def tr(e, tb=tb, i=i):
                    return e.transpose(psw_bf[:, i * 128:i * 128 + 64],
                                       h2fm[64:128, tb * 128:(tb + 1) * 128],
                                       idb_sb[64:128, 64:128])
                if i == 7:
                    m_vT[gs] = op("t", tr, inc="T")
                else:
                    op("t", tr)
            wt("s", "T", m_vT[gs])
            m_vE[gs] = op("s", lambda e, gs=gs: e.activation(
                out=v1[:, :].rearrange("p (t c) -> p t c", t=16, c=65)[:, gs * 8:gs * 8 + 8, 0:64],
                in_=psw_bf[:, 0:1024].rearrange("p (i c) -> p i c", i=8, c=128)[:, :, 0:64],
                func=AF.Copy, bias=0.0), inc="S")

        m_kT = [None] * 2
        m_ktm = [None] * 2
        for gs in range(2):
            wt("t", "S", m_vE[gs] if gs == 0 else m_ktm[0])
            if gs == 0:
                wt("t", "S", m_vE[1])
            for i in range(8):
                tb = gs * 8 + i
                def tr(e, tb=tb, i=i):
                    return e.transpose(psw_bf[:, i * 128:i * 128 + 64],
                                       h2fm[0:64, tb * 128:(tb + 1) * 128],
                                       idb_sb[0:64, 0:64])
                if i == 7:
                    m_kT[gs] = op("t", tr, inc="T")
                else:
                    op("t", tr)
            wt("s", "T", m_kT[gs])
            if gs == 0:
                wt("s", "V", m_rope[NTB - 1])
            m_ktm[gs] = op("s", lambda e, gs=gs: e.activation(
                out=qtm[:, gs * 512:gs * 512 + 512].rearrange("p (t c) -> p t c", t=8, c=64),
                in_=psw_bf[:, 0:1024].rearrange("p (i c) -> p i c", i=8, c=128)[:, :, 0:64],
                func=AF.Copy, bias=0.0), inc="S")

        m_krope = [None] * 2
        m_kback = [None] * NTB
        m_kvdE = {}
        m_kbackL = None
        for gs in range(2):
            wt("v", "S", m_ktm[gs])
            csc = cs_y[:, gs * 512:(gs + 1) * 512]
            css = cs_y[:, 1024 + gs * 512:1024 + (gs + 1) * 512]
            ktg = qtm[:, gs * 512:(gs + 1) * 512]
            m_sw = op("v", lambda e, ktg=ktg, css=css, gs=gs: e.tensor_tensor(
                out=tmbf[:, 1024 + gs * 512:1024 + gs * 512 + 512].rearrange(
                    "p (t a j) -> p t a j", t=8, a=2, j=32),
                in0=ktg.rearrange("p (t a j) -> p t a j", t=8, a=2, j=32)[:, :, ::-1, :],
                in1=css.rearrange("p (t a j) -> p t a j", t=8, a=2, j=32), op=A.mult), inc="V")
            vsync(m_sw)
            m_kc = op("v", lambda e, ktg=ktg, csc=csc: e.tensor_tensor(
                out=ktg, in0=ktg, in1=csc, op=A.mult), inc="V")
            vsync(m_kc)
            m_krope[gs] = op("v", lambda e, ktg=ktg, gs=gs: e.tensor_tensor(
                out=ktg, in0=ktg,
                in1=tmbf[:, 1024 + gs * 512:1024 + gs * 512 + 512], op=A.add), inc="V")
        for tb in range(NTB):
            if tb % 4 == 0:
                wt("t", "V", m_krope[(tb + 3) // 8])
                if tb == 0:
                    wt("t", "S", m_ktm[1])
                if tb >= 4:
                    wt("t", "S", m_kvdE.get(tb - 4))
            m_kback[tb] = op("t", lambda e, tb=tb: e.transpose(
                psw_bf[0:64, (tb % 4) * 128:(tb % 4) * 128 + 128],
                qtm[:, tb * 64:tb * 64 + 64], idb_sb[:, :]), inc="T")
            if tb % 4 == 3:
                wt("s", "T", m_kback[tb])
                mm_ = op("s", lambda e, tb=tb: e.activation(
                    out=kvd[0:64, (tb - 3) * 128:(tb + 1) * 128],
                    in_=psw_bf[0:64, 0:512], func=AF.Copy, bias=0.0), inc="S")
                for q_ in range(4):
                    m_kvdE[tb - q_] = mm_
        m_kbackL = m_kback[NTB - 1]
        wt("s", "S", m_kvdE[15])
        m_kdup = op("s", lambda e: e.activation(
            out=kvd[64:128, :], in_=kvd[0:64, :], func=AF.Copy, bias=0.0), inc="S")

        if stage <= 2:
            debug_finish()


        m_woTh = {}
        m_woT = [None] * NDC
        m_woE = [None] * NDC

        def emit_woT(ics):
            if ics[0] == 0:
                streams["g"].append(lambda e, v=m_qp[NTB - 1]: e.wait_ge(sems["T"][0], v))
                streams["g"].append(lambda e, v=m_win: e.wait_ge(sems["V"][0], v))
            for ic in ics:
                for cg in range(2):
                    ch = ic * 2 + cg
                    sl = ch % 2
                    prev2 = m_woTh.get((ic, cg - 2)) or m_woTh.get((ic - 1, cg))
                    if prev2 is not None:
                        streams["g"].append(lambda e, v=prev2: e.wait_ge(sems["T"][0], v))
                    m_load = op("g", lambda e, ic=ic, cg=cg, sl=sl: e.dma_start(
                        out=wstage_bf[:, sl * 512:sl * 512 + 512],
                        in_=wo[ic * 128:(ic + 1) * 128, cg * 512:(cg + 1) * 512]),
                        inc="dQ", inck=16)
                    wt("t", "dQ", m_load + (16 if ch % 2 == 0 else 0))
                    if ic == 0 and cg == 0:
                        wt("t", "S", m_kdup)
                    if ic >= 1 and cg == 0:
                        wt("t", "V", m_woE[ic - 1])
                    for j in range(4):
                        def tr(e, j=j, cg=cg, sl=sl):
                            return e.transpose(
                                psw_bf[:, cg * 512 + j * 128:cg * 512 + j * 128 + 128],
                                wstage_bf[:, sl * 512 + j * 128:sl * 512 + (j + 1) * 128], idb_sb[:, :])
                        if j == 3:
                            m_woTh[(ic, cg)] = op("t", tr, inc="T")
                        else:
                            op("t", tr)
                m_woT[ic] = m_woTh[(ic, 1)]
                wt("v", "T", m_woT[ic])
                if ic == 0:
                    wt("v", "S", m_qfm[NTB - 1])
                m_woE[ic] = op("v", lambda e, ic=ic: e.tensor_copy(
                    out=wT[:, :].rearrange("p (dc i) -> p dc i", dc=NDC, i=1024)[:, :, ic * 128:ic * 128 + 128],
                    in_=psw_bf[:, 0:1024].rearrange("p (s c) -> p s c", s=8, c=128)), inc="V")

        emit_woT(list(range(NDC)))
        strickle = []
        dqq = []

        m_ep = [[None] * NPAIR for _ in range(NQC)]
        m_aq = [[None] * NPAIR for _ in range(NQC)]
        m_bc = [[None] * NPAIR for _ in range(NQC)]
        m_nm = [[None] * NPAIR for _ in range(NQC)]
        m_recip = [None] * NQC
        m_wo = {}
        m_yev = {}
        m_nrm = {}
        m_add2 = {}
        m_add4 = {}
        m_n3 = {}
        m_h2T = {}
        m_h2E = {}
        m_fe = {}
        m_gb = [None] * NQC
        m_fpl = [None] * NQC
        m_accL = [None]
        exp_hist = []
        pv_km = {}
        exp_m = {}

        def emit_scores(qc, pair, kb):
            sub = kb % 2
            di = kb - 4 * qc
            col0 = 128 * di if di >= 0 else 0
            if qc == 0 and pair == 0 and kb < 2:
                wt("t", "S", m_qfm[NTB - 1])
                wt("t", "V", m_qtmE[NTB - 1])
                wt("t", "S", m_kdup)
                wt("t", "V", m_ones)
            if qc >= 2 and pair == 0 and kb == 0:
                wt("t", "S", m_qfm[4 * qc + 3])
            if len(exp_hist) >= 2:
                wt("t", "S", exp_hist[-2])
            mkey = None
            for hb in (0, 64):
                base = (0 if hb == 0 else 1024) + sub * 512

                def mm(e, qc=qc, pair=pair, kb=kb, hb=hb, base=base, col0=col0, di=di):
                    return e.matmul(
                        ps_st[:, base + col0:base + 512],
                        kvd[hb:hb + 64, kb * 128:(kb + 1) * 128],
                        qfm[hb:hb + 64, pair * 2048 + qc * 512 + col0:pair * 2048 + (qc + 1) * 512],
                        start=True, stop=(di < 0), tile_position=(hb, 0))
                if di < 0 and hb == 64:
                    mkey = op("t", mm, inc="T")
                else:
                    op("t", mm)
                if di >= 0:
                    op("t", lambda e, hb=hb, base=base, col0=col0: e.matmul(
                        ps_st[:, base + col0:base + col0 + 128],
                        tria_sb[hb:hb + 64, 0:128], negi_sb[hb:hb + 64, 0:128],
                        start=False, stop=False, tile_position=(hb, 0)))

                    def tr2(e, hb=hb, base=base, col0=col0):
                        return e.matmul(
                            ps_st[:, base + col0:base + col0 + 128],
                            tria_sb[hb:hb + 64, 128:256], negi_sb[hb:hb + 64, 128:256],
                            start=False, stop=True, tile_position=(hb, 0))
                    if hb == 64:
                        mkey = op("t", tr2, inc="T")
                    else:
                        op("t", tr2)
            return mkey

        def emit_exp(qc, pair, kb, m_scores):
            sub = kb % 2
            di = kb - 4 * qc
            col0 = 128 * di if di >= 0 else 0
            HS = 2048 if qc == 0 else 8192
            PT = h2fm if qc == 0 else bufA
            wt("s", "T", m_scores)
            if qc == 0 and pair == 0 and kb == 0:
                wt("s", "T", m_kT[1])
            if pair > 0:
                wt("s", "T", pv_km[(qc, pair - 1, kb)])
            elif qc > 0:
                prev_last = 4 * (qc - 1) + 3
                wt("s", "T", pv_km[(qc - 1, 7, min(kb, prev_last))])
            if qc == 1 and pair == 0 and kb == 0:
                wt("s", "T", m_qp[NTB - 1])
            in_ap = ps_st[:, :].rearrange("p (h c) -> p h c", h=2, c=1024)[:, :, sub * 512 + col0:sub * 512 + 512]
            out_ap = PT[:, :].rearrange("p (h k) -> p h k", h=2, k=HS)[:, :, kb * 512 + col0:kb * 512 + 512]
            m = op("s", lambda e, in_ap=in_ap, out_ap=out_ap: e.activation(
                out=out_ap, in_=in_ap, func=AF.Exp, bias=expb[:, 0:1], scale=0.125), inc="S")
            exp_hist.append(m)
            exp_m[(qc, pair, kb)] = m

        def emit_pv(qc, pair, kb):
            nkb = 4 * qc + 4
            di = kb - 4 * qc
            col0 = 128 * di if di >= 0 else 0
            HS = 2048 if qc == 0 else 8192
            PT = h2fm if qc == 0 else bufA
            wt("t", "S", exp_m[(qc, pair, kb)])
            mkey = None
            for h in range(2):
                def mm(e, pair=pair, kb=kb, h=h, col0=col0, HS=HS, PT=PT, nkb=nkb):
                    return e.matmul(
                        ps_pv[0:65, h * 512 + col0:h * 512 + 512],
                        v1[:, kb * 65:kb * 65 + 65],
                        PT[:, h * HS + kb * 512 + col0:h * HS + kb * 512 + 512],
                        start=(kb == 0), stop=(kb == nkb - 1))
                if h == 1:
                    mkey = op("t", mm, inc="T")
                else:
                    op("t", mm)
            pv_km[(qc, pair, kb)] = mkey

        def emit_epilogue(qc, pair):
            nkb = 4 * qc + 4
            es = "s" if qc <= 1 else "v"
            wt("v", "T", pv_km[(qc, pair, nkb - 1)])
            if es == "s":
                wt("s", "T", pv_km[(qc, pair, nkb - 1)])
            if qc > 0:
                wt("v", "T", m_wo.get((qc - 1, 4 * (qc - 1) + 3)))
                wt(es, "S", m_recipH[(qc - 1, 0 if pair < 4 else 1)])
            op("v", lambda e, pair=pair: e.tensor_copy(
                out=attnq[0:64, pair * 512:(pair + 1) * 512], in_=ps_pv[0:64, 0:512]))
            m_aq[qc][pair] = op("v", lambda e, pair=pair: e.tensor_copy(
                out=attnq[64:128, pair * 512:(pair + 1) * 512], in_=ps_pv[0:64, 512:1024]), inc="V")
            for hh in range(2):
                h = 2 * pair + hh
                b, c = h % 4, h // 4

                def cp(e, hh=hh, b=b, c=c):
                    if es == "s":
                        return e.activation(out=sums[32 * b:32 * b + 1, 512 * c:512 * c + 512],
                                            in_=ps_pv[64:65, hh * 512:hh * 512 + 512],
                                            func=AF.Copy, bias=0.0)
                    return e.tensor_copy(out=sums[32 * b:32 * b + 1, 512 * c:512 * c + 512],
                                         in_=ps_pv[64:65, hh * 512:hh * 512 + 512])
                if hh == 1:
                    m_ep[qc][pair] = op(es, cp, inc="S" if es == "s" else "V")
                else:
                    op(es, cp)

        m_recipH = {}

        def emit_recip_half(qc, half):
            wt("s", "S" if qc <= 1 else "V", m_ep[qc][3 if half == 0 else 7])
            c0 = half * 1024
            m_ln = op("s", lambda e, c0=c0: e.activation(
                out=sums[:, c0:c0 + 1024], in_=sums[:, c0:c0 + 1024], func=AF.Ln,
                bias=zerob[:, 0:1]), inc="S")
            ssync(m_ln)
            m_recipH[(qc, half)] = op("s", lambda e, c0=c0: e.activation(
                out=sums[:, c0:c0 + 1024], in_=sums[:, c0:c0 + 1024], func=AF.Exp,
                bias=zerob[:, 0:1], scale=-1.0), inc="S")

        def emit_recip(qc):
            if (qc, 0) not in m_recipH:
                emit_recip_half(qc, 0)
            emit_recip_half(qc, 1)
            m_recip[qc] = m_recipH[(qc, 1)]

        def emit_bcast_norm(qc):
            wt("v", "S" if qc <= 1 else "V", m_ep[qc][7])
            for pair in range(NPAIR):
                b, c = (2 * pair) % 4, (2 * pair) // 4
                wt("t", "S", m_recipH[(qc, 0 if pair < 4 else 1)])
                if pair >= 2:
                    wt("t", "V", m_nm[qc][pair - 2])
                if pair < 2:
                    wt("t", "V", m_woE[NDC - 1])
                m_bc[qc][pair] = op("t", lambda e, b=b, c=c, pair=pair: e.matmul(
                    ps_w[:, (pair % 2) * 512:(pair % 2) * 512 + 512],
                    fsel_sb[32 * b:32 * b + 64, :], sums[32 * b:32 * b + 64, 512 * c:512 * c + 512],
                    start=True, stop=True), inc="T")
                wt("v", "T", m_bc[qc][pair])
                m_nm[qc][pair] = op("v", lambda e, pair=pair: e.tensor_tensor(
                    out=attnq[:, pair * 512:(pair + 1) * 512],
                    in0=attnq[:, pair * 512:(pair + 1) * 512],
                    in1=ps_w[:, (pair % 2) * 512:(pair % 2) * 512 + 512], op=A.mult), inc="V")

        def emit_wo(qc, k):
            tb = 4 * qc + k
            wt("t", "V", m_nm[qc][7])
            if k >= 1:
                wt("t", "V", m_yev[(qc, tb - 1)])
            for dc in range(NDC):
                for ih in range(2):
                    def mm(e, qc=qc, k=k, dc=dc, ih=ih):
                        return e.matmul(ps_w[:, ih * 512:(ih + 1) * 512],
                                        attnq[:, dc * 512 + k * 128:dc * 512 + k * 128 + 128],
                                        wT[:, dc * 1024 + ih * 512:dc * 1024 + (ih + 1) * 512],
                                        start=(dc == 0), stop=(dc == NDC - 1))
                    if dc == NDC - 1 and ih == 1:
                        m_wo[(qc, tb)] = op("t", mm, inc="T")
                    else:
                        op("t", mm)

        def emit_yev(qc, k):
            tb = 4 * qc + k
            sl = k % 2
            wt("v", "T", m_wo[(qc, tb)])
            if k >= 2:
                if qc == NQC - 1:
                    vsync(m_nrm[(qc, tb - 2)])
                else:
                    wt("v", "G", m_nrm[(qc, tb - 2)])
            m_yev[(qc, tb)] = op("v", lambda e, sl=sl: e.tensor_copy(
                out=cs_y[:, sl * 1024:sl * 1024 + 1024], in_=ps_w[:, :]), inc="V")
            vsync(m_yev[(qc, tb)])
            op("v", lambda e, k=k, sl=sl: e.bn_stats(
                out=stats[:, k * 12:k * 12 + 6], in_=cs_y[:, sl * 1024:sl * 1024 + 512]))
            return op("v", lambda e, k=k, sl=sl: e.bn_stats(
                out=stats[:, k * 12 + 6:k * 12 + 12], in_=cs_y[:, sl * 1024 + 512:sl * 1024 + 1024]),
                inc="V")

        def ln_core(ks, statbase, mvbase, nmx, ivx, vepbase, mstats):
            vsync(mstats)
            m_ag = None
            for k in ks:
                m_ag = op("v", lambda e, k=k, statbase=statbase, mvbase=mvbase: e.bn_aggr(
                    out=mv[:, mvbase + k * 2:mvbase + k * 2 + 2],
                    in_=stats[:, statbase + k * 12:statbase + k * 12 + 12]), inc="V")
            vsync(m_ag)
            k0, w = ks[0], len(ks)
            op("v", lambda e, k0=k0, w=w, mvbase=mvbase, nmx=nmx: e.tensor_scalar(
                out=nmx[:, k0:k0 + w], in0=mv[:, mvbase + 2 * k0:mvbase + 2 * k0 + 2 * w:2],
                scalar1=-1.0, scalar2=None, op0=A.mult))
            mfv = op("v", lambda e, k0=k0, w=w, mvbase=mvbase, vepbase=vepbase: e.tensor_scalar(
                out=vep[:, vepbase + k0:vepbase + k0 + w],
                in0=mv[:, mvbase + 2 * k0 + 1:mvbase + 2 * k0 + 2 * w:2],
                scalar1=EPS, scalar2=None, op0=A.add), inc="V")
            wt("g", "V", mfv)
            mg = op("g", lambda e, k0=k0, w=w, ivx=ivx, vepbase=vepbase: e.tensor_tensor(
                out=ivx[:, k0:k0 + w], in0=vep[:, vepbase + k0:vepbase + k0 + w],
                in1=neghalf[:, k0:k0 + w], op=A.pow), inc="G")
            return mfv, mg

        def emit_ln2_half(qc, ks, mstats):
            fast = (qc == NQC - 1)
            eng, esem, syn = ("v", "V", vsync) if fast else ("g", "G", gsync)
            mfv, mg = ln_core(ks, 0, 0, nm2, iv2, 0, mstats)
            for k in ks:
                tb = 4 * qc + k
                sl = k % 2
                wt(eng, "V", mfv)
                wt(eng, "V", m_yev[(qc, tb)])
                wt(eng, "G", mg)
                if k != ks[0]:
                    syn(m_add2[(qc, tb - 1)])
                if fast and qc >= 1:
                    wt("v", "G", m_add4.get((qc - 1, 4 * (qc - 1) + 3)))
                m_nrm[(qc, tb)] = op(eng, lambda e, k=k, sl=sl: e.tensor_scalar(
                    out=lnbuf[:, :], in0=cs_y[:, sl * 1024:sl * 1024 + 1024],
                    scalar1=nm2[:, k:k + 1], scalar2=iv2[:, k:k + 1],
                    op0=A.add, op1=A.mult), inc=esem)
                syn(m_nrm[(qc, tb)])
                m_add2[(qc, tb)] = op(eng, lambda e, tb=tb: e.tensor_tensor(
                    out=x_sb[:, tb * 1024:(tb + 1) * 1024],
                    in0=x_sb[:, tb * 1024:(tb + 1) * 1024],
                    in1=lnbuf[:, :], op=A.add), inc=esem)

        def emit_ln3_half(qc, ks):
            fast = (qc == NQC - 1)
            eng, esem, syn = ("v", "V", vsync) if fast else ("g", "G", gsync)
            m_st = None
            for k in ks:
                tb = 4 * qc + k
                if fast:
                    vsync(m_add2[(qc, tb)])
                else:
                    wt("v", "G", m_add2[(qc, tb)])
                op("v", lambda e, k=k, tb=tb: e.bn_stats(
                    out=stats[:, 48 + k * 12:48 + k * 12 + 6], in_=x_sb[:, tb * 1024:tb * 1024 + 512]))
                m_st = op("v", lambda e, k=k, tb=tb: e.bn_stats(
                    out=stats[:, 48 + k * 12 + 6:48 + k * 12 + 12],
                    in_=x_sb[:, tb * 1024 + 512:tb * 1024 + 1024]), inc="V")
            mfv, mg = ln_core(ks, 48, 8, nm3, iv3, 4, m_st)
            for k in ks:
                tb = 4 * qc + k
                sl = k % 2
                wt(eng, "G", mg)
                syn(m_add2[(qc, tb)])
                if (qc, tb - 2) in m_h2T:
                    wt(eng, "T", m_h2T[(qc, tb - 2)])
                wt(eng, "V", mfv)
                m_n3[(qc, tb)] = op(eng, lambda e, k=k, tb=tb, sl=sl: e.tensor_scalar(
                    out=tmbf[:, sl * 1024:sl * 1024 + 1024],
                    in0=x_sb[:, tb * 1024:(tb + 1) * 1024],
                    scalar1=nm3[:, k:k + 1], scalar2=iv3[:, k:k + 1],
                    op0=A.add, op1=A.mult), inc=esem)

        def emit_h2T(qc, k):
            tb = 4 * qc + k
            sl = k % 2
            wt("t", "V" if qc == NQC - 1 else "G", m_n3[(qc, tb)])
            if k == 0:
                wt("t", "V", m_yev[(qc, 4 * qc + 3)])
            if (qc, tb - 1) in m_h2E:
                wt("t", "V", m_h2E[(qc, tb - 1)])
            for dc in range(NDC):
                def tr(e, dc=dc, sl=sl):
                    return e.transpose(psw_bf[:, sl * 1024 + dc * 128:sl * 1024 + dc * 128 + 128],
                                       tmbf[:, sl * 1024 + dc * 128:sl * 1024 + dc * 128 + 128],
                                       idb_sb[:, :])
                if dc == NDC - 1:
                    m_h2T[(qc, tb)] = op("t", tr, inc="T")
                else:
                    op("t", tr)
            wt("v", "T", m_h2T[(qc, tb)])
            m_h2E[(qc, tb)] = op("v", lambda e, k=k, sl=sl: e.tensor_copy(
                out=h2fm[:, :].rearrange("p (dc t) -> p dc t", dc=NDC, t=512)[:, :, k * 128:k * 128 + 128],
                in_=psw_bf[:, sl * 1024:sl * 1024 + 1024].rearrange("p (dc c) -> p dc c", dc=NDC, c=128)), inc="V")

        ffn_st = {}

        def emit_ffn_front(qc, part):
            if part == 0:
                wt("t", "V", m_h2E[(qc, 4 * qc + 3)])
                wt("t", "V", m_win)
                wt("t", "V", m_ffnw)
                if qc > 0:
                    wt("t", "V", m_gb[qc - 1])
                m_sp = None
                for dc in range(NDC):
                    for half in range(2):
                        def mm(e, dc=dc, half=half):
                            return e.matmul(
                                ps_w[half * 64:half * 64 + 64, 0:256],
                                win_sb[:, dc * 64:(dc + 1) * 64],
                                h2fm[:, dc * 512 + half * 256:dc * 512 + (half + 1) * 256],
                                start=(dc == 0), stop=(dc == NDC - 1), tile_position=(0, half * 64),
                                skip_group_check=True)
                        if dc == NDC - 1 and half == 1:
                            m_sp = op("t", mm, inc="T")
                        else:
                            op("t", mm)
                wt("v", "T", m_sp)
                if qc > 0:
                    wt("v", "T", m_fpl[qc - 1])
                m_sev = op("v", lambda e: e.tensor_copy(out=s_sbv, in_=ps_w[:, 0:256]), inc="V")
                ffn_st[qc] = {"sev": m_sev, "relu": {}, "acc": {}}
            fst = ffn_st[qc]
            m_sev, m_relu, m_acc = fst["sev"], fst["relu"], fst["acc"]

            fastq = (qc == NQC - 1)

            def step(i, qc=qc, fastq=fastq):
                if fastq:
                    wt("s", "V", m_sev if i < 2 else m_acc[i - 2])
                    m_relu[i] = op("s", lambda e, i=i: e.activation(
                        out=hbuf[i % 2], in_=s_sbv, func=AF.Relu,
                        bias=b1pd[:, i:i + 1], scale=w1d[:, i:i + 1]), inc="S")
                    wt("v", "S", m_relu[i])
                    if i >= 2:
                        vsync(m_acc[i - 2])
                    elif m_accL[0] is not None:
                        vsync(m_accL[0])
                elif i < 2:
                    vsync(m_sev)
                    if m_accL[0] is not None:
                        vsync(m_accL[0])
                else:
                    vsync(m_acc[i - 2])
                if not fastq:
                    m_pre = op("v", lambda e, i=i: e.tensor_scalar(
                        out=hbuf[i % 2], in0=s_sbv, scalar1=w1d[:, i:i + 1],
                        scalar2=b1pd[:, i:i + 1], op0=A.mult, op1=A.add), inc="V")
                    vsync(m_pre)
                    m_relu[i] = op("v", lambda e, i=i: e.tensor_scalar(
                        out=hbuf[i % 2], in0=hbuf[i % 2], scalar1=0.0, scalar2=None,
                        op0=A.max), inc="V")
                    vsync(m_relu[i])
                if i < 2:
                    m_acc[i] = op("v", lambda e, i=i: e.tensor_scalar(
                        out=gsl[i], in0=hbuf[i], scalar1=w2d[:, i:i + 1],
                        scalar2=(b2d[:, 0:1] if i == 0 else zerob[:, 0:1]),
                        op0=A.mult, op1=A.add), inc="V")
                else:
                    m_acc[i] = op("v", lambda e, i=i: e.scalar_tensor_tensor(
                        out=gsl[i % 2], in0=hbuf[i % 2], scalar=w2d[:, i:i + 1], in1=gsl[i % 2],
                        op0=A.mult, op1=A.add), inc="V")
                if i == 15:
                    vsync(m_acc[15])
                    m_gb[qc] = op("v", lambda e: e.tensor_tensor(
                        out=gb, in0=gsl[0], in1=gsl[1], op=A.add), inc="V")
                    m_accL[0] = m_acc[15]
            for i in (range(0, 8) if part == 0 else range(8, 16)):
                strickle.append(lambda i=i: step(i))

        def emit_fproj(qc, tt):
            hb = 64 * (tt // 2)
            gcol = (tt % 2) * 128
            wt("t", "V", m_gb[qc] if tt == 0 else m_fe[(qc, tt - 1)])
            m_fp = None
            for ih in range(2):
                def mm(e, hb=hb, gcol=gcol, ih=ih):
                    return e.matmul(ps_w[:, ih * 512:(ih + 1) * 512],
                                    gb[hb:hb + 64, gcol:gcol + 128],
                                    woutd[hb:hb + 64, ih * 512:(ih + 1) * 512],
                                    start=True, stop=True, tile_position=(hb, 0))
                if ih == 1:
                    m_fp = op("t", mm, inc="T")
                else:
                    op("t", mm)
            m_fpl[qc] = m_fp
            wt("v", "T", m_fp)
            if tt >= 1:
                if qc == NQC - 1:
                    vsync(m_nrm4.get((qc, tt - 1)))
                else:
                    wt("v", "G", m_nrm4.get((qc, tt - 1)))
            for ih in range(2):
                def ev(e, ih=ih):
                    return e.tensor_tensor(
                        out=f_sb[:, ih * 512:(ih + 1) * 512],
                        in0=ps_w[:, ih * 512:(ih + 1) * 512],
                        in1=boutb[:, ih * 512:(ih + 1) * 512], op=A.add)
                if ih == 1:
                    m_fe[(qc, tt)] = op("v", ev, inc="V")
                else:
                    op("v", ev)

        m_nrm4 = {}

        def emit_ln4_single(qc, k):
            fast = (qc == NQC - 1)
            eng, esem, syn = ("v", "V", vsync) if fast else ("g", "G", gsync)
            vsync(m_fe[(qc, k)])
            op("v", lambda e, k=k: e.bn_stats(
                out=stats[:, 96 + k * 12:96 + k * 12 + 6], in_=f_sb[:, 0:512]))
            m_st = op("v", lambda e, k=k: e.bn_stats(
                out=stats[:, 96 + k * 12 + 6:96 + k * 12 + 12], in_=f_sb[:, 512:1024]), inc="V")
            mfv, mg = ln_core((k,), 96, 16, nm4, iv4, 8, m_st)
            tb = 4 * qc + k
            wt(eng, "V", mfv)
            wt(eng, "V", m_fe[(qc, k)])
            wt(eng, "G", mg)
            syn(m_add2[(qc, tb)])
            if k >= 1:
                syn(m_add4[(qc, tb - 1)])
            m_nrm4[(qc, k)] = op(eng, lambda e, k=k: e.tensor_scalar(
                out=lnbuf[:, :], in0=f_sb[:, :],
                scalar1=nm4[:, k:k + 1], scalar2=iv4[:, k:k + 1],
                op0=A.add, op1=A.mult), inc=esem)
            syn(m_nrm4[(qc, k)])
            m_add4[(qc, tb)] = op(eng, lambda e, tb=tb: e.tensor_tensor(
                out=x_sb[:, tb * 1024:(tb + 1) * 1024],
                in0=x_sb[:, tb * 1024:(tb + 1) * 1024],
                in1=lnbuf[:, :], op=A.add), inc=esem)

        def emit_yout(qc):
            asem = "V" if qc == NQC - 1 else "G"
            for hf in range(2):
                streams["y"].append(lambda e, v=m_add4[(qc, 4 * qc + 2 * hf + 1)], asem=asem:
                                     e.wait_ge(sems[asem][0], v))
                streams["y"].append(lambda e, v=sems["dO"][1]: (e.wait_ge(sems["dO"][0], v) if v > 0 else None))
                op("y", lambda e, qc=qc, hf=hf: e.dma_start(
                    out=yout[qc * 512 + hf * 256:qc * 512 + (hf + 1) * 256, :].rearrange("(t p) d -> p t d", p=128),
                    in_=x_sb[:, qc * 4096 + hf * 2048:qc * 4096 + (hf + 1) * 2048].rearrange(
                        "p (t d) -> p t d", t=2, d=1024)), inc="dO", inck=16)

        def D_units(qc):
            def u1():
                emit_wo(qc, 0)
                emit_yev(qc, 0)
                emit_wo(qc, 1)
                ms1 = emit_yev(qc, 1)
                emit_ln2_half(qc, (0, 1), ms1)

            def u2():
                emit_wo(qc, 2)
                emit_yev(qc, 2)
                emit_wo(qc, 3)
                ms3 = emit_yev(qc, 3)
                emit_ln2_half(qc, (2, 3), ms3)

            return [
                lambda: emit_bcast_norm(qc),
                u1,
                u2,
                lambda: (emit_ln3_half(qc, (0, 1)), emit_h2T(qc, 0), emit_h2T(qc, 1)),
                lambda: (emit_ln3_half(qc, (2, 3)), emit_h2T(qc, 2), emit_h2T(qc, 3)),
                lambda: emit_ffn_front(qc, 0),
                lambda: emit_ffn_front(qc, 1),
                lambda: (emit_fproj(qc, 0), emit_ln4_single(qc, 0)),
                lambda: (emit_fproj(qc, 1), emit_ln4_single(qc, 1)),
                lambda: (emit_fproj(qc, 2), emit_ln4_single(qc, 2)),
                lambda: (emit_fproj(qc, 3), emit_ln4_single(qc, 3), emit_yout(qc)),
            ]

        if stage <= 3:
            debug_finish()
        nqc_run = NQC if stage >= 5 else (1 if stage == 4 else 0)
        do_D = stage >= 6
        dq = []
        trail = []
        prev_qc = None
        for qc in range(nqc_run):
            nkb = 4 * qc + 4
            for pair in range(NPAIR):
                for kb in range(nkb):
                    if pair == 0 and trail:
                        trail.pop(0)()
                    if pair >= 1:
                        if kb == 0:
                            if pair >= 2:
                                wt("t", "S" if qc <= 1 else "V", m_ep[qc][pair - 2])
                                wt("t", "V", m_aq[qc][pair - 2])
                            elif qc >= 1:
                                wt("t", "S" if qc - 1 <= 1 else "V", m_ep[qc - 1][7])
                                wt("t", "V", m_aq[qc - 1][7])
                        emit_pv(qc, pair - 1, kb)
                    m_sc = emit_scores(qc, pair, kb)
                    emit_exp(qc, pair, kb, m_sc)
                    if strickle:
                        strickle.pop(0)()
                    if (kb == nkb // 2 - 1 and pair == 0 and do_D and prev_qc is not None
                            and (prev_qc, 0) not in m_recipH):
                        emit_recip_half(prev_qc, 0)
                    if kb == nkb // 2 - 1:
                        if dqq:
                            dqq.pop(0)()
                        elif do_D and dq and pair >= 1 and not strickle:
                            dq.pop(0)()
                if pair == 0:
                    while trail:
                        trail.pop(0)()
                    if do_D and prev_qc is not None:
                        emit_recip(prev_qc)

                if dqq:
                    dqq.pop(0)()
                elif do_D and dq and (not strickle or pair <= 1):
                    dq.pop(0)()
                if pair >= 1:
                    emit_epilogue(qc, pair - 1)

            def mk_trail(qc=qc, nkb=nkb):
                t = []
                for kb in range(nkb):
                    def f(kb=kb, qc=qc):
                        if kb == 0:
                            wt("t", "S" if qc <= 1 else "V", m_ep[qc][6])
                            wt("t", "V", m_aq[qc][6])
                        emit_pv(qc, 7, kb)
                    t.append(f)
                t.append(lambda qc=qc: emit_epilogue(qc, 7))
                return t
            trail = mk_trail()
            if do_D:
                nu = D_units(qc)
                dq = nu[0:3] + dq + nu[3:]
                prev_qc = qc
        while trail:
            trail.pop(0)()
        if do_D:
            emit_recip(prev_qc)
            for u in dq:
                u()
                while strickle:
                    strickle.pop(0)()
        else:
            debug_finish()

        streams["y"].append(lambda e: e.wait_ge(sems["dO"][0], sems["dO"][1]))

        block = st.enter_context(nc.Block())

        @block.sync
        def _(eng):
            for f in streams["y"]:
                f(eng)

        @block.tensor
        def _(eng):
            for f in streams["t"]:
                f(eng)

        @block.vector
        def _(eng):
            for f in streams["v"]:
                f(eng)

        @block.scalar
        def _(eng):
            for f in streams["s"]:
                f(eng)

        @block.gpsimd
        def _(eng):
            for f in streams["g"]:
                f(eng)

    return nc


_CACHE = {}


def kernel(L0_attn_x, L0_attn_wq, L0_attn_wk, L0_attn_wv, L0_attn_wo,
           L0_ffn_Win, L0_ffn_bin_, L0_ffn_W1, L0_ffn_b1, L0_ffn_W2,
           L0_ffn_b2, L0_ffn_Wout, L0_ffn_bout):
    x = np.ascontiguousarray(np.asarray(L0_attn_x, dtype=np.float32))
    consts = _host_consts()
    if "nc" not in _CACHE:
        _CACHE["nc"] = build_nc()
    nc = _CACHE["nc"]
    shared = {
        "wq": np.ascontiguousarray(np.asarray(L0_attn_wq, dtype=np.float32)),
        "wk": np.ascontiguousarray(np.asarray(L0_attn_wk, dtype=np.float32)),
        "wv": np.ascontiguousarray(np.asarray(L0_attn_wv, dtype=np.float32)),
        "wo": np.ascontiguousarray(np.asarray(L0_attn_wo, dtype=np.float32)),
        "win": np.ascontiguousarray(np.asarray(L0_ffn_Win, dtype=np.float32)),
        "binp": np.ascontiguousarray(np.asarray(L0_ffn_bin_, dtype=np.float32).reshape(64, 1)),
        "w1": np.ascontiguousarray(np.asarray(L0_ffn_W1, dtype=np.float32)),
        "b1": np.ascontiguousarray(np.asarray(L0_ffn_b1, dtype=np.float32)),
        "w2": np.ascontiguousarray(np.asarray(L0_ffn_W2, dtype=np.float32)),
        "b2": np.ascontiguousarray(np.asarray(L0_ffn_b2, dtype=np.float32).reshape(64, 1)),
        "wout": np.ascontiguousarray(np.asarray(L0_ffn_Wout, dtype=np.float32)),
        "bout": np.ascontiguousarray(np.asarray(L0_ffn_bout, dtype=np.float32).reshape(1, D)),
    }
    shared.update(consts)
    in_maps = []
    for i in range(NCORE):
        m = dict(shared)
        m["x"] = np.ascontiguousarray(x[i])
        in_maps.append(m)
    res = run_bass_kernel_spmd(nc, in_maps, core_ids=list(range(NCORE)))
    out = np.stack([np.asarray(r["y"]) for r in res.results], axis=0)
    return out.astype(np.float32)


# version 32
# speedup vs baseline: 1.0398x; 1.0112x over previous
import os
import numpy as np
import ml_dtypes
import concourse.bass as bass
import concourse.mybir as mybir
from concourse.bass_utils import run_bass_kernel_spmd
from concourse.alu_op_type import AluOpType as A
from contextlib import ExitStack

F32 = mybir.dt.float32
BF16 = mybir.dt.bfloat16
AF = mybir.ActivationFunctionType

NCORE = 8
S_TOK = 2048
D = 1024
NTB = 16
NDC = 8
NQC = 4
NPAIR = 8
EPS = 1e-5
ESHIFT = -8.0


def _host_consts():
    ident = np.eye(128, dtype=np.float32)
    identb = np.eye(128, dtype=ml_dtypes.bfloat16)
    j = np.arange(64)
    p = np.arange(128)
    tria = np.zeros((128, 256), dtype=np.float32)
    tria[0:64, 0:128] = (p[None, :] > j[:, None]).astype(np.float32)
    tria[0:64, 128:256] = (p[None, :] > j[:, None] + 64).astype(np.float32)
    tria[64:128] = tria[0:64]
    negi = np.zeros((128, 256), dtype=np.float32)
    negi[j, j] = -1e30
    negi[j, 128 + 64 + j] = -1e30
    negi[64:128] = negi[0:64]
    fsel = np.zeros((128, 128), dtype=np.float32)
    fsel[0, 0:64] = 1.0
    fsel[32, 64:128] = 1.0
    fsel[64:128] = fsel[0:64]
    t = np.arange(S_TOK, dtype=np.float64)[:, None]
    f = 1000.0 ** (-np.arange(32, dtype=np.float64) / 32.0)[None, :]
    tf = t * f
    two_pi = 2.0 * np.pi

    def hilo(theta):
        y = np.mod(theta, two_pi) - np.pi
        hi = y.astype(np.float32)
        lo = (y - hi.astype(np.float64)).astype(np.float32)
        return hi, lo
    thc = np.concatenate([tf, tf], axis=1) + np.pi / 2.0
    ths = np.concatenate([-tf, tf], axis=1)
    yc, ycd = hilo(thc)
    ys, ysd = hilo(ths)
    return {
        "c_ident": ident, "c_identb": identb,
        "c_tria": tria.astype(ml_dtypes.bfloat16), "c_negi": negi.astype(ml_dtypes.bfloat16),
        "c_fsel": fsel,
        "c_yc": yc, "c_ycd": ycd, "c_ys": ys, "c_ysd": ysd,
    }


def build_nc():
    stage = int(os.environ.get("KSTAGE", "6"))
    nc = bass.Bass("TRN2", target_bir_lowering=False)
    xin = nc.dram_tensor("x", [S_TOK, D], F32, kind="ExternalInput").ap()
    wq = nc.dram_tensor("wq", [D, D], F32, kind="ExternalInput").ap()
    wk = nc.dram_tensor("wk", [D, 64], F32, kind="ExternalInput").ap()
    wv = nc.dram_tensor("wv", [D, 64], F32, kind="ExternalInput").ap()
    wo = nc.dram_tensor("wo", [D, D], F32, kind="ExternalInput").ap()
    win = nc.dram_tensor("win", [D, 64], F32, kind="ExternalInput").ap()
    binp = nc.dram_tensor("binp", [64, 1], F32, kind="ExternalInput").ap()
    w1 = nc.dram_tensor("w1", [64, 16], F32, kind="ExternalInput").ap()
    b1 = nc.dram_tensor("b1", [64, 16], F32, kind="ExternalInput").ap()
    w2 = nc.dram_tensor("w2", [64, 16], F32, kind="ExternalInput").ap()
    b2 = nc.dram_tensor("b2", [64, 1], F32, kind="ExternalInput").ap()
    wout = nc.dram_tensor("wout", [64, D], F32, kind="ExternalInput").ap()
    bout = nc.dram_tensor("bout", [1, D], F32, kind="ExternalInput").ap()
    c_ident = nc.dram_tensor("c_ident", [128, 128], F32, kind="ExternalInput").ap()
    c_identb = nc.dram_tensor("c_identb", [128, 128], BF16, kind="ExternalInput").ap()
    c_tria = nc.dram_tensor("c_tria", [128, 256], BF16, kind="ExternalInput").ap()
    c_negi = nc.dram_tensor("c_negi", [128, 256], BF16, kind="ExternalInput").ap()
    c_fsel = nc.dram_tensor("c_fsel", [128, 128], F32, kind="ExternalInput").ap()
    c_yc = nc.dram_tensor("c_yc", [S_TOK, 64], F32, kind="ExternalInput").ap()
    c_ycd = nc.dram_tensor("c_ycd", [S_TOK, 64], F32, kind="ExternalInput").ap()
    c_ys = nc.dram_tensor("c_ys", [S_TOK, 64], F32, kind="ExternalInput").ap()
    c_ysd = nc.dram_tensor("c_ysd", [S_TOK, 64], F32, kind="ExternalInput").ap()
    yout = nc.dram_tensor("y", [S_TOK, D], F32, kind="ExternalOutput").ap()

    with ExitStack() as st:
        def sb(name, shape, dt=F32):
            return st.enter_context(nc.sbuf_tensor(name, shape, dt))

        x_sb = sb("x_sb", [128, NTB * 1024])
        bufA = sb("bufA", [128, 16384], BF16)
        qfm = sb("qfm", [128, 16384], BF16)
        wT = sb("wT", [128, 8192], BF16)
        h2fm = sb("h2fm", [128, 4096], BF16)
        attnq = sb("attnq", [128, 4096], BF16)
        kvd = sb("kvd", [128, 2048], BF16)
        v1 = sb("v1", [128, 16 * 65], BF16)
        wstage = sb("wstage", [128, 512])
        tmroll = sb("tmroll", [128, 1024])
        qtm = sb("qtm", [128, 1024], BF16)
        qrope = sb("qrope", [128, 2048], BF16)
        cs_y = sb("cs_y", [128, 2048], BF16)
        sums = sb("sums", [128, 2048])
        lnbuf = sb("lnbuf", [128, 1024], BF16)
        f_sb = sb("f_sb", [128, 1024], BF16)
        kvw = sb("kvw", [128, 1024], BF16)
        woutd = sb("woutd", [128, 1024], BF16)
        boutb = sb("boutb", [128, 1024], BF16)
        w1d = sb("w1d", [128, 16])
        b1pd = sb("b1pd", [128, 16])
        w2d = sb("w2d", [128, 16])
        b2d = sb("b2d", [128, 1])
        stats = sb("stats", [128, 192])
        bind = stats[:, 174:175]
        b1d = stats[:, 176:192]
        win_sb = kvw[:, 0:512]
        mv = sb("mv", [128, 32])
        nm1 = sb("nm1", [128, 16])
        iv1 = sb("iv1", [128, 16])
        vep = sb("vep", [128, 16])
        nm2 = sb("nm2", [128, 4]); iv2 = sb("iv2", [128, 4])
        nm3 = sb("nm3", [128, 4]); iv3 = sb("iv3", [128, 4])
        nm4 = sb("nm4", [128, 4]); iv4 = sb("iv4", [128, 4])
        neghalf = sb("neghalf", [128, 16])
        negone = sb("negone", [128, 1])
        expb = sb("expb", [128, 1])
        ones1 = sb("ones1", [1, 128])
        zerob = sb("zerob", [128, 1])
        junk = sb("junk", [128, 16])
        junkg = sb("junkg", [128, 16])
        id_sb = sb("id_sb", [128, 128])
        idb_sb = sb("idb_sb", [128, 128], BF16)
        tria_sb = sb("tria_sb", [128, 256], BF16)
        negi_sb = sb("negi_sb", [128, 256], BF16)
        fsel_sb = sb("fsel_sb", [128, 128])

        ps_st = st.enter_context(nc.psum_tensor("ps_st", [128, 2048], F32))
        ps_pv = st.enter_context(nc.psum_tensor("ps_pv", [128, 1024], F32))
        ps_w = st.enter_context(nc.psum_tensor("ps_w", [128, 1024], F32))

        s_sbv = qtm[:, 0:256]
        hbuf = [qtm[:, 256:512], qtm[:, 512:768]]
        gb = qtm[:, 768:1024]
        gsl = [qrope[:, :].bitcast(F32)[:, 0:256], qrope[:, :].bitcast(F32)[:, 256:512]]

        tmbf = tmroll[:, :].bitcast(BF16)
        qfm_f32 = qfm[:, :].bitcast(F32)
        psst_bf = ps_st[:, :].bitcast(BF16)
        psw_bf = ps_w[:, :].bitcast(BF16)
        pspv_bf = ps_pv[:, :].bitcast(BF16)
        wstage_bf = wstage[:, :].bitcast(BF16)

        sems = {}
        for s in ("dX", "dW", "dO", "dQ", "T", "V", "S", "G"):
            sems[s] = [st.enter_context(nc.semaphore(s)), 0]
        streams = {k: [] for k in ("t", "v", "s", "g", "y")}

        def op(eng, build, inc=None, inck=1):
            if inc is not None:
                sems[inc][1] += inck
            val = sems[inc][1] if inc is not None else None

            def f(e, build=build, inc=inc, inck=inck):
                ins = build(e)
                if inc is not None:
                    ins.then_inc(sems[inc][0], inck)
            streams[eng].append(f)
            return val

        def wt(eng, sem, val):
            if val is None or val <= 0:
                return
            streams[eng].append(lambda e, s=sem, v=val: e.wait_ge(sems[s][0], v))

        def vsync(val):
            wt("v", "V", val)

        def gsync(val):
            wt("g", "G", val)

        def ssync(val):
            wt("s", "S", val)

        _fr = [0, 0]

        def vfence(inc=None):
            _fr[0] += 1
            c = _fr[0] % 16
            return op("v", lambda e, c=c: e.memset(junk[:, c:c + 1], 0.0), inc=inc)

        def gfence(inc=None):
            _fr[1] += 1
            c = _fr[1] % 16
            return op("g", lambda e, c=c: e.memset(junkg[:, c:c + 1], 0.0), inc=inc)

        _done = [False]

        def debug_finish():
            if _done[0]:
                return
            _done[0] = True
            for semname in ("T", "V", "S", "G"):
                v = sems[semname][1]
                if v > 0:
                    streams["y"].append(lambda e, sn=semname, v=v: e.wait_ge(sems[sn][0], v))
            op("y", lambda e: e.dma_start(
                out=yout.rearrange("(t p) d -> p t d", p=128),
                in_=x_sb[:, :].rearrange("p (t d) -> p t d", t=NTB, d=1024)),
                inc="dO", inck=16)

        op("v", lambda e: e.memset(expb[:, :], ESHIFT))
        op("v", lambda e: e.memset(zerob[:, :], 0.0))
        op("v", lambda e: e.memset(neghalf[:, :], -0.5))
        m_negone = op("v", lambda e: e.memset(negone[:, :], -1.0), inc="V")
        op("v", lambda e: e.memset(sums[:, :], 1.0))
        m_ones = op("v", lambda e: e.memset(
            v1[:, :].rearrange("p (t c) -> p t c", t=16, c=65)[:, :, 64:65], 1.0), inc="V")

        def dma(build):
            return op("y", build, inc="dW", inck=16)

        m_xg = [None] * 4

        dma(lambda e: e.dma_start(out=id_sb[:, :], in_=c_ident[:, :]))
        m_idB = dma(lambda e: e.dma_start(out=idb_sb[:, :], in_=c_identb[:, :]))
        streams["y"].append(lambda e, v=m_idB: e.wait_ge(sems["dW"][0], v))
        dma(lambda e: e.dma_start(out=tria_sb[:, :], in_=c_tria[:, :]))
        dma(lambda e: e.dma_start(out=negi_sb[:, :], in_=c_negi[:, :]))
        dma(lambda e: e.dma_start(out=fsel_sb[:, :], in_=c_fsel[:, :]))
        dma(lambda e: e.dma_start(out=w1d[0:64, :], in_=w1[:, :]))
        dma(lambda e: e.dma_start(out=b1d[0:64, :], in_=b1[:, :]))
        dma(lambda e: e.dma_start(out=w2d[0:64, :], in_=w2[:, :]))
        dma(lambda e: e.dma_start(out=b2d[0:64, :], in_=b2[:, :]))
        m_small = dma(lambda e: e.dma_start(out=bind[0:64, :], in_=binp[:, :]))

        def tab_view(tab, half=None):
            v = tab.rearrange("(t p) j -> p t j", p=128)
            if half is None:
                return v
            return v[:, half * 8:(half + 1) * 8, :]

        m_tabL = None
        for i, tab in enumerate((c_yc, c_ycd, c_ys, c_ysd)):
            m_tabL = dma(lambda e, tab=tab, i=i: e.dma_start(
                out=qfm_f32[:, i * 1024:(i + 1) * 1024].rearrange("p (t j) -> p t j", t=16, j=64),
                in_=tab_view(tab)))
        streams["y"].append(lambda e, v=m_tabL: e.wait_ge(sems["dW"][0], v))
        wq_r = wq.rearrange("(ic r) d -> r ic d", r=128)
        m_wqL = [None]
        for h2 in range(2):
            for eo in range(2):
                src = wq_r[h2 * 64 + eo:h2 * 64 + 64:2, :, :]
                dst = qfm[h2 * 64 + eo * 32:h2 * 64 + eo * 32 + 32, 8192:16384].rearrange(
                    "p (ic d) -> p ic d", ic=8)
                m_wqL[0] = op("g", lambda e, src=src, dst=dst: e.dma_start(out=dst, in_=src),
                              inc="dQ", inck=16)
        streams["g"].append(lambda e, v=m_wqL[0]: e.wait_ge(sems["dQ"][0], v))
        for g in range(4):
            m_xg[g] = op("g", lambda e, g=g: e.dma_start(
                out=x_sb[:, g * 4096:(g + 1) * 4096].rearrange("p (t d) -> p t d", t=4, d=1024),
                in_=xin.rearrange("(t p) d -> p t d", p=128)[:, g * 4:(g + 1) * 4, :]),
                inc="dX", inck=16)
            streams["g"].append(lambda e, v=m_xg[g]: e.wait_ge(sems["dX"][0], v))
        m_x = m_xg[3]
        def emit_sin(hi_col, lo_col, dstcol, prev_gate):
            if prev_gate is not None:
                ssync(prev_gate)
                wt("v", "S", prev_gate)
            wt("v", "dW", m_tabL)
            m_add = op("v", lambda e, hi_col=hi_col, lo_col=lo_col: e.tensor_tensor(
                out=tmroll[:, :], in0=qfm_f32[:, hi_col:hi_col + 1024],
                in1=qfm_f32[:, lo_col:lo_col + 1024], op=A.add), inc="V")
            wt("s", "V", m_add)
            return op("s", lambda e, dstcol=dstcol: e.activation(
                out=cs_y[:, dstcol:dstcol + 1024], in_=tmroll[:, :],
                func=AF.Sin, bias=zerob[:, 0:1], scale=-1.0), inc="S")

        m_sin1 = emit_sin(0, 1024, 0, None)
        m_sin2 = emit_sin(2048, 3072, 1024, m_sin1)

        m_wkL = dma(lambda e: e.dma_start(
            out=wstage[:, :].rearrange("p (c h) -> p c h", c=8, h=64),
            in_=wk.rearrange("(c p) h -> p c h", p=128)))
        wt("v", "dW", m_wkL)
        wsv = wstage[:, :].rearrange("p (c e) -> p c e", c=512, e=1)
        m_wkC = None
        for dc in range(NDC):
            op("v", lambda e, dc=dc: e.tensor_copy(
                out=kvw[:, dc * 128 + 0:dc * 128 + 32],
                in_=wsv[:, dc * 64 + 0:dc * 64 + 64:2, 0]))
            m_wkC = op("v", lambda e, dc=dc: e.tensor_copy(
                out=kvw[:, dc * 128 + 32:dc * 128 + 64],
                in_=wsv[:, dc * 64 + 1:dc * 64 + 64:2, 0]), inc="V")
        streams["y"].append(lambda e, v=m_wkC: e.wait_ge(sems["V"][0], v))
        m_wvL = dma(lambda e: e.dma_start(
            out=wstage[:, :].rearrange("p (c h) -> p c h", c=8, h=64),
            in_=wv.rearrange("(c p) h -> p c h", p=128)))
        wt("v", "dW", m_wvL)
        m_kvw = None
        for dc in range(NDC):
            m_kvw = op("v", lambda e, dc=dc: e.tensor_copy(
                out=kvw[:, dc * 128 + 64:dc * 128 + 128],
                in_=wstage[:, dc * 64:dc * 64 + 64]), inc="V")
        streams["y"].append(lambda e, v=m_kvw: e.wait_ge(sems["V"][0], v))
        m_wout = None
        for half in range(2):
            m_woutL = dma(lambda e, half=half: e.dma_start(
                out=wstage[0:64, :], in_=wout[:, half * 512:(half + 1) * 512]))
            wt("v", "dW", m_woutL)
            op("v", lambda e, half=half: e.tensor_copy(
                out=woutd[0:64, half * 512:(half + 1) * 512], in_=wstage[0:64, :]))
            m_wout = op("v", lambda e, half=half: e.tensor_copy(
                out=woutd[64:128, half * 512:(half + 1) * 512], in_=wstage[0:64, :]), inc="V")
            streams["y"].append(lambda e, v=m_wout: e.wait_ge(sems["V"][0], v))
        m_ones1 = op("v", lambda e: e.memset(ones1[:, :], 1.0), inc="V")
        m_boutc = None
        m_bmm = None
        for half in range(2):
            m_boutL = dma(lambda e, half=half: e.dma_start(
                out=wstage[0:1, :], in_=bout[:, half * 512:(half + 1) * 512]))
            wt("t", "dW", m_boutL)
            wt("t", "V", m_ones1)
            if half == 1:
                wt("t", "V", m_boutc)
            m_bmm = op("t", lambda e: e.matmul(
                ps_w[:, 0:512], ones1[0:1, :], wstage[0:1, :], start=True, stop=True), inc="T")
            wt("v", "T", m_bmm)
            m_boutc = op("v", lambda e, half=half: e.tensor_copy(
                out=boutb[:, half * 512:(half + 1) * 512], in_=ps_w[:, 0:512]), inc="V")
            streams["y"].append(lambda e, v=m_boutc: e.wait_ge(sems["V"][0], v))
        op("v", lambda e: e.tensor_copy(out=w1d[64:128, :], in_=w1d[0:64, :]))
        op("v", lambda e: e.tensor_copy(out=b1d[64:128, :], in_=b1d[0:64, :]))
        op("v", lambda e: e.tensor_copy(out=w2d[64:128, :], in_=w2d[0:64, :]))
        op("v", lambda e: e.tensor_copy(out=b2d[64:128, :], in_=b2d[0:64, :]))
        m_dups = op("v", lambda e: e.tensor_copy(out=bind[64:128, :], in_=bind[0:64, :]), inc="V")
        vsync(m_dups)
        m_ffnw = op("v", lambda e: e.scalar_tensor_tensor(
            out=b1pd[:, :], in0=w1d[:, :], scalar=bind[:, 0:1], in1=b1d[:, :],
            op0=A.mult, op1=A.add), inc="V")

        m_ln1g = [None] * 4
        for g in range(4):
            wt("v", "dX", m_xg[g])
            if g == 0:
                vsync(m_ffnw)
            m_st1 = None
            for tb in range(4 * g, 4 * g + 4):
                op("v", lambda e, tb=tb: e.bn_stats(out=stats[:, tb * 12:tb * 12 + 6],
                                                    in_=x_sb[:, tb * 1024:tb * 1024 + 512]))
                m_st1 = op("v", lambda e, tb=tb: e.bn_stats(out=stats[:, tb * 12 + 6:tb * 12 + 12],
                                                            in_=x_sb[:, tb * 1024 + 512:tb * 1024 + 1024]),
                           inc="V")
            vsync(m_st1)
            m_ag1 = None
            for tb in range(4 * g, 4 * g + 4):
                m_ag1 = op("v", lambda e, tb=tb: e.bn_aggr(out=mv[:, tb * 2:tb * 2 + 2],
                                                           in_=stats[:, tb * 12:tb * 12 + 12]), inc="V")
            vsync(m_ag1)
            op("v", lambda e, g=g: e.tensor_scalar(
                out=nm1[:, 4 * g:4 * g + 4], in0=mv[:, 8 * g:8 * g + 8:2],
                scalar1=-1.0, scalar2=None, op0=A.mult))
            m_ln1v = op("v", lambda e, g=g: e.tensor_scalar(
                out=vep[:, 4 * g:4 * g + 4], in0=mv[:, 8 * g + 1:8 * g + 8:2],
                scalar1=EPS, scalar2=None, op0=A.add), inc="V")
            wt("g", "V", m_ln1v)
            m_ln1g[g] = op("g", lambda e, g=g: e.tensor_tensor(
                out=iv1[:, 4 * g:4 * g + 4], in0=vep[:, 4 * g:4 * g + 4],
                in1=neghalf[:, 4 * g:4 * g + 4], op=A.pow), inc="G")

        m_wqT = [None] * NDC
        m_wqE = [None] * NDC
        for ic in range(NDC):
            wt("t", "dQ", m_wqL[0])
            if ic == 0:
                wt("t", "dW", m_idB)
            if ic >= 1:
                wt("t", "S", m_wqE[ic - 1])
            for dc in range(NDC):
                def tr(e, ic=ic, dc=dc):
                    return e.transpose(
                        psst_bf[:, 2048 + dc * 128:2048 + dc * 128 + 128],
                        qfm[:, 8192 + ic * 1024 + dc * 128:8192 + ic * 1024 + (dc + 1) * 128],
                        idb_sb[:, :])
                if dc == NDC - 1:
                    m_wqT[ic] = op("t", tr, inc="T")
                else:
                    op("t", tr)
            wt("s", "T", m_wqT[ic])
            m_wqE[ic] = op("s", lambda e, ic=ic: e.activation(
                out=wT[:, :].rearrange("p (dc i) -> p dc i", dc=NDC, i=1024)[:, :, ic * 128:ic * 128 + 128],
                in_=psst_bf[:, 2048:3072].rearrange("p (s c) -> p s c", s=8, c=128),
                func=AF.Copy, bias=0.0), inc="S")

        m_qp = [None] * NTB
        m_qtmE = [None] * NTB
        m_qT = [None] * NTB
        m_qfm = [None] * NTB
        m_rope = [None] * NTB

        def emit_qproj(tb):
            par = tb % 2
            wt("t", "S", m_h1fm[tb])
            if tb == 1:
                wt("t", "S", m_wqE[NDC - 1])
            if tb >= 2:
                wt("t", "V", m_qtmE[tb - 2])
            for dc in range(NDC):
                for ih in range(2):
                    def mm(e, tb=tb, dc=dc, ih=ih, par=par):
                        out = ps_st[:, par * 1024 + ih * 512:par * 1024 + (ih + 1) * 512]
                        return e.matmul(out,
                                        bufA[:, dc * 2048 + tb * 128:dc * 2048 + tb * 128 + 128],
                                        wT[:, dc * 1024 + ih * 512:dc * 1024 + (ih + 1) * 512],
                                        start=(dc == 0), stop=(dc == NDC - 1))
                    if dc == NDC - 1 and ih == 1:
                        m_qp[tb] = op("t", mm, inc="T")
                    else:
                        op("t", mm)

        def emit_qrope(tb):
            par = tb % 2
            csc = cs_y[:, tb * 64:(tb + 1) * 64]
            css = cs_y[:, 1024 + tb * 64:1024 + (tb + 1) * 64]
            wt("v", "T", m_qp[tb])
            psq = ps_st[:, par * 1024:par * 1024 + 1024]
            if tb >= 1:
                vsync(m_rope[tb - 1])
            if tb >= 2 and m_qT[tb - 2] is not None:
                wt("v", "T", m_qT[tb - 2])
            op("v", lambda e, css=css, psq=psq: e.tensor_tensor(
                out=qtm[:, :].rearrange("p (h a j) -> p h a j", h=16, a=2, j=32),
                in0=psq.rearrange("p (h a j) -> p h a j", h=16, a=2, j=32)[:, :, ::-1, :],
                in1=css.rearrange("p (a j) -> p a j", a=2, j=32).unsqueeze(1).broadcast_to((128, 16, 2, 32)),
                op=A.mult))
            m_qtmE[tb] = op("v", lambda e, csc=csc, psq=psq, par=par: e.tensor_tensor(
                out=qrope[:, par * 1024:par * 1024 + 1024].rearrange("p (h f) -> p h f", h=16, f=64),
                in0=psq.rearrange("p (h f) -> p h f", h=16, f=64),
                in1=csc.unsqueeze(1).broadcast_to((128, 16, 64)), op=A.mult), inc="V")
            vsync(m_qtmE[tb])
            m_rope[tb] = op("v", lambda e, par=par: e.tensor_tensor(
                out=qrope[:, par * 1024:par * 1024 + 1024],
                in0=qrope[:, par * 1024:par * 1024 + 1024],
                in1=qtm[:, :], op=A.add), inc="V")

        def emit_qT(tb):
            par = tb % 2
            pso = par
            wt("t", "V", m_rope[tb])
            if tb < 2:
                wt("t", "V", m_boutc)
            else:
                wt("t", "S", m_qfm[tb - 2])
            for pr in range(NPAIR):
                def tr(e, pr=pr, par=par, pso=pso):
                    return e.transpose(psw_bf[:, pso * 1024 + pr * 128:pso * 1024 + pr * 128 + 128],
                                       qrope[:, par * 1024 + pr * 128:par * 1024 + (pr + 1) * 128], idb_sb[:, :])
                if pr == NPAIR - 1:
                    m_qT[tb] = op("t", tr, inc="T")
                else:
                    op("t", tr)
            wt("s", "T", m_qT[tb])
            m_qfm[tb] = op("s", lambda e, tb=tb, pso=pso: e.activation(
                out=qfm[:, :].rearrange("p (pr t) -> p pr t", pr=NPAIR, t=2048)[:, :, tb * 128:tb * 128 + 128],
                in_=psw_bf[:, pso * 1024:pso * 1024 + 1024].rearrange("p (pr c) -> p pr c", pr=NPAIR, c=128),
                func=AF.Copy, bias=0.0), inc="S")

        m_h1T = [None] * NTB
        m_h1fm = [None] * NTB
        wt("g", "S", m_sin2)
        for tb in range(NTB):
            slot = tb % 2
            if tb % 4 == 0:
                gsync(m_ln1g[tb // 4])
            if tb >= 2:
                wt("g", "T", m_h1T[tb - 2])
            m_n1 = op("g", lambda e, tb=tb, slot=slot: e.tensor_scalar(
                out=tmbf[:, slot * 1024:slot * 1024 + 1024],
                in0=x_sb[:, tb * 1024:(tb + 1) * 1024],
                scalar1=nm1[:, tb:tb + 1], scalar2=iv1[:, tb:tb + 1],
                op0=A.add, op1=A.mult), inc="G")
            wt("t", "G", m_n1)
            if tb >= 2:
                wt("t", "S", m_h1fm[tb - 2])
            for dc in range(NDC):
                def tr(e, dc=dc, slot=slot):
                    return e.transpose(
                        pspv_bf[:, slot * 1024 + dc * 128:slot * 1024 + dc * 128 + 128],
                        tmbf[:, slot * 1024 + dc * 128:slot * 1024 + dc * 128 + 128],
                        idb_sb[:, :])
                if dc == NDC - 1:
                    m_h1T[tb] = op("t", tr, inc="T")
                else:
                    op("t", tr)
            wt("s", "T", m_h1T[tb])
            m_h1fm[tb] = op("s", lambda e, tb=tb, slot=slot: e.activation(
                out=bufA[:, :].rearrange("p (dc t) -> p dc t", dc=NDC, t=2048)[:, :, tb * 128:tb * 128 + 128],
                in_=pspv_bf[:, slot * 1024:slot * 1024 + 1024].rearrange("p (dc c) -> p dc c", dc=NDC, c=128),
                func=AF.Copy, bias=0.0), inc="S")
            if tb >= 1:
                if tb >= 3:
                    emit_qT(tb - 3)
                emit_qproj(tb - 1)
                emit_qrope(tb - 1)
            if tb == NTB - 1:
                emit_qT(tb - 2)
                emit_qproj(tb)
                emit_qrope(tb)
                emit_qT(tb - 1)
                emit_qT(tb)

        if stage <= 1:
            debug_finish()
        m_kvT = [None] * 4
        m_kvE = [None] * 4
        for tc in range(4):
            if tc == 0:
                wt("t", "S", m_qfm[NTB - 1])
            wt("t", "V", m_kvw)
            wt("t", "S", m_h1fm[4 * tc + 3])
            if tc >= 1:
                wt("t", "S", m_kvE[tc - 1])
            for dc in range(NDC):
                def mm(e, tc=tc, dc=dc):
                    return e.matmul(ps_w[:, 0:512], kvw[:, dc * 128:(dc + 1) * 128],
                                    bufA[:, dc * 2048 + tc * 512:dc * 2048 + (tc + 1) * 512],
                                    start=(dc == 0), stop=(dc == NDC - 1))
                if dc == NDC - 1:
                    m_kvT[tc] = op("t", mm, inc="T")
                else:
                    op("t", mm)
            wt("s", "T", m_kvT[tc])
            m_kvE[tc] = op("s", lambda e, tc=tc: e.activation(
                out=h2fm[:, tc * 512:(tc + 1) * 512], in_=ps_w[:, 0:512],
                func=AF.Copy, bias=0.0), inc="S")


        streams["y"].append(lambda e, v=m_kvT[3]: e.wait_ge(sems["T"][0], v))
        m_winL = dma(lambda e: e.dma_start(
            out=wstage[:, :].rearrange("p (c h) -> p c h", c=8, h=64),
            in_=win.rearrange("(c p) h -> p c h", p=128)))
        wt("v", "dW", m_winL)
        wt("v", "T", m_kvT[3])
        m_win = op("v", lambda e: e.tensor_copy(out=win_sb, in_=wstage[:, :]), inc="V")

        m_vT = [None] * 2
        m_vE = [None] * 2
        for gs in range(2):
            wt("t", "S", m_kvE[3])
            if gs == 1:
                wt("t", "S", m_vE[0])
            for i in range(8):
                tb = gs * 8 + i
                def tr(e, tb=tb, i=i):
                    return e.transpose(psw_bf[:, i * 128:i * 128 + 64],
                                       h2fm[64:128, tb * 128:(tb + 1) * 128],
                                       idb_sb[64:128, 64:128])
                if i == 7:
                    m_vT[gs] = op("t", tr, inc="T")
                else:
                    op("t", tr)
            wt("s", "T", m_vT[gs])
            m_vE[gs] = op("s", lambda e, gs=gs: e.activation(
                out=v1[:, :].rearrange("p (t c) -> p t c", t=16, c=65)[:, gs * 8:gs * 8 + 8, 0:64],
                in_=psw_bf[:, 0:1024].rearrange("p (i c) -> p i c", i=8, c=128)[:, :, 0:64],
                func=AF.Copy, bias=0.0), inc="S")

        m_kT = [None] * 2
        m_ktm = [None] * 2
        for gs in range(2):
            wt("t", "S", m_vE[gs] if gs == 0 else m_ktm[0])
            if gs == 0:
                wt("t", "S", m_vE[1])
            for i in range(8):
                tb = gs * 8 + i
                def tr(e, tb=tb, i=i):
                    return e.transpose(psw_bf[:, i * 128:i * 128 + 64],
                                       h2fm[0:64, tb * 128:(tb + 1) * 128],
                                       idb_sb[0:64, 0:64])
                if i == 7:
                    m_kT[gs] = op("t", tr, inc="T")
                else:
                    op("t", tr)
            wt("s", "T", m_kT[gs])
            if gs == 0:
                wt("s", "V", m_rope[NTB - 1])
            m_ktm[gs] = op("s", lambda e, gs=gs: e.activation(
                out=qtm[:, gs * 512:gs * 512 + 512].rearrange("p (t c) -> p t c", t=8, c=64),
                in_=psw_bf[:, 0:1024].rearrange("p (i c) -> p i c", i=8, c=128)[:, :, 0:64],
                func=AF.Copy, bias=0.0), inc="S")

        m_krope = [None] * 2
        m_kback = [None] * NTB
        m_kvdE = {}
        m_kbackL = None
        for gs in range(2):
            wt("v", "S", m_ktm[gs])
            csc = cs_y[:, gs * 512:(gs + 1) * 512]
            css = cs_y[:, 1024 + gs * 512:1024 + (gs + 1) * 512]
            ktg = qtm[:, gs * 512:(gs + 1) * 512]
            m_sw = op("v", lambda e, ktg=ktg, css=css, gs=gs: e.tensor_tensor(
                out=tmbf[:, 1024 + gs * 512:1024 + gs * 512 + 512].rearrange(
                    "p (t a j) -> p t a j", t=8, a=2, j=32),
                in0=ktg.rearrange("p (t a j) -> p t a j", t=8, a=2, j=32)[:, :, ::-1, :],
                in1=css.rearrange("p (t a j) -> p t a j", t=8, a=2, j=32), op=A.mult), inc="V")
            vsync(m_sw)
            m_kc = op("v", lambda e, ktg=ktg, csc=csc: e.tensor_tensor(
                out=ktg, in0=ktg, in1=csc, op=A.mult), inc="V")
            vsync(m_kc)
            m_krope[gs] = op("v", lambda e, ktg=ktg, gs=gs: e.tensor_tensor(
                out=ktg, in0=ktg,
                in1=tmbf[:, 1024 + gs * 512:1024 + gs * 512 + 512], op=A.add), inc="V")
        for tb in range(NTB):
            if tb % 4 == 0:
                wt("t", "V", m_krope[(tb + 3) // 8])
                if tb == 0:
                    wt("t", "S", m_ktm[1])
                if tb >= 4:
                    wt("t", "S", m_kvdE.get(tb - 4))
            m_kback[tb] = op("t", lambda e, tb=tb: e.transpose(
                psw_bf[0:64, (tb % 4) * 128:(tb % 4) * 128 + 128],
                qtm[:, tb * 64:tb * 64 + 64], idb_sb[:, :]), inc="T")
            if tb % 4 == 3:
                wt("s", "T", m_kback[tb])
                mm_ = op("s", lambda e, tb=tb: e.activation(
                    out=kvd[0:64, (tb - 3) * 128:(tb + 1) * 128],
                    in_=psw_bf[0:64, 0:512], func=AF.Copy, bias=0.0), inc="S")
                for q_ in range(4):
                    m_kvdE[tb - q_] = mm_
        m_kbackL = m_kback[NTB - 1]
        wt("s", "S", m_kvdE[15])
        m_kdup = op("s", lambda e: e.activation(
            out=kvd[64:128, :], in_=kvd[0:64, :], func=AF.Copy, bias=0.0), inc="S")

        if stage <= 2:
            debug_finish()


        m_woTh = {}
        m_woT = [None] * NDC
        m_woE = [None] * NDC

        def emit_woT(ics):
            if ics[0] == 0:
                streams["g"].append(lambda e, v=m_qp[NTB - 1]: e.wait_ge(sems["T"][0], v))
                streams["g"].append(lambda e, v=m_win: e.wait_ge(sems["V"][0], v))
            for ic in ics:
                for cg in range(2):
                    ch = ic * 2 + cg
                    sl = ch % 2
                    prev2 = m_woTh.get((ic, cg - 2)) or m_woTh.get((ic - 1, cg))
                    if prev2 is not None:
                        streams["g"].append(lambda e, v=prev2: e.wait_ge(sems["T"][0], v))
                    m_load = op("g", lambda e, ic=ic, cg=cg, sl=sl: e.dma_start(
                        out=wstage_bf[:, sl * 512:sl * 512 + 512],
                        in_=wo[ic * 128:(ic + 1) * 128, cg * 512:(cg + 1) * 512]),
                        inc="dQ", inck=16)
                    wt("t", "dQ", m_load + (16 if ch % 2 == 0 else 0))
                    if ic == 0 and cg == 0:
                        wt("t", "S", m_kdup)
                    if ic >= 1 and cg == 0:
                        wt("t", "V", m_woE[ic - 1])
                    for j in range(4):
                        def tr(e, j=j, cg=cg, sl=sl):
                            return e.transpose(
                                psw_bf[:, cg * 512 + j * 128:cg * 512 + j * 128 + 128],
                                wstage_bf[:, sl * 512 + j * 128:sl * 512 + (j + 1) * 128], idb_sb[:, :])
                        if j == 3:
                            m_woTh[(ic, cg)] = op("t", tr, inc="T")
                        else:
                            op("t", tr)
                m_woT[ic] = m_woTh[(ic, 1)]
                wt("v", "T", m_woT[ic])
                if ic == 0:
                    wt("v", "S", m_qfm[NTB - 1])
                m_woE[ic] = op("v", lambda e, ic=ic: e.tensor_copy(
                    out=wT[:, :].rearrange("p (dc i) -> p dc i", dc=NDC, i=1024)[:, :, ic * 128:ic * 128 + 128],
                    in_=psw_bf[:, 0:1024].rearrange("p (s c) -> p s c", s=8, c=128)), inc="V")

        emit_woT(list(range(NDC)))
        strickle = []
        dqq = []

        m_ep = [[None] * NPAIR for _ in range(NQC)]
        m_aq = [[None] * NPAIR for _ in range(NQC)]
        m_bc = [[None] * NPAIR for _ in range(NQC)]
        m_nm = [[None] * NPAIR for _ in range(NQC)]
        m_recip = [None] * NQC
        m_wo = {}
        m_yev = {}
        m_nrm = {}
        m_add2 = {}
        m_add4 = {}
        m_n3 = {}
        m_h2T = {}
        m_h2E = {}
        m_fe = {}
        m_gb = [None] * NQC
        m_fpl = [None] * NQC
        m_accL = [None]
        exp_hist = []
        pv_km = {}
        exp_m = {}

        def emit_scores(qc, pair, kb):
            sub = kb % 2
            di = kb - 4 * qc
            col0 = 128 * di if di >= 0 else 0
            if qc == 0 and pair == 0 and kb < 2:
                wt("t", "S", m_qfm[NTB - 1])
                wt("t", "V", m_qtmE[NTB - 1])
                wt("t", "S", m_kdup)
                wt("t", "V", m_ones)
            if qc >= 2 and pair == 0 and kb == 0:
                wt("t", "S", m_qfm[4 * qc + 3])
            if len(exp_hist) >= 2:
                wt("t", "S", exp_hist[-2])
            mkey = None
            for hb in (0, 64):
                base = (0 if hb == 0 else 1024) + sub * 512

                def mm(e, qc=qc, pair=pair, kb=kb, hb=hb, base=base, col0=col0, di=di):
                    return e.matmul(
                        ps_st[:, base + col0:base + 512],
                        kvd[hb:hb + 64, kb * 128:(kb + 1) * 128],
                        qfm[hb:hb + 64, pair * 2048 + qc * 512 + col0:pair * 2048 + (qc + 1) * 512],
                        start=True, stop=(di < 0), tile_position=(hb, 0))
                if di < 0 and hb == 64:
                    mkey = op("t", mm, inc="T")
                else:
                    op("t", mm)
                if di >= 0:
                    op("t", lambda e, hb=hb, base=base, col0=col0: e.matmul(
                        ps_st[:, base + col0:base + col0 + 128],
                        tria_sb[hb:hb + 64, 0:128], negi_sb[hb:hb + 64, 0:128],
                        start=False, stop=False, tile_position=(hb, 0)))

                    def tr2(e, hb=hb, base=base, col0=col0):
                        return e.matmul(
                            ps_st[:, base + col0:base + col0 + 128],
                            tria_sb[hb:hb + 64, 128:256], negi_sb[hb:hb + 64, 128:256],
                            start=False, stop=True, tile_position=(hb, 0))
                    if hb == 64:
                        mkey = op("t", tr2, inc="T")
                    else:
                        op("t", tr2)
            return mkey

        def emit_exp(qc, pair, kb, m_scores):
            sub = kb % 2
            di = kb - 4 * qc
            col0 = 128 * di if di >= 0 else 0
            HS = 2048 if qc == 0 else 8192
            PT = h2fm if qc == 0 else bufA
            wt("s", "T", m_scores)
            if qc == 0 and pair == 0 and kb == 0:
                wt("s", "T", m_kT[1])
            if pair > 0:
                wt("s", "T", pv_km[(qc, pair - 1, kb)])
            elif qc > 0:
                prev_last = 4 * (qc - 1) + 3
                wt("s", "T", pv_km[(qc - 1, 7, min(kb, prev_last))])
            if qc == 1 and pair == 0 and kb == 0:
                wt("s", "T", m_qp[NTB - 1])
            in_ap = ps_st[:, :].rearrange("p (h c) -> p h c", h=2, c=1024)[:, :, sub * 512 + col0:sub * 512 + 512]
            out_ap = PT[:, :].rearrange("p (h k) -> p h k", h=2, k=HS)[:, :, kb * 512 + col0:kb * 512 + 512]
            m = op("s", lambda e, in_ap=in_ap, out_ap=out_ap: e.activation(
                out=out_ap, in_=in_ap, func=AF.Exp, bias=expb[:, 0:1], scale=0.125), inc="S")
            exp_hist.append(m)
            exp_m[(qc, pair, kb)] = m

        def emit_pv(qc, pair, kb):
            nkb = 4 * qc + 4
            di = kb - 4 * qc
            col0 = 128 * di if di >= 0 else 0
            HS = 2048 if qc == 0 else 8192
            PT = h2fm if qc == 0 else bufA
            wt("t", "S", exp_m[(qc, pair, kb)])
            mkey = None
            for h in range(2):
                def mm(e, pair=pair, kb=kb, h=h, col0=col0, HS=HS, PT=PT, nkb=nkb):
                    return e.matmul(
                        ps_pv[0:65, h * 512 + col0:h * 512 + 512],
                        v1[:, kb * 65:kb * 65 + 65],
                        PT[:, h * HS + kb * 512 + col0:h * HS + kb * 512 + 512],
                        start=(kb == 0), stop=(kb == nkb - 1))
                if h == 1:
                    mkey = op("t", mm, inc="T")
                else:
                    op("t", mm)
            pv_km[(qc, pair, kb)] = mkey

        def emit_epilogue(qc, pair):
            nkb = 4 * qc + 4
            es = "s" if qc <= 2 else "v"
            wt("v", "T", pv_km[(qc, pair, nkb - 1)])
            if es == "s":
                wt("s", "T", pv_km[(qc, pair, nkb - 1)])
            if qc > 0:
                wt("v", "T", m_wo.get((qc - 1, 4 * (qc - 1) + 3)))
                wt(es, "S", m_recipH[(qc - 1, 0 if pair < 4 else 1)])
            op("v", lambda e, pair=pair: e.tensor_copy(
                out=attnq[0:64, pair * 512:(pair + 1) * 512], in_=ps_pv[0:64, 0:512]))
            m_aq[qc][pair] = op("v", lambda e, pair=pair: e.tensor_copy(
                out=attnq[64:128, pair * 512:(pair + 1) * 512], in_=ps_pv[0:64, 512:1024]), inc="V")
            for hh in range(2):
                h = 2 * pair + hh
                b, c = h % 4, h // 4

                def cp(e, hh=hh, b=b, c=c):
                    if es == "s":
                        return e.activation(out=sums[32 * b:32 * b + 1, 512 * c:512 * c + 512],
                                            in_=ps_pv[64:65, hh * 512:hh * 512 + 512],
                                            func=AF.Copy, bias=0.0)
                    return e.tensor_copy(out=sums[32 * b:32 * b + 1, 512 * c:512 * c + 512],
                                         in_=ps_pv[64:65, hh * 512:hh * 512 + 512])
                if hh == 1:
                    m_ep[qc][pair] = op(es, cp, inc="S" if es == "s" else "V")
                else:
                    op(es, cp)

        m_recipH = {}

        def emit_recip_half(qc, half):
            wt("s", "S" if qc <= 2 else "V", m_ep[qc][3 if half == 0 else 7])
            c0 = half * 1024
            m_ln = op("s", lambda e, c0=c0: e.activation(
                out=sums[:, c0:c0 + 1024], in_=sums[:, c0:c0 + 1024], func=AF.Ln,
                bias=zerob[:, 0:1]), inc="S")
            ssync(m_ln)
            m_recipH[(qc, half)] = op("s", lambda e, c0=c0: e.activation(
                out=sums[:, c0:c0 + 1024], in_=sums[:, c0:c0 + 1024], func=AF.Exp,
                bias=zerob[:, 0:1], scale=-1.0), inc="S")

        def emit_recip(qc):
            if (qc, 0) not in m_recipH:
                emit_recip_half(qc, 0)
            emit_recip_half(qc, 1)
            m_recip[qc] = m_recipH[(qc, 1)]

        def emit_bcast_norm(qc):
            wt("v", "S" if qc <= 2 else "V", m_ep[qc][7])
            for pair in range(NPAIR):
                b, c = (2 * pair) % 4, (2 * pair) // 4
                wt("t", "S", m_recipH[(qc, 0 if pair < 4 else 1)])
                if pair >= 2:
                    wt("t", "V", m_nm[qc][pair - 2])
                if pair < 2:
                    wt("t", "V", m_woE[NDC - 1])
                m_bc[qc][pair] = op("t", lambda e, b=b, c=c, pair=pair: e.matmul(
                    ps_w[:, (pair % 2) * 512:(pair % 2) * 512 + 512],
                    fsel_sb[32 * b:32 * b + 64, :], sums[32 * b:32 * b + 64, 512 * c:512 * c + 512],
                    start=True, stop=True), inc="T")
                wt("v", "T", m_bc[qc][pair])
                m_nm[qc][pair] = op("v", lambda e, pair=pair: e.tensor_tensor(
                    out=attnq[:, pair * 512:(pair + 1) * 512],
                    in0=attnq[:, pair * 512:(pair + 1) * 512],
                    in1=ps_w[:, (pair % 2) * 512:(pair % 2) * 512 + 512], op=A.mult), inc="V")

        def emit_wo(qc, k):
            tb = 4 * qc + k
            wt("t", "V", m_nm[qc][7])
            if k >= 1:
                wt("t", "V", m_yev[(qc, tb - 1)])
            for dc in range(NDC):
                for ih in range(2):
                    def mm(e, qc=qc, k=k, dc=dc, ih=ih):
                        return e.matmul(ps_w[:, ih * 512:(ih + 1) * 512],
                                        attnq[:, dc * 512 + k * 128:dc * 512 + k * 128 + 128],
                                        wT[:, dc * 1024 + ih * 512:dc * 1024 + (ih + 1) * 512],
                                        start=(dc == 0), stop=(dc == NDC - 1))
                    if dc == NDC - 1 and ih == 1:
                        m_wo[(qc, tb)] = op("t", mm, inc="T")
                    else:
                        op("t", mm)

        def emit_yev(qc, k):
            tb = 4 * qc + k
            sl = k % 2
            wt("v", "T", m_wo[(qc, tb)])
            if k >= 2:
                if qc == NQC - 1:
                    vsync(m_nrm[(qc, tb - 2)])
                else:
                    wt("v", "G", m_nrm[(qc, tb - 2)])
            m_yev[(qc, tb)] = op("v", lambda e, sl=sl: e.tensor_copy(
                out=cs_y[:, sl * 1024:sl * 1024 + 1024], in_=ps_w[:, :]), inc="V")
            vsync(m_yev[(qc, tb)])
            op("v", lambda e, k=k, sl=sl: e.bn_stats(
                out=stats[:, k * 12:k * 12 + 6], in_=cs_y[:, sl * 1024:sl * 1024 + 512]))
            return op("v", lambda e, k=k, sl=sl: e.bn_stats(
                out=stats[:, k * 12 + 6:k * 12 + 12], in_=cs_y[:, sl * 1024 + 512:sl * 1024 + 1024]),
                inc="V")

        def ln_core(ks, statbase, mvbase, nmx, ivx, vepbase, mstats):
            vsync(mstats)
            m_ag = None
            for k in ks:
                m_ag = op("v", lambda e, k=k, statbase=statbase, mvbase=mvbase: e.bn_aggr(
                    out=mv[:, mvbase + k * 2:mvbase + k * 2 + 2],
                    in_=stats[:, statbase + k * 12:statbase + k * 12 + 12]), inc="V")
            vsync(m_ag)
            k0, w = ks[0], len(ks)
            op("v", lambda e, k0=k0, w=w, mvbase=mvbase, nmx=nmx: e.tensor_scalar(
                out=nmx[:, k0:k0 + w], in0=mv[:, mvbase + 2 * k0:mvbase + 2 * k0 + 2 * w:2],
                scalar1=-1.0, scalar2=None, op0=A.mult))
            mfv = op("v", lambda e, k0=k0, w=w, mvbase=mvbase, vepbase=vepbase: e.tensor_scalar(
                out=vep[:, vepbase + k0:vepbase + k0 + w],
                in0=mv[:, mvbase + 2 * k0 + 1:mvbase + 2 * k0 + 2 * w:2],
                scalar1=EPS, scalar2=None, op0=A.add), inc="V")
            wt("g", "V", mfv)
            mg = op("g", lambda e, k0=k0, w=w, ivx=ivx, vepbase=vepbase: e.tensor_tensor(
                out=ivx[:, k0:k0 + w], in0=vep[:, vepbase + k0:vepbase + k0 + w],
                in1=neghalf[:, k0:k0 + w], op=A.pow), inc="G")
            return mfv, mg

        def emit_ln2_half(qc, ks, mstats):
            fast = (qc == NQC - 1)
            eng, esem, syn = ("v", "V", vsync) if fast else ("g", "G", gsync)
            mfv, mg = ln_core(ks, 0, 0, nm2, iv2, 0, mstats)
            for k in ks:
                tb = 4 * qc + k
                sl = k % 2
                wt(eng, "V", mfv)
                wt(eng, "V", m_yev[(qc, tb)])
                wt(eng, "G", mg)
                if k != ks[0]:
                    syn(m_add2[(qc, tb - 1)])
                if fast and qc >= 1:
                    wt("v", "G", m_add4.get((qc - 1, 4 * (qc - 1) + 3)))
                m_nrm[(qc, tb)] = op(eng, lambda e, k=k, sl=sl: e.tensor_scalar(
                    out=lnbuf[:, :], in0=cs_y[:, sl * 1024:sl * 1024 + 1024],
                    scalar1=nm2[:, k:k + 1], scalar2=iv2[:, k:k + 1],
                    op0=A.add, op1=A.mult), inc=esem)
                syn(m_nrm[(qc, tb)])
                m_add2[(qc, tb)] = op(eng, lambda e, tb=tb: e.tensor_tensor(
                    out=x_sb[:, tb * 1024:(tb + 1) * 1024],
                    in0=x_sb[:, tb * 1024:(tb + 1) * 1024],
                    in1=lnbuf[:, :], op=A.add), inc=esem)

        def emit_ln3_half(qc, ks):
            fast = (qc == NQC - 1)
            eng, esem, syn = ("v", "V", vsync) if fast else ("g", "G", gsync)
            m_st = None
            for k in ks:
                tb = 4 * qc + k
                if fast:
                    vsync(m_add2[(qc, tb)])
                else:
                    wt("v", "G", m_add2[(qc, tb)])
                op("v", lambda e, k=k, tb=tb: e.bn_stats(
                    out=stats[:, 48 + k * 12:48 + k * 12 + 6], in_=x_sb[:, tb * 1024:tb * 1024 + 512]))
                m_st = op("v", lambda e, k=k, tb=tb: e.bn_stats(
                    out=stats[:, 48 + k * 12 + 6:48 + k * 12 + 12],
                    in_=x_sb[:, tb * 1024 + 512:tb * 1024 + 1024]), inc="V")
            mfv, mg = ln_core(ks, 48, 8, nm3, iv3, 4, m_st)
            for k in ks:
                tb = 4 * qc + k
                sl = k % 2
                wt(eng, "G", mg)
                syn(m_add2[(qc, tb)])
                if (qc, tb - 2) in m_h2T:
                    wt(eng, "T", m_h2T[(qc, tb - 2)])
                wt(eng, "V", mfv)
                m_n3[(qc, tb)] = op(eng, lambda e, k=k, tb=tb, sl=sl: e.tensor_scalar(
                    out=tmbf[:, sl * 1024:sl * 1024 + 1024],
                    in0=x_sb[:, tb * 1024:(tb + 1) * 1024],
                    scalar1=nm3[:, k:k + 1], scalar2=iv3[:, k:k + 1],
                    op0=A.add, op1=A.mult), inc=esem)

        def emit_h2T(qc, k):
            tb = 4 * qc + k
            sl = k % 2
            wt("t", "V" if qc == NQC - 1 else "G", m_n3[(qc, tb)])
            if k == 0:
                wt("t", "V", m_yev[(qc, 4 * qc + 3)])
            if (qc, tb - 1) in m_h2E:
                wt("t", "V", m_h2E[(qc, tb - 1)])
            for dc in range(NDC):
                def tr(e, dc=dc, sl=sl):
                    return e.transpose(psw_bf[:, sl * 1024 + dc * 128:sl * 1024 + dc * 128 + 128],
                                       tmbf[:, sl * 1024 + dc * 128:sl * 1024 + dc * 128 + 128],
                                       idb_sb[:, :])
                if dc == NDC - 1:
                    m_h2T[(qc, tb)] = op("t", tr, inc="T")
                else:
                    op("t", tr)
            wt("v", "T", m_h2T[(qc, tb)])
            m_h2E[(qc, tb)] = op("v", lambda e, k=k, sl=sl: e.tensor_copy(
                out=h2fm[:, :].rearrange("p (dc t) -> p dc t", dc=NDC, t=512)[:, :, k * 128:k * 128 + 128],
                in_=psw_bf[:, sl * 1024:sl * 1024 + 1024].rearrange("p (dc c) -> p dc c", dc=NDC, c=128)), inc="V")

        ffn_st = {}

        def emit_ffn_front(qc, part):
            if part == 0:
                wt("t", "V", m_h2E[(qc, 4 * qc + 3)])
                wt("t", "V", m_win)
                wt("t", "V", m_ffnw)
                if qc > 0:
                    wt("t", "V", m_gb[qc - 1])
                m_sp = None
                for dc in range(NDC):
                    for half in range(2):
                        def mm(e, dc=dc, half=half):
                            return e.matmul(
                                ps_w[half * 64:half * 64 + 64, 0:256],
                                win_sb[:, dc * 64:(dc + 1) * 64],
                                h2fm[:, dc * 512 + half * 256:dc * 512 + (half + 1) * 256],
                                start=(dc == 0), stop=(dc == NDC - 1), tile_position=(0, half * 64),
                                skip_group_check=True)
                        if dc == NDC - 1 and half == 1:
                            m_sp = op("t", mm, inc="T")
                        else:
                            op("t", mm)
                wt("v", "T", m_sp)
                if qc > 0:
                    wt("v", "T", m_fpl[qc - 1])
                m_sev = op("v", lambda e: e.tensor_copy(out=s_sbv, in_=ps_w[:, 0:256]), inc="V")
                ffn_st[qc] = {"sev": m_sev, "relu": {}, "acc": {}}
            fst = ffn_st[qc]
            m_sev, m_relu, m_acc = fst["sev"], fst["relu"], fst["acc"]

            fastq = (qc == NQC - 1)

            def step(i, qc=qc, fastq=fastq):
                if fastq:
                    wt("s", "V", m_sev if i < 2 else m_acc[i - 2])
                    m_relu[i] = op("s", lambda e, i=i: e.activation(
                        out=hbuf[i % 2], in_=s_sbv, func=AF.Relu,
                        bias=b1pd[:, i:i + 1], scale=w1d[:, i:i + 1]), inc="S")
                    wt("v", "S", m_relu[i])
                    if i >= 2:
                        vsync(m_acc[i - 2])
                    elif m_accL[0] is not None:
                        vsync(m_accL[0])
                elif i < 2:
                    vsync(m_sev)
                    if m_accL[0] is not None:
                        vsync(m_accL[0])
                else:
                    vsync(m_acc[i - 2])
                if not fastq:
                    m_pre = op("v", lambda e, i=i: e.tensor_scalar(
                        out=hbuf[i % 2], in0=s_sbv, scalar1=w1d[:, i:i + 1],
                        scalar2=b1pd[:, i:i + 1], op0=A.mult, op1=A.add), inc="V")
                    vsync(m_pre)
                    m_relu[i] = op("v", lambda e, i=i: e.tensor_scalar(
                        out=hbuf[i % 2], in0=hbuf[i % 2], scalar1=0.0, scalar2=None,
                        op0=A.max), inc="V")
                    vsync(m_relu[i])
                if i < 2:
                    m_acc[i] = op("v", lambda e, i=i: e.tensor_scalar(
                        out=gsl[i], in0=hbuf[i], scalar1=w2d[:, i:i + 1],
                        scalar2=(b2d[:, 0:1] if i == 0 else zerob[:, 0:1]),
                        op0=A.mult, op1=A.add), inc="V")
                else:
                    m_acc[i] = op("v", lambda e, i=i: e.scalar_tensor_tensor(
                        out=gsl[i % 2], in0=hbuf[i % 2], scalar=w2d[:, i:i + 1], in1=gsl[i % 2],
                        op0=A.mult, op1=A.add), inc="V")
                if i == 15:
                    vsync(m_acc[15])
                    m_gb[qc] = op("v", lambda e: e.tensor_tensor(
                        out=gb, in0=gsl[0], in1=gsl[1], op=A.add), inc="V")
                    m_accL[0] = m_acc[15]
            for i in (range(0, 8) if part == 0 else range(8, 16)):
                strickle.append(lambda i=i: step(i))

        def emit_fproj(qc, tt):
            hb = 64 * (tt // 2)
            gcol = (tt % 2) * 128
            wt("t", "V", m_gb[qc] if tt == 0 else m_fe[(qc, tt - 1)])
            m_fp = None
            for ih in range(2):
                def mm(e, hb=hb, gcol=gcol, ih=ih):
                    return e.matmul(ps_w[:, ih * 512:(ih + 1) * 512],
                                    gb[hb:hb + 64, gcol:gcol + 128],
                                    woutd[hb:hb + 64, ih * 512:(ih + 1) * 512],
                                    start=True, stop=True, tile_position=(hb, 0))
                if ih == 1:
                    m_fp = op("t", mm, inc="T")
                else:
                    op("t", mm)
            m_fpl[qc] = m_fp
            wt("v", "T", m_fp)
            if tt >= 1:
                if qc == NQC - 1:
                    vsync(m_nrm4.get((qc, tt - 1)))
                else:
                    wt("v", "G", m_nrm4.get((qc, tt - 1)))
            for ih in range(2):
                def ev(e, ih=ih):
                    return e.tensor_tensor(
                        out=f_sb[:, ih * 512:(ih + 1) * 512],
                        in0=ps_w[:, ih * 512:(ih + 1) * 512],
                        in1=boutb[:, ih * 512:(ih + 1) * 512], op=A.add)
                if ih == 1:
                    m_fe[(qc, tt)] = op("v", ev, inc="V")
                else:
                    op("v", ev)

        m_nrm4 = {}

        def emit_ln4_single(qc, k):
            fast = (qc == NQC - 1)
            eng, esem, syn = ("v", "V", vsync) if fast else ("g", "G", gsync)
            vsync(m_fe[(qc, k)])
            op("v", lambda e, k=k: e.bn_stats(
                out=stats[:, 96 + k * 12:96 + k * 12 + 6], in_=f_sb[:, 0:512]))
            m_st = op("v", lambda e, k=k: e.bn_stats(
                out=stats[:, 96 + k * 12 + 6:96 + k * 12 + 12], in_=f_sb[:, 512:1024]), inc="V")
            mfv, mg = ln_core((k,), 96, 16, nm4, iv4, 8, m_st)
            tb = 4 * qc + k
            wt(eng, "V", mfv)
            wt(eng, "V", m_fe[(qc, k)])
            wt(eng, "G", mg)
            syn(m_add2[(qc, tb)])
            if k >= 1:
                syn(m_add4[(qc, tb - 1)])
            m_nrm4[(qc, k)] = op(eng, lambda e, k=k: e.tensor_scalar(
                out=lnbuf[:, :], in0=f_sb[:, :],
                scalar1=nm4[:, k:k + 1], scalar2=iv4[:, k:k + 1],
                op0=A.add, op1=A.mult), inc=esem)
            syn(m_nrm4[(qc, k)])
            m_add4[(qc, tb)] = op(eng, lambda e, tb=tb: e.tensor_tensor(
                out=x_sb[:, tb * 1024:(tb + 1) * 1024],
                in0=x_sb[:, tb * 1024:(tb + 1) * 1024],
                in1=lnbuf[:, :], op=A.add), inc=esem)

        def emit_yout(qc):
            asem = "V" if qc == NQC - 1 else "G"
            for hf in range(2):
                streams["y"].append(lambda e, v=m_add4[(qc, 4 * qc + 2 * hf + 1)], asem=asem:
                                     e.wait_ge(sems[asem][0], v))
                streams["y"].append(lambda e, v=sems["dO"][1]: (e.wait_ge(sems["dO"][0], v) if v > 0 else None))
                op("y", lambda e, qc=qc, hf=hf: e.dma_start(
                    out=yout[qc * 512 + hf * 256:qc * 512 + (hf + 1) * 256, :].rearrange("(t p) d -> p t d", p=128),
                    in_=x_sb[:, qc * 4096 + hf * 2048:qc * 4096 + (hf + 1) * 2048].rearrange(
                        "p (t d) -> p t d", t=2, d=1024)), inc="dO", inck=16)

        def D_units(qc):
            def u1():
                emit_wo(qc, 0)
                emit_yev(qc, 0)
                emit_wo(qc, 1)
                ms1 = emit_yev(qc, 1)
                emit_ln2_half(qc, (0, 1), ms1)

            def u2():
                emit_wo(qc, 2)
                emit_yev(qc, 2)
                emit_wo(qc, 3)
                ms3 = emit_yev(qc, 3)
                emit_ln2_half(qc, (2, 3), ms3)

            return [
                lambda: emit_bcast_norm(qc),
                u1,
                u2,
                lambda: (emit_ln3_half(qc, (0, 1)), emit_h2T(qc, 0), emit_h2T(qc, 1)),
                lambda: (emit_ln3_half(qc, (2, 3)), emit_h2T(qc, 2), emit_h2T(qc, 3)),
                lambda: emit_ffn_front(qc, 0),
                lambda: emit_ffn_front(qc, 1),
                lambda: (emit_fproj(qc, 0), emit_ln4_single(qc, 0)),
                lambda: (emit_fproj(qc, 1), emit_ln4_single(qc, 1)),
                lambda: (emit_fproj(qc, 2), emit_ln4_single(qc, 2)),
                lambda: (emit_fproj(qc, 3), emit_ln4_single(qc, 3), emit_yout(qc)),
            ]

        if stage <= 3:
            debug_finish()
        nqc_run = NQC if stage >= 5 else (1 if stage == 4 else 0)
        do_D = stage >= 6
        dq = []
        trail = []
        prev_qc = None
        for qc in range(nqc_run):
            nkb = 4 * qc + 4
            for pair in range(NPAIR):
                for kb in range(nkb):
                    if pair == 0 and trail:
                        trail.pop(0)()
                    if pair >= 1:
                        if kb == 0:
                            if pair >= 2:
                                wt("t", "S" if qc <= 2 else "V", m_ep[qc][pair - 2])
                                wt("t", "V", m_aq[qc][pair - 2])
                            elif qc >= 1:
                                wt("t", "S" if qc - 1 <= 2 else "V", m_ep[qc - 1][7])
                                wt("t", "V", m_aq[qc - 1][7])
                        emit_pv(qc, pair - 1, kb)
                    m_sc = emit_scores(qc, pair, kb)
                    emit_exp(qc, pair, kb, m_sc)
                    if strickle:
                        strickle.pop(0)()
                    if (kb == nkb // 2 - 1 and pair == 0 and do_D and prev_qc is not None
                            and (prev_qc, 0) not in m_recipH):
                        emit_recip_half(prev_qc, 0)
                    if kb == nkb // 2 - 1:
                        if dqq:
                            dqq.pop(0)()
                        elif do_D and dq and pair >= 1 and not strickle:
                            dq.pop(0)()
                if pair == 0:
                    while trail:
                        trail.pop(0)()
                    if do_D and prev_qc is not None:
                        emit_recip(prev_qc)

                if dqq:
                    dqq.pop(0)()
                elif do_D and dq and (not strickle or pair <= 1):
                    dq.pop(0)()
                if pair >= 1:
                    emit_epilogue(qc, pair - 1)

            def mk_trail(qc=qc, nkb=nkb):
                t = []
                for kb in range(nkb):
                    def f(kb=kb, qc=qc):
                        if kb == 0:
                            wt("t", "S" if qc <= 2 else "V", m_ep[qc][6])
                            wt("t", "V", m_aq[qc][6])
                        emit_pv(qc, 7, kb)
                    t.append(f)
                t.append(lambda qc=qc: emit_epilogue(qc, 7))
                return t
            trail = mk_trail()
            if do_D:
                nu = D_units(qc)
                dq = nu[0:3] + dq + nu[3:]
                prev_qc = qc
        while trail:
            trail.pop(0)()
        if do_D:
            emit_recip(prev_qc)
            for u in dq:
                u()
                while strickle:
                    strickle.pop(0)()
        else:
            debug_finish()

        streams["y"].append(lambda e: e.wait_ge(sems["dO"][0], sems["dO"][1]))

        block = st.enter_context(nc.Block())

        @block.sync
        def _(eng):
            for f in streams["y"]:
                f(eng)

        @block.tensor
        def _(eng):
            for f in streams["t"]:
                f(eng)

        @block.vector
        def _(eng):
            for f in streams["v"]:
                f(eng)

        @block.scalar
        def _(eng):
            for f in streams["s"]:
                f(eng)

        @block.gpsimd
        def _(eng):
            for f in streams["g"]:
                f(eng)

    return nc


_CACHE = {}


def kernel(L0_attn_x, L0_attn_wq, L0_attn_wk, L0_attn_wv, L0_attn_wo,
           L0_ffn_Win, L0_ffn_bin_, L0_ffn_W1, L0_ffn_b1, L0_ffn_W2,
           L0_ffn_b2, L0_ffn_Wout, L0_ffn_bout):
    x = np.ascontiguousarray(np.asarray(L0_attn_x, dtype=np.float32))
    consts = _host_consts()
    if "nc" not in _CACHE:
        _CACHE["nc"] = build_nc()
    nc = _CACHE["nc"]
    shared = {
        "wq": np.ascontiguousarray(np.asarray(L0_attn_wq, dtype=np.float32)),
        "wk": np.ascontiguousarray(np.asarray(L0_attn_wk, dtype=np.float32)),
        "wv": np.ascontiguousarray(np.asarray(L0_attn_wv, dtype=np.float32)),
        "wo": np.ascontiguousarray(np.asarray(L0_attn_wo, dtype=np.float32)),
        "win": np.ascontiguousarray(np.asarray(L0_ffn_Win, dtype=np.float32)),
        "binp": np.ascontiguousarray(np.asarray(L0_ffn_bin_, dtype=np.float32).reshape(64, 1)),
        "w1": np.ascontiguousarray(np.asarray(L0_ffn_W1, dtype=np.float32)),
        "b1": np.ascontiguousarray(np.asarray(L0_ffn_b1, dtype=np.float32)),
        "w2": np.ascontiguousarray(np.asarray(L0_ffn_W2, dtype=np.float32)),
        "b2": np.ascontiguousarray(np.asarray(L0_ffn_b2, dtype=np.float32).reshape(64, 1)),
        "wout": np.ascontiguousarray(np.asarray(L0_ffn_Wout, dtype=np.float32)),
        "bout": np.ascontiguousarray(np.asarray(L0_ffn_bout, dtype=np.float32).reshape(1, D)),
    }
    shared.update(consts)
    in_maps = []
    for i in range(NCORE):
        m = dict(shared)
        m["x"] = np.ascontiguousarray(x[i])
        in_maps.append(m)
    res = run_bass_kernel_spmd(nc, in_maps, core_ids=list(range(NCORE)))
    out = np.stack([np.asarray(r["y"]) for r in res.results], axis=0)
    return out.astype(np.float32)
